# version 05
# baseline (speedup 1.0000x reference)
import contextlib
import numpy as np
import concourse.bass as bass
import concourse.mybir as mybir
from concourse.bass_utils import run_bass_kernel_spmd

F32 = mybir.dt.float32
BF16 = mybir.dt.bfloat16
FP16 = mybir.dt.float16
AF = mybir.ActivationFunctionType
ALU = mybir.AluOpType
AX = mybir.AxisListType

S = 2048
D = 4096
NH = 32
NC = 8
HALF = 1024
SHIFT = 12.0
MASKVAL = -60000.0
RELW = 2560
WAR = 29184
SCALE = 1.0 / float(np.sqrt(128.0))

O_WKV = 0
O_XBF = 8192
O_SQA = 16384
O_KSC = 20480
O_WQ = 0
O_PRB = 0
O_WO = 0
O_SQ2 = 0
O_X2 = 8192
O_ART = 4096
O_WSC = 6656
O_WX = 0
O_WRT = 20992


def _alibi_slopes(n_heads=32):
    import math
    closest = 2 ** math.floor(math.log2(n_heads))
    base = 2.0 ** (-(2.0 ** (-(math.log2(closest) - 3))))
    slopes = [base ** i for i in range(1, closest + 1)]
    if closest != n_heads:
        extra = 2.0 ** (-(2.0 ** (-(math.log2(2 * closest) - 3))))
        n_extra = min(closest, n_heads - closest)
        slopes += [extra ** i for i in range(1, 2 * n_extra + 1, 2)]
    return [float(np.float32(s)) for s in slopes]

SLOPES = _alibi_slopes(NH)


class Ctr:
    def __init__(self, es, nc, name):
        self.h = es.enter_context(nc.semaphore(name))
        self.n = 0

    def bump(self, k=1):
        self.n += k
        return self.n


class Build:
    def __init__(self, nc, es, serial=False):
        self.nc = nc
        self.es = es
        self.serial = serial
        self.last = None
        self.q = {e: [] for e in ("t", "v", "s", "g", "x")}
        self.c = {}
        for e in ("t", "v", "s", "g", "x"):
            self.ctr(e)
        self.waited = {e: {} for e in self.q}
        self.guards = {}

    def ctr(self, name):
        if name not in self.c:
            self.c[name] = Ctr(self.es, self.nc, "sem_" + name)
        return self.c[name]

    def wait(self, eng, ctr_name, mark):
        if mark is None or mark <= 0 or mark <= self.waited[eng].get(ctr_name, 0):
            return
        self.waited[eng][ctr_name] = mark
        h = self.ctr(ctr_name).h
        self.q[eng].append(lambda e, h=h, m=mark: e.wait_ge(h, m))

    def emit(self, eng, fn, inc=None, inck=1):
        if self.serial:
            if self.last is not None:
                self.wait(eng, self.last[0], self.last[1])
            if inc is None:
                inc = {"t": "t", "v": "v", "s": "s", "g": "gp", "x": "gp"}[eng]
        if inc is None:
            self.q[eng].append(fn)
            return None
        ctr = self.ctr(inc)
        mark = ctr.bump(inck)
        h = ctr.h

        def run(e, fn=fn, h=h, k=inck):
            fn(e).then_inc(h, k)
        self.q[eng].append(run)
        self.last = (inc, mark)
        return mark

    def need(self, eng, *keys):
        for key in keys:
            g = self.guards.get(key)
            if g is not None:
                self.wait(eng, g[0], g[1])

    def guard(self, key, ctr_name, mark):
        old = self.guards.get(key)
        if old is None or old[0] != ctr_name or (mark is not None and mark > old[1]):
            if mark is not None:
                self.guards[key] = (ctr_name, mark)


DUMP_H = 0


def build_nc(dump=False, serial=False, serial_phases=()):
    nc = bass.Bass("TRN2", target_bir_lowering=False, debug=False)
    x_d = nc.dram_tensor("x", [S, D], F32, kind="ExternalInput").ap()
    wq_d = nc.dram_tensor("wq", [D, D], F32, kind="ExternalInput").ap()
    wkv_d = nc.dram_tensor("wkv", [D, 256], F32, kind="ExternalInput").ap()
    wo_d = nc.dram_tensor("wo", [D, D], F32, kind="ExternalInput").ap()
    bwt_d = nc.dram_tensor("bwt", [D, D], F32, kind="ExternalInput").ap()
    bext_d = nc.dram_tensor("bext", [8, 128, D], F32, kind="ExternalInput").ap()
    art_d = nc.dram_tensor("art", [32, 128, 80], F32, kind="ExternalInput").ap()
    bb_d = nc.dram_tensor("bb", [1, D], F32, kind="ExternalInput").ap()
    rel_d = nc.dram_tensor("rel", [128, RELW], FP16, kind="ExternalInput").ap()
    ident_d = nc.dram_tensor("ident", [128, 128], F32, kind="ExternalInput").ap()
    consts_d = nc.dram_tensor("consts", [128, 8], F32, kind="ExternalInput").ap()
    y_d = nc.dram_tensor("y", [S, D], F32, kind="ExternalOutput").ap()
    dmp = {}
    if dump:
        for nm, shp, dt in [("d_lnT", [128, 32768], BF16), ("d_kT", [128, 2048], BF16),
                            ("d_v", [128, 16 * 132], BF16), ("d_qT", [128, 32768], BF16),
                            ("d_attnT", [128, 32768], BF16), ("d_zr", [128, 8], F32),
                            ("d_asc", [128, 128], BF16), ("d_st", [128, 8], F32),
                            ("d_att0", [128, 512], BF16), ("d_ps", [128, 256], BF16), ("d_stA", [128, 8], F32),
                            ("d_x1", [128, 32768], BF16), ("d_ln2T", [128, 32768], BF16),
                            ("d_prb", [128, 8192], BF16)]:
            dmp[nm] = nc.dram_tensor(nm, shp, dt, kind="ExternalOutput").ap()

    es = contextlib.ExitStack()
    with es:
        sbt = lambda n, s, d: es.enter_context(nc.sbuf_tensor(n, s, d))
        pst = lambda n, s, d: es.enter_context(nc.psum_tensor(n, s, d))
        A = sbt("A", [128, 32 * 1024], BF16)
        B = sbt("Bb", [128, 32 * 1024], BF16)
        kT = sbt("kT", [128, 2048], BF16)
        v_ext = sbt("v_ext", [128, 16 * 132], BF16)
        W = sbt("W", [128, WAR], BF16)
        FA = sbt("FA", [128, 1024], F32)
        rel_sb = sbt("rel_sb", [128, RELW], FP16)
        ident_bf = sbt("ident_bf", [128, 128], BF16)
        consts = sbt("consts_sb", [128, 8], F32)
        ones_r = sbt("ones_r", [1, 128], BF16)
        ones_c = sbt("ones_c", [128, 1], BF16)
        zr = sbt("zr", [128, 8], F32)
        stat = sbt("stat", [128, 8], F32)
        PS = [pst(f"PS{i}", [128, 512], F32) for i in range(4)]
        PT = [pst("PTA", [128, 1024], BF16), pst("PTB", [128, 1024], BF16)]
        PSR = pst("PSR", [128, 512], F32)
        asc = sbt("asc", [128, 256], BF16)

        bd = Build(nc, es, serial=serial)
        t, v, s_, g, x_ = "t", "v", "s", "g", "x"

        bd.emit(g, lambda e: e.dma_start(ident_bf[:, :], ident_d), inc="ident", inck=16)
        bd.emit(g, lambda e: e.memset(ones_r[:, :], 1.0), inc="gp")
        bd.emit(g, lambda e: e.memset(ones_c[:, :], 1.0), inc="gp")
        bd.emit(g, lambda e: e.memset(W[:, O_WSC:O_WSC + 256], 0.0), inc="gp")
        bd.emit(g, lambda e: e.memset(v_ext[:, :], 1.0), inc="gp")
        bd.emit(x_, lambda e: e.dma_start(rel_sb[:, :], rel_d), inc="rel", inck=16)
        bd.emit(x_, lambda e: e.dma_start(consts[:, :], consts_d), inc="cst", inck=16)
        bd.wait(t, "ident", 16)
        bd.wait(t, "gp", 4)
        bd.wait(g, "gp", 4)
        bd.wait(v, "rel", 16)
        bd.wait(v, "cst", 16)
        bd.wait(s_, "cst", 16)

        def stats_ln(src_ap, sq_off, needkeys):
            bd.emit(v, lambda e: e.reduce_sum(stat[:, 0:1], src_ap, axis=AX.X))
            bd.need(v, *needkeys)
            mk_sq = bd.emit(v, lambda e: e.tensor_mul(W[:, sq_off:sq_off + D], src_ap, src_ap),
                            inc="v")
            bd.wait(v, "v", mk_sq)
            mk = bd.emit(v, lambda e: e.reduce_sum(stat[:, 1:2], W[:, sq_off:sq_off + D],
                                                   axis=AX.X), inc="v")
            bd.wait(v, "v", mk)
            mk = bd.emit(v, lambda e: e.tensor_scalar(stat[:, 2:3], stat[:, 0:1], 1.0 / D,
                                                      None, ALU.mult), inc="v")
            bd.wait(v, "v", mk)
            mk = bd.emit(v, lambda e: e.tensor_scalar(stat[:, 3:4], stat[:, 1:2], 1.0 / D,
                                                      None, ALU.mult), inc="v")
            bd.wait(v, "v", mk)
            mk = bd.emit(v, lambda e: e.tensor_mul(stat[:, 4:5], stat[:, 2:3], stat[:, 2:3]),
                         inc="v")
            bd.wait(v, "v", mk)
            mk_v = bd.emit(v, lambda e: e.scalar_tensor_tensor(
                stat[:, 5:6], stat[:, 4:5], -1.0, stat[:, 3:4], ALU.mult, ALU.add), inc="v")
            bd.wait(s_, "v", mk_v)
            mk_l = bd.emit(s_, lambda e: e.activation(stat[:, 6:7], stat[:, 5:6], AF.Ln,
                                                      bias=consts[:, 3:4]), inc="s")
            bd.wait(s_, "s", mk_l)
            mk_r = bd.emit(s_, lambda e: e.activation(stat[:, 7:8], stat[:, 6:7], AF.Exp,
                                                      scale=consts[:, 1:2]), inc="s")
            return mk_r

        def transpose_tile_to_A(xoff, sI):
            last_t = None
            for gI in range(8):
                pt = PT[gI % 2]
                bd.need(t, f"pt{gI % 2}")
                for j in range(4):
                    ic = gI * 4 + j
                    mk_t = bd.emit(t, lambda e, pt=pt, j=j, xo=xoff, ic=ic: e.transpose(
                        pt[:, j * 128:j * 128 + 128], W[:, xo + ic * 128:xo + ic * 128 + 128],
                        ident_bf[:, :]), inc=("t" if j == 3 else None))
                bd.wait(v, "t", mk_t)
                bd.need(v, "bufA")
                mk_e = bd.emit(v, lambda e, pt=pt, gI=gI, sI=sI: e.tensor_copy(
                    A[:, :].rearrange("p (c n) -> p c n", n=1024)[:, gI * 4:gI * 4 + 4,
                                                                 sI * 128:sI * 128 + 128],
                    pt[:, 0:512].rearrange("p (c n) -> p c n", c=4)), inc="v")
                bd.guard(f"pt{gI % 2}", "v", mk_e)
                last_t = mk_t
            return last_t, mk_e

        qe_marks = {}
        for h in range(2):
            tok0 = h * HALF

            bd.serial = serial or ('A' in serial_phases)
            bd.need(g, "war_lo")
            mk_wkv = bd.emit(g, lambda e: e.dma_start(
                W[:, O_WKV:O_WKV + 8192].rearrange("p (c n) -> p c n", c=32),
                wkv_d.rearrange("(c p) n -> p c n", p=128)), inc="wkv", inck=16)
            for sI in range(8):
                t0 = tok0 + sI * 128
                gc = 8 * h + sI
                xoff = O_XBF + 4096 * (sI % 2)
                bd.need(g, f"xbf{sI % 2}")
                mk_x = bd.emit(g, lambda e, t0=t0, xo=xoff: e.dma_start(
                    W[:, xo:xo + D], x_d[t0:t0 + 128, :]), inc=f"xb{sI % 2}", inck=16)
                bd.wait(v, f"xb{sI % 2}", mk_x)
                bd.wait(s_, f"xb{sI % 2}", mk_x)
                mk_r = stats_ln(W[:, xoff:xoff + D], O_SQA, ("sqA", "wx1"))
                bd.guard("sqA", "s", bd.c["s"].n)
                bd.wait(v, "s", mk_r)
                if dump and h == DUMP_H and sI == 0:
                    mkst = bd.emit(v, lambda e: e.tensor_copy(FA[:, 896:904], stat[:, :]), inc="v")
                    bd.wait(x_, "v", mkst)
                    bd.emit(x_, lambda e: e.dma_start(dmp["d_stA"], FA[:, 896:904]),
                            inc="dmps", inck=16)
                mk_ln = bd.emit(v, lambda e, xo=xoff: e.tensor_scalar(
                    W[:, xo:xo + D], W[:, xo:xo + D], stat[:, 2:3], stat[:, 7:8],
                    ALU.subtract, ALU.mult), inc="v")
                bd.wait(t, "v", mk_ln)
                mk_t, lnT_done = transpose_tile_to_A(xoff, sI)
                bd.guard(f"xbf{sI % 2}", "t", mk_t)
                bd.wait(t, "v", lnT_done)
                bd.wait(t, "wkv", mk_wkv)
                bd.need(t, "ps2")
                for ic in range(32):
                    mk_kv = bd.emit(t, lambda e, ic=ic, sI=sI: e.matmul(
                        PS[2][:, 0:256], A[:, ic * 1024 + sI * 128:ic * 1024 + sI * 128 + 128],
                        W[:, O_WKV + ic * 256:O_WKV + ic * 256 + 256],
                        start=(ic == 0), stop=(ic == 31)), inc=("t" if ic == 31 else None))
                bd.wait(v, "t", mk_kv)
                bd.need(v, "kscr")
                mk_k = bd.emit(v, lambda e: e.tensor_scalar(
                    W[:, O_KSC:O_KSC + 128], PS[2][:, 0:128], SCALE, None, ALU.mult), inc="v")
                bd.wait(v, "gp", 4)
                mk_vv = bd.emit(v, lambda e, gc=gc: e.tensor_copy(
                    v_ext[:, gc * 132:gc * 132 + 128], PS[2][:, 128:256]), inc="v")
                bd.guard("ps2", "v", mk_vv)
                bd.wait(t, "v", mk_k)
                bd.need(t, "pt0")
                mk_tk = bd.emit(t, lambda e: e.transpose(
                    PT[0][:, 0:128], W[:, O_KSC:O_KSC + 128], ident_bf[:, :]), inc="t")
                bd.guard("kscr", "t", mk_tk)
                bd.wait(v, "t", mk_tk)
                mk_kT = bd.emit(v, lambda e, gc=gc: e.tensor_copy(
                    kT[:, gc * 128:gc * 128 + 128], PT[0][:, 0:128]), inc="v")
                bd.guard("pt0", "v", mk_kT)
                last_kv_t = mk_tk
            bd.guard("wq0", "t", last_kv_t)
            bd.guard("wq1", "t", last_kv_t)
            phaseA_v = bd.c["v"].n
            if dump and h == DUMP_H:
                bd.wait(x_, "v", phaseA_v)
                bd.emit(x_, lambda e: e.dma_start(dmp["d_lnT"], A[:, :]), inc="dmps", inck=16)
                bd.emit(x_, lambda e: e.dma_start(dmp["d_kT"], kT[:, :]), inc="dmps", inck=16)
                mkd = bd.emit(x_, lambda e: e.dma_start(dmp["d_v"], v_ext[:, :]), inc="dmps", inck=16)
                bd.guard("bufA", "dmps", mkd)

            bd.serial = serial or ('Q' in serial_phases)
            for hd in range(NH):
                poff = O_WQ + 4096 * (hd % 2)
                bd.need(g, f"wq{hd % 2}")
                mk_wq = bd.emit(g, lambda e, hd=hd, po=poff: e.dma_start(
                    W[:, po:po + 4096].rearrange("p (c n) -> p c n", c=32),
                    wq_d.rearrange("(c p) n -> p c n", p=128)[:, :, hd * 128:hd * 128 + 128]),
                    inc=f"wqs{hd % 2}", inck=16)
                bd.wait(t, f"wqs{hd % 2}", mk_wq)
                bd.wait(t, "v", phaseA_v)
                for tb in range(2):
                    psq = PS[tb % 2]
                    bd.need(t, f"ps{tb % 2}")
                    for ic in range(32):
                        mk_q = bd.emit(t, lambda e, psq=psq, po=poff, ic=ic, tb=tb: e.matmul(
                            psq[:, :], W[:, po + ic * 128:po + ic * 128 + 128],
                            A[:, ic * 1024 + tb * 512:ic * 1024 + tb * 512 + 512],
                            start=(ic == 0), stop=(ic == 31)), inc=("t" if ic == 31 else None))
                    bd.guard(f"wq{hd % 2}", "t", mk_q)
                    bd.wait(v, "t", mk_q)
                    mk_qe = bd.emit(v, lambda e, psq=psq, hd=hd, tb=tb: e.tensor_copy(
                        B[:, hd * 1024 + tb * 512:hd * 1024 + tb * 512 + 512], psq[:, :]), inc="v")
                    bd.guard(f"ps{tb % 2}", "v", mk_qe)
                    qe_marks[(h, hd, tb)] = mk_qe
            phaseQ_t = bd.c["t"].n
            if dump and h == DUMP_H:
                bd.wait(x_, "v", bd.c["v"].n)
                mkd = bd.emit(x_, lambda e: e.dma_start(dmp["d_qT"], B[:, :]), inc="dmps", inck=16)
                bd.guard("bufB", "dmps", mkd)

            bd.serial = serial or ('ATTN' in serial_phases)
            bd.guard("probsT0", "t", phaseQ_t)
            bd.guard("probsT1", "t", phaseQ_t)
            bd.need(v, "ob0", "ob1")
            scidx = 0
            pvidx = 0

            def emit_scores(hd, tb):
                nonlocal scidx
                slope = SLOPES[hd]
                t0b = tok0 + tb * 512
                K_v = (t0b + 512) // 128
                seqp = (hd * 2 + tb) % 2
                pT_off = O_PRB + 8192 * seqp
                bd.need(s_, f"probsT{seqp}")
                bd.wait(t, "v", qe_marks[(h, hd, tb)])
                exp_last = None
                for kc in range(K_v):
                    bank = scidx % 2
                    scidx += 1
                    scf = 512 * bank
                    bd.need(t, f"ps{bank}")
                    mk_sc = bd.emit(t, lambda e, bank=bank, kc=kc, hd=hd, tb=tb: e.matmul(
                        PS[bank][:, :], kT[:, kc * 128:kc * 128 + 128],
                        B[:, hd * 1024 + tb * 512:hd * 1024 + tb * 512 + 512],
                        start=True, stop=True), inc="t")
                    bd.wait(v, "t", mk_sc)
                    bd.need(v, f"sc{bank}")
                    shift2 = 511 - kc * 128 + t0b
                    mk_stt = bd.emit(v, lambda e, bank=bank, scf=scf, sh=shift2, sl=slope:
                                     e.scalar_tensor_tensor(
                                         FA[:, scf:scf + 512], rel_sb[:, sh:sh + 512], sl,
                                         PS[bank][:, :], ALU.mult, ALU.add), inc="v")
                    bd.guard(f"ps{bank}", "v", mk_stt)
                    if kc % 2 == 1:
                        bd.wait(s_, "v", mk_stt)
                        mk_exp = bd.emit(s_, lambda e, po=pT_off, kc=kc: e.activation(
                            W[:, po + (kc - 1) * 512:po + (kc + 1) * 512], FA[:, 0:1024],
                            AF.Exp, bias=consts[:, 0:1]), inc="s")
                        bd.guard("sc0", "s", mk_exp)
                        bd.guard("sc1", "s", mk_exp)
                        exp_last = mk_exp
                return exp_last

            def emit_pv(hd, tb, exp_last):
                nonlocal pvidx
                t0b = tok0 + tb * 512
                K_v = (t0b + 512) // 128
                seqp = (hd * 2 + tb) % 2
                pT_off = O_PRB + 8192 * seqp
                for s4 in range(4):
                    pvb = 2 + (pvidx % 2)
                    pvidx += 1
                    bd.wait(t, "s", exp_last)
                    bd.need(t, f"ps{pvb}")
                    for kc in range(K_v):
                        mk_pv = bd.emit(t, lambda e, pvb=pvb, kc=kc, s4=s4, po=pT_off: e.matmul(
                            PS[pvb][:, 0:130],
                            W[:, po + kc * 512 + s4 * 128:po + kc * 512 + s4 * 128 + 128],
                            v_ext[:, kc * 132:kc * 132 + 130],
                            start=(kc == 0), stop=(kc == K_v - 1)),
                            inc=("t" if kc == K_v - 1 else None))
                    bd.guard(f"probsT{seqp}", "t", mk_pv)
                    bd.wait(v, "t", mk_pv)
                    mk_rc = bd.emit(v, lambda e, pvb=pvb, s4=s4: e.reciprocal(
                        stat[:, s4:s4 + 1], PS[pvb][:, 128:129]), inc="v")
                    bd.wait(v, "v", mk_rc)
                    bd.need(v, f"asc{s4 % 2}")
                    mk_sc2 = bd.emit(v, lambda e, pvb=pvb, s4=s4: e.tensor_scalar(
                        asc[:, (s4 % 2) * 128:(s4 % 2) * 128 + 128], PS[pvb][:, 0:128],
                        stat[:, s4:s4 + 1], None, ALU.mult), inc="v")
                    bd.guard(f"ps{pvb}", "v", mk_sc2)
                    bd.wait(t, "v", mk_sc2)
                    bd.need(t, f"pt{s4 % 2}")
                    mk_ta = bd.emit(t, lambda e, s4=s4: e.transpose(
                        PT[s4 % 2][:, 0:128], asc[:, (s4 % 2) * 128:(s4 % 2) * 128 + 128],
                        ident_bf[:, :]), inc="t")
                    bd.guard(f"asc{s4 % 2}", "t", mk_ta)
                    bd.wait(v, "t", mk_ta)
                    bd.need(v, "bufA")
                    mk_ae = bd.emit(v, lambda e, s4=s4, hd=hd, tb=tb: e.tensor_copy(
                        A[:, hd * 1024 + tb * 512 + s4 * 128:hd * 1024 + tb * 512 + s4 * 128 + 128],
                        PT[s4 % 2][:, 0:128]), inc="v")
                    bd.guard(f"pt{s4 % 2}", "v", mk_ae)

            prev = None
            for hd in range(NH):
                for tb in range(2):
                    el = emit_scores(hd, tb)
                    if prev is not None:
                        emit_pv(prev[0], prev[1], prev[2])
                    prev = (hd, tb, el)
            emit_pv(prev[0], prev[1], prev[2])
            attn_v = bd.c["v"].n
            if dump and h == DUMP_H:
                bd.wait(x_, "v", attn_v)
                bd.emit(x_, lambda e: e.dma_start(dmp["d_attnT"], A[:, :]), inc="dmps", inck=16)
                mkd = bd.emit(x_, lambda e: e.dma_start(dmp["d_zr"], zr[:, :]), inc="dmps", inck=16)
                bd.guard("bufA", "dmps", mkd)

            bd.serial = serial or ('C' in serial_phases)
            for op in range(16):
                poff = O_WO + 8192 * (op % 2)
                bd.need(g, f"wo{op % 2}", f"probsT{op % 2}")
                mk_wo = bd.emit(g, lambda e, op=op, po=poff: e.dma_start(
                    W[:, po:po + 8192].rearrange("p (c n) -> p c n", c=32),
                    wo_d.rearrange("(c p) n -> p c n", p=128)[:, :, op * 256:op * 256 + 256]),
                    inc=f"wos{op % 2}", inck=16)
                bd.wait(t, f"wos{op % 2}", mk_wo)
                bd.wait(t, "v", attn_v)
                for sI in range(8):
                    bank = sI % 2
                    bd.need(t, f"ps{bank}")
                    for dc in range(32):
                        mk_c = bd.emit(t, lambda e, bank=bank, dc=dc, sI=sI, po=poff: e.matmul(
                            PS[bank][:, 0:256],
                            A[:, dc * 1024 + sI * 128:dc * 1024 + sI * 128 + 128],
                            W[:, po + dc * 256:po + dc * 256 + 256],
                            start=(dc == 0), stop=(dc == 31)), inc=("t" if dc == 31 else None))
                    bd.guard(f"wo{op % 2}", "t", mk_c)
                    bd.wait(s_, "t", mk_c)
                    bd.need(s_, "bufB")
                    mk_cn = bd.emit(s_, lambda e, bank=bank, sI=sI, op=op: e.activation(
                        B[:, sI * 4096 + op * 256:sI * 4096 + op * 256 + 256],
                        PS[bank][:, 0:256], AF.Copy), inc="s")
                    bd.guard(f"ps{bank}", "s", mk_cn)
            phaseC_t = bd.c["t"].n
            phaseC_s = bd.c["s"].n

            bd.serial = serial or ('LN2' in serial_phases) or ('DP' in serial_phases)
            bd.need(g, "art", "wo0")
            mk_art = bd.emit(g, lambda e: e.dma_start(
                W[:, O_ART:O_ART + 2560].rearrange("p (c n) -> p c n", c=32),
                art_d.rearrange("c p n -> p c n")), inc="arts", inck=16)
            for sI in range(8):
                t0 = tok0 + sI * 128
                xoff = O_X2 + 4096 * (sI % 2)
                bd.need(g, f"x2bf{sI % 2}", "wo1")
                mk_x = bd.emit(g, lambda e, t0=t0, xo=xoff: e.dma_start(
                    W[:, xo:xo + D], x_d[t0:t0 + 128, :]), inc=f"x2b{sI % 2}", inck=16)
                bd.wait(v, f"x2b{sI % 2}", mk_x)
                bd.wait(v, "s", phaseC_s)
                mk_x1 = bd.emit(v, lambda e, sI=sI, xo=xoff: e.tensor_add(
                    B[:, sI * 4096:sI * 4096 + D], B[:, sI * 4096:sI * 4096 + D],
                    W[:, xo:xo + D]), inc="v")
                bd.wait(s_, "v", mk_x1)
                mk_r = stats_ln(B[:, sI * 4096:sI * 4096 + D], O_SQ2, ("sq2", "wo0"))
                bd.guard("sq2", "s", bd.c["s"].n)
                bd.wait(v, "s", mk_r)
                mk_ln = bd.emit(v, lambda e, sI=sI, xo=xoff: e.tensor_scalar(
                    W[:, xo:xo + D], B[:, sI * 4096:sI * 4096 + D], stat[:, 2:3], stat[:, 7:8],
                    ALU.subtract, ALU.mult), inc="v")
                bd.wait(t, "v", mk_ln)
                mk_t, _ = transpose_tile_to_A(xoff, sI)
                bd.guard(f"x2bf{sI % 2}", "t", mk_t)
                ln2_done_mk = bd.c["v"].n
                bd.wait(t, "arts", mk_art)
                bd.wait(t, "v", ln2_done_mk)
                bd.need(t, "psr")
                for ic in range(32):
                    mk_rh = bd.emit(t, lambda e, ic=ic, sI=sI: e.matmul(
                        PSR[:, 0:80], A[:, ic * 1024 + sI * 128:ic * 1024 + sI * 128 + 128],
                        W[:, O_ART + ic * 80:O_ART + ic * 80 + 80],
                        start=(ic == 0), stop=(ic == 31)), inc=("t" if ic == 31 else None))
                bd.guard("art", "t", mk_rh)
                bd.wait(v, "t", mk_rh)
                bd.emit(v, lambda e: e.tensor_copy(FA[:, 48:112], PSR[:, 0:64]))
                mk_nm = bd.emit(v, lambda e: e.reduce_max(stat[:, 6:7], PSR[:, 64:80],
                                                          axis=AX.X, negate=True), inc="v")
                bd.wait(s_, "v", mk_nm)
                mk_el = bd.emit(s_, lambda e: e.activation(FA[:, 0:16], PSR[:, 64:80],
                                                           AF.Exp, bias=stat[:, 6:7]), inc="s")
                bd.guard("psr", "s", mk_el)
                bd.wait(v, "s", mk_el)
                tiny = []
                tiny.append(lambda e: e.reduce_sum(stat[:, 3:4], FA[:, 0:16], axis=AX.X))
                tiny.append(lambda e: e.reduce_max(stat[:, 0:1], FA[:, 0:16], axis=AX.X))
                tiny.append(lambda e: e.tensor_scalar(FA[:, 32:48], FA[:, 0:16], stat[:, 0:1],
                                                      None, ALU.is_lt))
                tiny.append(lambda e: e.tensor_mul(FA[:, 16:32], FA[:, 0:16], FA[:, 32:48]))
                tiny.append(lambda e: e.reduce_max(stat[:, 2:3], FA[:, 16:32], axis=AX.X))
                tiny.append(lambda e: e.tensor_scalar(stat[:, 3:4], stat[:, 3:4], 1e-6, None,
                                                      ALU.mult))
                tiny.append(lambda e: e.tensor_add(stat[:, 3:4], stat[:, 3:4], stat[:, 0:1]))
                tiny.append(lambda e: e.tensor_add(stat[:, 3:4], stat[:, 3:4], stat[:, 2:3]))
                tiny.append(lambda e: e.reciprocal(stat[:, 4:5], stat[:, 3:4]))
                tiny.append(lambda e: e.tensor_scalar(FA[:, 32:48], FA[:, 0:16], stat[:, 2:3],
                                                      None, ALU.is_ge))
                tiny.append(lambda e: e.tensor_scalar(FA[:, 16:32], FA[:, 0:16], stat[:, 4:5],
                                                      0.5, ALU.mult, ALU.mult))
                tiny.append(lambda e: e.tensor_mul(FA[:, 16:32], FA[:, 16:32], FA[:, 32:48]))
                for fn in tiny:
                    mk_cmb = bd.emit(v, fn, inc="v")
                    bd.wait(v, "v", mk_cmb)
                for eI in range(16):
                    slot = O_WSC + 128 * (eI % 2)
                    bd.need(v, f"wrh{eI % 2}", "wrhT")
                    mk_w = bd.emit(v, lambda e, sl=slot, eI=eI: e.tensor_scalar(
                        W[:, sl:sl + 64], FA[:, 48:112], FA[:, 16 + eI:17 + eI], None, ALU.mult),
                        inc="v")
                    bd.wait(t, "v", mk_w)
                    bd.need(t, f"pt{eI % 2}")
                    mk_tw = bd.emit(t, lambda e, sl=slot, eI=eI: e.transpose(
                        PT[eI % 2][:, 0:128], W[:, sl:sl + 128], ident_bf[:, :]), inc="t")
                    bd.guard(f"wrh{eI % 2}", "t", mk_tw)
                    bd.wait(v, "t", mk_tw)
                    half_p = 64 * (eI % 2)
                    dst = O_WRT + (eI // 2) * 1024 + sI * 128
                    mk_we = bd.emit(v, lambda e, eI=eI, hp=half_p, do=dst: e.tensor_copy(
                        W[hp:hp + 64, do:do + 128], PT[eI % 2][0:64, 0:128]), inc="v")
                    bd.guard(f"pt{eI % 2}", "v", mk_we)
            dpre_v = bd.c["v"].n

            bd.serial = serial or ('DM' in serial_phases)
            for op in range(16):
                slot = (op + 1) % 2
                poff = O_WX + 10496 * slot
                bd.need(g, f"wx{slot}")
                if op == 0:
                    bd.wait(g, "t", phaseC_t)
                    bd.need(g, "x2bf0", "x2bf1")
                if op == 1:
                    bd.wait(g, "v", dpre_v)
                mk_w1 = bd.emit(g, lambda e, op=op, po=poff: e.dma_start(
                    W[:, po:po + 8192].rearrange("p (c n) -> p c n", c=32),
                    bwt_d.rearrange("(c p) n -> p c n", p=128)[:, :, op * 256:op * 256 + 256]),
                    inc=f"wxs{slot}", inck=16)
                mk_w2 = bd.emit(g, lambda e, op=op, po=poff: e.dma_start(
                    W[:, po + 8192:po + 10240].rearrange("p (c n) -> p c n", c=8),
                    bext_d.rearrange("c p n -> p c n")[:, :, op * 256:op * 256 + 256]),
                    inc=f"wxs{slot}", inck=16)
                mk_w3 = bd.emit(g, lambda e, op=op, po=poff: e.dma_start(
                    W[0:1, po + 10240:po + 10496], bb_d[0:1, op * 256:op * 256 + 256]),
                    inc=f"wxs{slot}", inck=16)
                bd.wait(t, f"wxs{slot}", mk_w3)
                bd.wait(t, "v", dpre_v)
                for sI in range(8):
                    bank = sI % 2
                    bd.need(t, f"ps{bank}")
                    for j in range(32):
                        bd.emit(t, lambda e, bank=bank, j=j, sI=sI, po=poff: e.matmul(
                            PS[bank][:, 0:256],
                            A[:, j * 1024 + sI * 128:j * 1024 + sI * 128 + 128],
                            W[:, po + j * 256:po + j * 256 + 256], start=(j == 0), stop=False))
                    for j in range(8):
                        bd.emit(t, lambda e, bank=bank, j=j, sI=sI, po=poff: e.matmul(
                            PS[bank][:, 0:256],
                            W[:, O_WRT + j * 1024 + sI * 128:O_WRT + j * 1024 + sI * 128 + 128],
                            W[:, po + (32 + j) * 256:po + (32 + j) * 256 + 256],
                            start=False, stop=False))
                    mk_d = bd.emit(t, lambda e, bank=bank, po=poff: e.matmul(
                        PS[bank][:, 0:256], ones_r[:, :], W[0:1, po + 10240:po + 10496],
                        start=False, stop=True), inc="t")
                    bd.guard(f"wx{slot}", "t", mk_d)
                    bd.wait(v, "t", mk_d)
                    ob = 256 * (sI % 2)
                    bd.need(v, f"ob{sI % 2}")
                    mk_o = bd.emit(v, lambda e, bank=bank, ob=ob, sI=sI, op=op: e.tensor_add(
                        FA[:, ob:ob + 256], PS[bank][:, 0:256],
                        B[:, sI * 4096 + op * 256:sI * 4096 + op * 256 + 256]), inc="v")
                    bd.guard(f"ps{bank}", "v", mk_o)
                    bd.wait(x_, "v", mk_o)
                    t0 = tok0 + sI * 128
                    mk_y = bd.emit(x_, lambda e, ob=ob, t0=t0, op=op: e.dma_start(
                        y_d[t0:t0 + 128, op * 256:op * 256 + 256], FA[:, ob:ob + 256]),
                        inc=f"obs{sI % 2}", inck=16)
                    bd.guard(f"ob{sI % 2}", f"obs{sI % 2}", mk_y)
            end_t = bd.c["t"].n
            for key in ("war_lo", "xbf0", "xbf1", "wq0", "wq1", "wrhT", "bufA",
                        "kscr", "sqA"):
                bd.guard(key, "t", end_t)

        for sem_name in ("obs0", "obs1", "dmps"):
            if sem_name in bd.c:
                fin = bd.c[sem_name].n
                bd.wait(t, sem_name, fin)
                bd.wait(v, sem_name, fin)

        with nc.Block() as block:
            @block.gpsimd
            def _(e):
                for f in bd.q["g"]:
                    f(e)

            @block.sync
            def _(e):
                for f in bd.q["x"]:
                    f(e)

            @block.tensor
            def _(e):
                for f in bd.q["t"]:
                    f(e)

            @block.vector
            def _(e):
                for f in bd.q["v"]:
                    f(e)

            @block.scalar
            def _(e):
                for f in bd.q["s"]:
                    f(e)
    return nc


def _host_inputs(L0_attn_hidden_states, L0_attn_Wq, L0_attn_Wkv, L0_attn_Wo,
                 L0_ffn_base_w, L0_ffn_base_b, L0_ffn_A, L0_ffn_B, L0_ffn_router_w):
    f32 = np.float32
    wq = np.ascontiguousarray(L0_attn_Wq, dtype=f32)
    wkv = np.ascontiguousarray(L0_attn_Wkv, dtype=f32)
    wo = np.ascontiguousarray(L0_attn_Wo, dtype=f32)
    bwt = np.ascontiguousarray(np.asarray(L0_ffn_base_w, dtype=f32).T)
    bext = np.ascontiguousarray(
        np.asarray(L0_ffn_B, dtype=f32).transpose(0, 2, 1).reshape(8, 128, D))
    art = np.ascontiguousarray(np.concatenate(
        [np.asarray(L0_ffn_A, dtype=f32).T.reshape(32, 128, 64),
         np.asarray(L0_ffn_router_w, dtype=f32).T.reshape(32, 128, 16)], axis=2))
    bb = np.ascontiguousarray(np.asarray(L0_ffn_base_b, dtype=f32).reshape(1, D))
    p = np.arange(128)[:, None]
    j = np.arange(RELW)[None, :]
    u = (p - j + 511).astype(f32)
    relf = np.where(u <= 0, u, np.float32(MASKVAL)).astype(np.float16)
    ident = np.eye(128, dtype=f32)
    consts = np.zeros((128, 8), dtype=f32)
    consts[:, 0] = -SHIFT
    consts[:, 1] = -0.5
    consts[:, 3] = 1e-5
    shared = dict(wq=wq, wkv=wkv, wo=wo, bwt=bwt, bext=bext, art=art, bb=bb,
                  rel=relf, ident=ident, consts=consts)
    xs = np.ascontiguousarray(np.asarray(L0_attn_hidden_states, dtype=f32))
    return xs, shared


def kernel(**inputs):
    xs, shared = _host_inputs(**inputs)
    nc = build_nc()
    in_maps = [dict(shared, x=np.ascontiguousarray(xs[c])) for c in range(NC)]
    res = run_bass_kernel_spmd(nc, in_maps, core_ids=list(range(NC)))
    return np.stack([np.asarray(res.results[c]["y"], dtype=np.float32) for c in range(NC)],
                    axis=0)


# version 06
# speedup vs baseline: 1.0719x; 1.0719x over previous
import contextlib
import numpy as np
import concourse.bass as bass
import concourse.mybir as mybir
from concourse.bass_utils import run_bass_kernel_spmd

F32 = mybir.dt.float32
BF16 = mybir.dt.bfloat16
FP16 = mybir.dt.float16
AF = mybir.ActivationFunctionType
ALU = mybir.AluOpType
AX = mybir.AxisListType

S = 2048
D = 4096
NH = 32
NC = 8
HALF = 1024
SHIFT = 12.0
MASKVAL = -60000.0
RELW = 2560
WAR = 29184
SCALE = 1.0 / float(np.sqrt(128.0))

O_WKV = 0
O_XBF = 8192
O_SQA = 16384
O_KSC = 20480
O_WQ = 0
O_PRB = 0
O_WO = 0
O_SQ2 = 0
O_X2 = 8192
O_ART = 4096
O_WSC = 6656
O_WX = 0
O_WRT = 20992


def _alibi_slopes(n_heads=32):
    import math
    closest = 2 ** math.floor(math.log2(n_heads))
    base = 2.0 ** (-(2.0 ** (-(math.log2(closest) - 3))))
    slopes = [base ** i for i in range(1, closest + 1)]
    if closest != n_heads:
        extra = 2.0 ** (-(2.0 ** (-(math.log2(2 * closest) - 3))))
        n_extra = min(closest, n_heads - closest)
        slopes += [extra ** i for i in range(1, 2 * n_extra + 1, 2)]
    return [float(np.float32(s)) for s in slopes]

SLOPES = _alibi_slopes(NH)


class Ctr:
    def __init__(self, es, nc, name):
        self.h = es.enter_context(nc.semaphore(name))
        self.n = 0

    def bump(self, k=1):
        self.n += k
        return self.n


class Build:
    def __init__(self, nc, es, serial=False):
        self.nc = nc
        self.es = es
        self.serial = serial
        self.last = None
        self.q = {e: [] for e in ("t", "v", "s", "g", "x")}
        self.c = {}
        for e in ("t", "v", "s", "g", "x"):
            self.ctr(e)
        self.waited = {e: {} for e in self.q}
        self.guards = {}

    def ctr(self, name):
        if name not in self.c:
            self.c[name] = Ctr(self.es, self.nc, "sem_" + name)
        return self.c[name]

    def wait(self, eng, ctr_name, mark):
        if mark is None or mark <= 0 or mark <= self.waited[eng].get(ctr_name, 0):
            return
        self.waited[eng][ctr_name] = mark
        h = self.ctr(ctr_name).h
        self.q[eng].append(lambda e, h=h, m=mark: e.wait_ge(h, m))

    def emit(self, eng, fn, inc=None, inck=1):
        if self.serial:
            if self.last is not None:
                self.wait(eng, self.last[0], self.last[1])
            if inc is None:
                inc = {"t": "t", "v": "v", "s": "s", "g": "gp", "x": "gp"}[eng]
        if inc is None:
            self.q[eng].append(fn)
            return None
        ctr = self.ctr(inc)
        mark = ctr.bump(inck)
        h = ctr.h

        def run(e, fn=fn, h=h, k=inck):
            fn(e).then_inc(h, k)
        self.q[eng].append(run)
        self.last = (inc, mark)
        return mark

    def need(self, eng, *keys):
        for key in keys:
            g = self.guards.get(key)
            if g is not None:
                self.wait(eng, g[0], g[1])

    def guard(self, key, ctr_name, mark):
        old = self.guards.get(key)
        if old is None or old[0] != ctr_name or (mark is not None and mark > old[1]):
            if mark is not None:
                self.guards[key] = (ctr_name, mark)


DUMP_H = 0


def build_nc(dump=False, serial=False, serial_phases=()):
    nc = bass.Bass("TRN2", target_bir_lowering=False, debug=False)
    x_d = nc.dram_tensor("x", [S, D], F32, kind="ExternalInput").ap()
    wq_d = nc.dram_tensor("wq", [D, D], F32, kind="ExternalInput").ap()
    wkv_d = nc.dram_tensor("wkv", [D, 256], F32, kind="ExternalInput").ap()
    wo_d = nc.dram_tensor("wo", [D, D], F32, kind="ExternalInput").ap()
    bwt_d = nc.dram_tensor("bwt", [D, D], F32, kind="ExternalInput").ap()
    bext_d = nc.dram_tensor("bext", [8, 128, D], F32, kind="ExternalInput").ap()
    art_d = nc.dram_tensor("art", [32, 128, 80], F32, kind="ExternalInput").ap()
    bb_d = nc.dram_tensor("bb", [1, D], F32, kind="ExternalInput").ap()
    rel_d = nc.dram_tensor("rel", [128, RELW], FP16, kind="ExternalInput").ap()
    ident_d = nc.dram_tensor("ident", [128, 128], F32, kind="ExternalInput").ap()
    consts_d = nc.dram_tensor("consts", [128, 8], F32, kind="ExternalInput").ap()
    y_d = nc.dram_tensor("y", [S, D], F32, kind="ExternalOutput").ap()
    dmp = {}
    if dump:
        for nm, shp, dt in [("d_lnT", [128, 32768], BF16), ("d_kT", [128, 2048], BF16),
                            ("d_v", [128, 16 * 132], BF16), ("d_qT", [128, 32768], BF16),
                            ("d_attnT", [128, 32768], BF16), ("d_zr", [128, 8], F32),
                            ("d_asc", [128, 128], BF16), ("d_st", [128, 8], F32),
                            ("d_att0", [128, 512], BF16), ("d_ps", [128, 256], BF16), ("d_stA", [128, 8], F32),
                            ("d_x1", [128, 32768], BF16), ("d_ln2T", [128, 32768], BF16),
                            ("d_prb", [128, 8192], BF16)]:
            dmp[nm] = nc.dram_tensor(nm, shp, dt, kind="ExternalOutput").ap()

    es = contextlib.ExitStack()
    with es:
        sbt = lambda n, s, d: es.enter_context(nc.sbuf_tensor(n, s, d))
        pst = lambda n, s, d: es.enter_context(nc.psum_tensor(n, s, d))
        A = sbt("A", [128, 32 * 1024], BF16)
        B = sbt("Bb", [128, 32 * 1024], BF16)
        kT = sbt("kT", [128, 2048], BF16)
        v_ext = sbt("v_ext", [128, 16 * 132], BF16)
        W = sbt("W", [128, WAR], BF16)
        FA = sbt("FA", [128, 1024], F32)
        rel_sb = sbt("rel_sb", [128, RELW], FP16)
        ident_bf = sbt("ident_bf", [128, 128], BF16)
        consts = sbt("consts_sb", [128, 8], F32)
        ones_r = sbt("ones_r", [1, 128], BF16)
        ones_c = sbt("ones_c", [128, 1], BF16)
        zr = sbt("zr", [128, 8], F32)
        stat = sbt("stat", [128, 8], F32)
        PS = [pst(f"PS{i}", [128, 512], F32) for i in range(4)]
        PT = [pst("PTA", [128, 1024], BF16), pst("PTB", [128, 1024], BF16)]
        PSR = pst("PSR", [128, 512], F32)
        asc = sbt("asc", [128, 256], BF16)

        bd = Build(nc, es, serial=serial)
        t, v, s_, g, x_ = "t", "v", "s", "g", "x"

        bd.emit(g, lambda e: e.dma_start(ident_bf[:, :], ident_d), inc="ident", inck=16)
        bd.emit(g, lambda e: e.memset(ones_r[:, :], 1.0), inc="gp")
        bd.emit(g, lambda e: e.memset(ones_c[:, :], 1.0), inc="gp")
        bd.emit(g, lambda e: e.memset(W[:, O_WSC:O_WSC + 256], 0.0), inc="gp")
        bd.emit(g, lambda e: e.memset(v_ext[:, :], 1.0), inc="gp")
        bd.emit(x_, lambda e: e.dma_start(rel_sb[:, :], rel_d), inc="rel", inck=16)
        bd.emit(x_, lambda e: e.dma_start(consts[:, :], consts_d), inc="cst", inck=16)
        bd.wait(t, "ident", 16)
        bd.wait(t, "gp", 4)
        bd.wait(g, "gp", 4)
        bd.wait(v, "rel", 16)
        bd.wait(v, "cst", 16)
        bd.wait(s_, "cst", 16)

        def stats_ln(src_ap, sq_off, needkeys):
            bd.emit(v, lambda e: e.reduce_sum(stat[:, 0:1], src_ap, axis=AX.X))
            bd.need(v, *needkeys)
            mk_sq = bd.emit(v, lambda e: e.tensor_mul(W[:, sq_off:sq_off + D], src_ap, src_ap),
                            inc="v")
            bd.wait(v, "v", mk_sq)
            mk = bd.emit(v, lambda e: e.reduce_sum(stat[:, 1:2], W[:, sq_off:sq_off + D],
                                                   axis=AX.X), inc="v")
            bd.wait(v, "v", mk)
            mk = bd.emit(v, lambda e: e.tensor_scalar(stat[:, 2:3], stat[:, 0:1], 1.0 / D,
                                                      None, ALU.mult), inc="v")
            bd.wait(v, "v", mk)
            mk = bd.emit(v, lambda e: e.tensor_scalar(stat[:, 3:4], stat[:, 1:2], 1.0 / D,
                                                      None, ALU.mult), inc="v")
            bd.wait(v, "v", mk)
            mk = bd.emit(v, lambda e: e.tensor_mul(stat[:, 4:5], stat[:, 2:3], stat[:, 2:3]),
                         inc="v")
            bd.wait(v, "v", mk)
            mk_v = bd.emit(v, lambda e: e.scalar_tensor_tensor(
                stat[:, 5:6], stat[:, 4:5], -1.0, stat[:, 3:4], ALU.mult, ALU.add), inc="v")
            bd.wait(s_, "v", mk_v)
            mk_l = bd.emit(s_, lambda e: e.activation(stat[:, 6:7], stat[:, 5:6], AF.Ln,
                                                      bias=consts[:, 3:4]), inc="s")
            bd.wait(s_, "s", mk_l)
            mk_r = bd.emit(s_, lambda e: e.activation(stat[:, 7:8], stat[:, 6:7], AF.Exp,
                                                      scale=consts[:, 1:2]), inc="s")
            return mk_r

        def transpose_tile_to_A(xoff, sI):
            last_t = None
            for gI in range(8):
                pt = PT[gI % 2]
                bd.need(t, f"pt{gI % 2}")
                for j in range(4):
                    ic = gI * 4 + j
                    mk_t = bd.emit(t, lambda e, pt=pt, j=j, xo=xoff, ic=ic: e.transpose(
                        pt[:, j * 128:j * 128 + 128], W[:, xo + ic * 128:xo + ic * 128 + 128],
                        ident_bf[:, :]), inc=("t" if j == 3 else None))
                bd.wait(v, "t", mk_t)
                bd.need(v, "bufA")
                mk_e = bd.emit(v, lambda e, pt=pt, gI=gI, sI=sI: e.tensor_copy(
                    A[:, :].rearrange("p (c n) -> p c n", n=1024)[:, gI * 4:gI * 4 + 4,
                                                                 sI * 128:sI * 128 + 128],
                    pt[:, 0:512].rearrange("p (c n) -> p c n", c=4)), inc="v")
                bd.guard(f"pt{gI % 2}", "v", mk_e)
                last_t = mk_t
            return last_t, mk_e

        qe_marks = {}
        for h in range(2):
            tok0 = h * HALF

            bd.serial = serial or ('A' in serial_phases)
            bd.need(g, "war_lo")
            mk_wkv = bd.emit(g, lambda e: e.dma_start(
                W[:, O_WKV:O_WKV + 8192].rearrange("p (c n) -> p c n", c=32),
                wkv_d.rearrange("(c p) n -> p c n", p=128)), inc="wkv", inck=16)
            for sI in range(8):
                t0 = tok0 + sI * 128
                gc = 8 * h + sI
                xoff = O_XBF + 4096 * (sI % 2)
                bd.need(g, f"xbf{sI % 2}")
                mk_x = bd.emit(g, lambda e, t0=t0, xo=xoff: e.dma_start(
                    W[:, xo:xo + D], x_d[t0:t0 + 128, :]), inc=f"xb{sI % 2}", inck=16)
                bd.wait(v, f"xb{sI % 2}", mk_x)
                bd.wait(s_, f"xb{sI % 2}", mk_x)
                mk_r = stats_ln(W[:, xoff:xoff + D], O_SQA, ("sqA", "wx1"))
                bd.guard("sqA", "s", bd.c["s"].n)
                bd.wait(v, "s", mk_r)
                if dump and h == DUMP_H and sI == 0:
                    mkst = bd.emit(v, lambda e: e.tensor_copy(FA[:, 896:904], stat[:, :]), inc="v")
                    bd.wait(x_, "v", mkst)
                    bd.emit(x_, lambda e: e.dma_start(dmp["d_stA"], FA[:, 896:904]),
                            inc="dmps", inck=16)
                mk_ln = bd.emit(v, lambda e, xo=xoff: e.tensor_scalar(
                    W[:, xo:xo + D], W[:, xo:xo + D], stat[:, 2:3], stat[:, 7:8],
                    ALU.subtract, ALU.mult), inc="v")
                bd.wait(t, "v", mk_ln)
                mk_t, lnT_done = transpose_tile_to_A(xoff, sI)
                bd.guard(f"xbf{sI % 2}", "t", mk_t)
                bd.wait(t, "v", lnT_done)
                bd.wait(t, "wkv", mk_wkv)
                bd.need(t, "ps2")
                for ic in range(32):
                    mk_kv = bd.emit(t, lambda e, ic=ic, sI=sI: e.matmul(
                        PS[2][:, 0:256], A[:, ic * 1024 + sI * 128:ic * 1024 + sI * 128 + 128],
                        W[:, O_WKV + ic * 256:O_WKV + ic * 256 + 256],
                        start=(ic == 0), stop=(ic == 31)), inc=("t" if ic == 31 else None))
                bd.wait(v, "t", mk_kv)
                bd.need(v, "kscr")
                mk_k = bd.emit(v, lambda e: e.tensor_scalar(
                    W[:, O_KSC:O_KSC + 128], PS[2][:, 0:128], SCALE, None, ALU.mult), inc="v")
                bd.wait(v, "gp", 4)
                mk_vv = bd.emit(v, lambda e, gc=gc: e.tensor_copy(
                    v_ext[:, gc * 132:gc * 132 + 128], PS[2][:, 128:256]), inc="v")
                bd.guard("ps2", "v", mk_vv)
                bd.wait(t, "v", mk_k)
                bd.need(t, "pt0")
                mk_tk = bd.emit(t, lambda e: e.transpose(
                    PT[0][:, 0:128], W[:, O_KSC:O_KSC + 128], ident_bf[:, :]), inc="t")
                bd.guard("kscr", "t", mk_tk)
                bd.wait(v, "t", mk_tk)
                mk_kT = bd.emit(v, lambda e, gc=gc: e.tensor_copy(
                    kT[:, gc * 128:gc * 128 + 128], PT[0][:, 0:128]), inc="v")
                bd.guard("pt0", "v", mk_kT)
                last_kv_t = mk_tk
            bd.guard("wq0", "t", last_kv_t)
            bd.guard("wq1", "t", last_kv_t)
            phaseA_v = bd.c["v"].n
            if dump and h == DUMP_H:
                bd.wait(x_, "v", phaseA_v)
                bd.emit(x_, lambda e: e.dma_start(dmp["d_lnT"], A[:, :]), inc="dmps", inck=16)
                bd.emit(x_, lambda e: e.dma_start(dmp["d_kT"], kT[:, :]), inc="dmps", inck=16)
                mkd = bd.emit(x_, lambda e: e.dma_start(dmp["d_v"], v_ext[:, :]), inc="dmps", inck=16)
                bd.guard("bufA", "dmps", mkd)

            bd.serial = serial or ('Q' in serial_phases)
            for hd in range(NH):
                poff = O_WQ + 4096 * (hd % 2)
                bd.need(g, f"wq{hd % 2}")
                mk_wq = bd.emit(g, lambda e, hd=hd, po=poff: e.dma_start(
                    W[:, po:po + 4096].rearrange("p (c n) -> p c n", c=32),
                    wq_d.rearrange("(c p) n -> p c n", p=128)[:, :, hd * 128:hd * 128 + 128]),
                    inc=f"wqs{hd % 2}", inck=16)
                bd.wait(t, f"wqs{hd % 2}", mk_wq)
                bd.wait(t, "v", phaseA_v)
                for tb in range(2):
                    psq = PS[tb % 2]
                    bd.need(t, f"ps{tb % 2}")
                    for ic in range(32):
                        mk_q = bd.emit(t, lambda e, psq=psq, po=poff, ic=ic, tb=tb: e.matmul(
                            psq[:, :], W[:, po + ic * 128:po + ic * 128 + 128],
                            A[:, ic * 1024 + tb * 512:ic * 1024 + tb * 512 + 512],
                            start=(ic == 0), stop=(ic == 31)), inc=("t" if ic == 31 else None))
                    bd.guard(f"wq{hd % 2}", "t", mk_q)
                    bd.wait(v, "t", mk_q)
                    mk_qe = bd.emit(v, lambda e, psq=psq, hd=hd, tb=tb: e.tensor_copy(
                        B[:, hd * 1024 + tb * 512:hd * 1024 + tb * 512 + 512], psq[:, :]), inc="v")
                    bd.guard(f"ps{tb % 2}", "v", mk_qe)
                    qe_marks[(h, hd, tb)] = mk_qe
            phaseQ_t = bd.c["t"].n
            if dump and h == DUMP_H:
                bd.wait(x_, "v", bd.c["v"].n)
                mkd = bd.emit(x_, lambda e: e.dma_start(dmp["d_qT"], B[:, :]), inc="dmps", inck=16)
                bd.guard("bufB", "dmps", mkd)

            bd.serial = serial or ('ATTN' in serial_phases)
            bd.guard("probsT0", "t", phaseQ_t)
            bd.guard("probsT1", "t", phaseQ_t)
            bd.need(v, "ob0", "ob1")
            scidx = 0
            pvidx = 0

            def emit_scores(hd, tb):
                nonlocal scidx
                slope = SLOPES[hd]
                t0b = tok0 + tb * 512
                K_v = (t0b + 512) // 128
                seqp = (hd * 2 + tb) % 2
                pT_off = O_PRB + 8192 * seqp
                bd.need(s_, f"probsT{seqp}")
                bd.wait(t, "v", qe_marks[(h, hd, tb)])
                exp_last = None
                for kc in range(K_v):
                    bank = scidx % 2
                    scidx += 1
                    scf = 512 * bank
                    bd.need(t, f"ps{bank}")
                    mk_sc = bd.emit(t, lambda e, bank=bank, kc=kc, hd=hd, tb=tb: e.matmul(
                        PS[bank][:, :], kT[:, kc * 128:kc * 128 + 128],
                        B[:, hd * 1024 + tb * 512:hd * 1024 + tb * 512 + 512],
                        start=True, stop=True), inc="t")
                    bd.wait(v, "t", mk_sc)
                    bd.need(v, f"sc{bank}")
                    shift2 = 511 - kc * 128 + t0b
                    mk_stt = bd.emit(v, lambda e, bank=bank, scf=scf, sh=shift2, sl=slope:
                                     e.scalar_tensor_tensor(
                                         FA[:, scf:scf + 512], rel_sb[:, sh:sh + 512], sl,
                                         PS[bank][:, :], ALU.mult, ALU.add), inc="v")
                    bd.guard(f"ps{bank}", "v", mk_stt)
                    bd.wait(s_, "v", mk_stt)
                    mk_exp = bd.emit(s_, lambda e, scf=scf, po=pT_off, kc=kc: e.activation(
                        W[:, po + kc * 512:po + kc * 512 + 512], FA[:, scf:scf + 512],
                        AF.Exp, bias=consts[:, 0:1]), inc="s")
                    bd.guard(f"sc{bank}", "s", mk_exp)
                    exp_last = mk_exp
                return exp_last

            def emit_pv(hd, tb, exp_last):
                nonlocal pvidx
                t0b = tok0 + tb * 512
                K_v = (t0b + 512) // 128
                seqp = (hd * 2 + tb) % 2
                pT_off = O_PRB + 8192 * seqp
                for s4 in range(4):
                    pvb = 2 + (pvidx % 2)
                    pvidx += 1
                    bd.wait(t, "s", exp_last)
                    bd.need(t, f"ps{pvb}")
                    for kc in range(K_v):
                        mk_pv = bd.emit(t, lambda e, pvb=pvb, kc=kc, s4=s4, po=pT_off: e.matmul(
                            PS[pvb][:, 0:130],
                            W[:, po + kc * 512 + s4 * 128:po + kc * 512 + s4 * 128 + 128],
                            v_ext[:, kc * 132:kc * 132 + 130],
                            start=(kc == 0), stop=(kc == K_v - 1)),
                            inc=("t" if kc == K_v - 1 else None))
                    bd.guard(f"probsT{seqp}", "t", mk_pv)
                    bd.wait(v, "t", mk_pv)
                    mk_rc = bd.emit(v, lambda e, pvb=pvb, s4=s4: e.reciprocal(
                        stat[:, s4:s4 + 1], PS[pvb][:, 128:129]), inc="v")
                    bd.wait(v, "v", mk_rc)
                    bd.need(v, f"asc{s4 % 2}")
                    mk_sc2 = bd.emit(v, lambda e, pvb=pvb, s4=s4: e.tensor_scalar(
                        asc[:, (s4 % 2) * 128:(s4 % 2) * 128 + 128], PS[pvb][:, 0:128],
                        stat[:, s4:s4 + 1], None, ALU.mult), inc="v")
                    bd.guard(f"ps{pvb}", "v", mk_sc2)
                    bd.wait(t, "v", mk_sc2)
                    bd.need(t, f"pt{s4 % 2}")
                    mk_ta = bd.emit(t, lambda e, s4=s4: e.transpose(
                        PT[s4 % 2][:, 0:128], asc[:, (s4 % 2) * 128:(s4 % 2) * 128 + 128],
                        ident_bf[:, :]), inc="t")
                    bd.guard(f"asc{s4 % 2}", "t", mk_ta)
                    bd.wait(v, "t", mk_ta)
                    bd.need(v, "bufA")
                    mk_ae = bd.emit(v, lambda e, s4=s4, hd=hd, tb=tb: e.tensor_copy(
                        A[:, hd * 1024 + tb * 512 + s4 * 128:hd * 1024 + tb * 512 + s4 * 128 + 128],
                        PT[s4 % 2][:, 0:128]), inc="v")
                    bd.guard(f"pt{s4 % 2}", "v", mk_ae)

            prev = None
            for hd in range(NH):
                for tb in range(2):
                    el = emit_scores(hd, tb)
                    if prev is not None:
                        emit_pv(prev[0], prev[1], prev[2])
                    prev = (hd, tb, el)
            emit_pv(prev[0], prev[1], prev[2])
            attn_v = bd.c["v"].n
            if dump and h == DUMP_H:
                bd.wait(x_, "v", attn_v)
                bd.emit(x_, lambda e: e.dma_start(dmp["d_attnT"], A[:, :]), inc="dmps", inck=16)
                mkd = bd.emit(x_, lambda e: e.dma_start(dmp["d_zr"], zr[:, :]), inc="dmps", inck=16)
                bd.guard("bufA", "dmps", mkd)

            bd.serial = serial or ('C' in serial_phases)
            for op in range(16):
                poff = O_WO + 8192 * (op % 2)
                bd.need(g, f"wo{op % 2}", f"probsT{op % 2}")
                mk_wo = bd.emit(g, lambda e, op=op, po=poff: e.dma_start(
                    W[:, po:po + 8192].rearrange("p (c n) -> p c n", c=32),
                    wo_d.rearrange("(c p) n -> p c n", p=128)[:, :, op * 256:op * 256 + 256]),
                    inc=f"wos{op % 2}", inck=16)
                bd.wait(t, f"wos{op % 2}", mk_wo)
                bd.wait(t, "v", attn_v)
                for sI in range(8):
                    bank = sI % 2
                    bd.need(t, f"ps{bank}")
                    for dc in range(32):
                        mk_c = bd.emit(t, lambda e, bank=bank, dc=dc, sI=sI, po=poff: e.matmul(
                            PS[bank][:, 0:256],
                            A[:, dc * 1024 + sI * 128:dc * 1024 + sI * 128 + 128],
                            W[:, po + dc * 256:po + dc * 256 + 256],
                            start=(dc == 0), stop=(dc == 31)), inc=("t" if dc == 31 else None))
                    bd.guard(f"wo{op % 2}", "t", mk_c)
                    bd.wait(s_, "t", mk_c)
                    bd.need(s_, "bufB")
                    mk_cn = bd.emit(s_, lambda e, bank=bank, sI=sI, op=op: e.activation(
                        B[:, sI * 4096 + op * 256:sI * 4096 + op * 256 + 256],
                        PS[bank][:, 0:256], AF.Copy), inc="s")
                    bd.guard(f"ps{bank}", "s", mk_cn)
            phaseC_t = bd.c["t"].n
            phaseC_s = bd.c["s"].n

            bd.serial = serial or ('LN2' in serial_phases) or ('DP' in serial_phases)
            bd.need(g, "art", "wo0")
            mk_art = bd.emit(g, lambda e: e.dma_start(
                W[:, O_ART:O_ART + 2560].rearrange("p (c n) -> p c n", c=32),
                art_d.rearrange("c p n -> p c n")), inc="arts", inck=16)
            for sI in range(8):
                t0 = tok0 + sI * 128
                xoff = O_X2 + 4096 * (sI % 2)
                bd.need(g, f"x2bf{sI % 2}", "wo1")
                mk_x = bd.emit(g, lambda e, t0=t0, xo=xoff: e.dma_start(
                    W[:, xo:xo + D], x_d[t0:t0 + 128, :]), inc=f"x2b{sI % 2}", inck=16)
                bd.wait(v, f"x2b{sI % 2}", mk_x)
                bd.wait(v, "s", phaseC_s)
                mk_x1 = bd.emit(v, lambda e, sI=sI, xo=xoff: e.tensor_add(
                    B[:, sI * 4096:sI * 4096 + D], B[:, sI * 4096:sI * 4096 + D],
                    W[:, xo:xo + D]), inc="v")
                bd.wait(s_, "v", mk_x1)
                mk_r = stats_ln(B[:, sI * 4096:sI * 4096 + D], O_SQ2, ("sq2", "wo0"))
                bd.guard("sq2", "s", bd.c["s"].n)
                bd.wait(v, "s", mk_r)
                mk_ln = bd.emit(v, lambda e, sI=sI, xo=xoff: e.tensor_scalar(
                    W[:, xo:xo + D], B[:, sI * 4096:sI * 4096 + D], stat[:, 2:3], stat[:, 7:8],
                    ALU.subtract, ALU.mult), inc="v")
                bd.wait(t, "v", mk_ln)
                mk_t, _ = transpose_tile_to_A(xoff, sI)
                bd.guard(f"x2bf{sI % 2}", "t", mk_t)
                ln2_done_mk = bd.c["v"].n
                bd.wait(t, "arts", mk_art)
                bd.wait(t, "v", ln2_done_mk)
                bd.need(t, "psr")
                for ic in range(32):
                    mk_rh = bd.emit(t, lambda e, ic=ic, sI=sI: e.matmul(
                        PSR[:, 0:80], A[:, ic * 1024 + sI * 128:ic * 1024 + sI * 128 + 128],
                        W[:, O_ART + ic * 80:O_ART + ic * 80 + 80],
                        start=(ic == 0), stop=(ic == 31)), inc=("t" if ic == 31 else None))
                bd.guard("art", "t", mk_rh)
                bd.wait(v, "t", mk_rh)
                bd.emit(v, lambda e: e.tensor_copy(FA[:, 48:112], PSR[:, 0:64]))
                mk_nm = bd.emit(v, lambda e: e.reduce_max(stat[:, 6:7], PSR[:, 64:80],
                                                          axis=AX.X, negate=True), inc="v")
                bd.wait(s_, "v", mk_nm)
                mk_el = bd.emit(s_, lambda e: e.activation(FA[:, 0:16], PSR[:, 64:80],
                                                           AF.Exp, bias=stat[:, 6:7]), inc="s")
                bd.guard("psr", "s", mk_el)
                bd.wait(v, "s", mk_el)
                tiny = []
                tiny.append(lambda e: e.reduce_sum(stat[:, 3:4], FA[:, 0:16], axis=AX.X))
                tiny.append(lambda e: e.reduce_max(stat[:, 0:1], FA[:, 0:16], axis=AX.X))
                tiny.append(lambda e: e.tensor_scalar(FA[:, 32:48], FA[:, 0:16], stat[:, 0:1],
                                                      None, ALU.is_lt))
                tiny.append(lambda e: e.tensor_mul(FA[:, 16:32], FA[:, 0:16], FA[:, 32:48]))
                tiny.append(lambda e: e.reduce_max(stat[:, 2:3], FA[:, 16:32], axis=AX.X))
                tiny.append(lambda e: e.tensor_scalar(stat[:, 3:4], stat[:, 3:4], 1e-6, None,
                                                      ALU.mult))
                tiny.append(lambda e: e.tensor_add(stat[:, 3:4], stat[:, 3:4], stat[:, 0:1]))
                tiny.append(lambda e: e.tensor_add(stat[:, 3:4], stat[:, 3:4], stat[:, 2:3]))
                tiny.append(lambda e: e.reciprocal(stat[:, 4:5], stat[:, 3:4]))
                tiny.append(lambda e: e.tensor_scalar(FA[:, 32:48], FA[:, 0:16], stat[:, 2:3],
                                                      None, ALU.is_ge))
                tiny.append(lambda e: e.tensor_scalar(FA[:, 16:32], FA[:, 0:16], stat[:, 4:5],
                                                      0.5, ALU.mult, ALU.mult))
                tiny.append(lambda e: e.tensor_mul(FA[:, 16:32], FA[:, 16:32], FA[:, 32:48]))
                for fn in tiny:
                    mk_cmb = bd.emit(v, fn, inc="v")
                    bd.wait(v, "v", mk_cmb)
                for eI in range(16):
                    slot = O_WSC + 128 * (eI % 2)
                    bd.need(v, f"wrh{eI % 2}", "wrhT")
                    mk_w = bd.emit(v, lambda e, sl=slot, eI=eI: e.tensor_scalar(
                        W[:, sl:sl + 64], FA[:, 48:112], FA[:, 16 + eI:17 + eI], None, ALU.mult),
                        inc="v")
                    bd.wait(t, "v", mk_w)
                    bd.need(t, f"pt{eI % 2}")
                    mk_tw = bd.emit(t, lambda e, sl=slot, eI=eI: e.transpose(
                        PT[eI % 2][:, 0:128], W[:, sl:sl + 128], ident_bf[:, :]), inc="t")
                    bd.guard(f"wrh{eI % 2}", "t", mk_tw)
                    bd.wait(v, "t", mk_tw)
                    half_p = 64 * (eI % 2)
                    dst = O_WRT + (eI // 2) * 1024 + sI * 128
                    mk_we = bd.emit(v, lambda e, eI=eI, hp=half_p, do=dst: e.tensor_copy(
                        W[hp:hp + 64, do:do + 128], PT[eI % 2][0:64, 0:128]), inc="v")
                    bd.guard(f"pt{eI % 2}", "v", mk_we)
            dpre_v = bd.c["v"].n

            bd.serial = serial or ('DM' in serial_phases)
            for op in range(16):
                slot = (op + 1) % 2
                poff = O_WX + 10496 * slot
                bd.need(g, f"wx{slot}")
                if op == 0:
                    bd.wait(g, "t", phaseC_t)
                    bd.need(g, "x2bf0", "x2bf1")
                if op == 1:
                    bd.wait(g, "v", dpre_v)
                mk_w1 = bd.emit(g, lambda e, op=op, po=poff: e.dma_start(
                    W[:, po:po + 8192].rearrange("p (c n) -> p c n", c=32),
                    bwt_d.rearrange("(c p) n -> p c n", p=128)[:, :, op * 256:op * 256 + 256]),
                    inc=f"wxs{slot}", inck=16)
                mk_w2 = bd.emit(g, lambda e, op=op, po=poff: e.dma_start(
                    W[:, po + 8192:po + 10240].rearrange("p (c n) -> p c n", c=8),
                    bext_d.rearrange("c p n -> p c n")[:, :, op * 256:op * 256 + 256]),
                    inc=f"wxs{slot}", inck=16)
                mk_w3 = bd.emit(g, lambda e, op=op, po=poff: e.dma_start(
                    W[0:1, po + 10240:po + 10496], bb_d[0:1, op * 256:op * 256 + 256]),
                    inc=f"wxs{slot}", inck=16)
                bd.wait(t, f"wxs{slot}", mk_w3)
                bd.wait(t, "v", dpre_v)
                for sI in range(8):
                    bank = sI % 2
                    bd.need(t, f"ps{bank}")
                    for j in range(32):
                        bd.emit(t, lambda e, bank=bank, j=j, sI=sI, po=poff: e.matmul(
                            PS[bank][:, 0:256],
                            A[:, j * 1024 + sI * 128:j * 1024 + sI * 128 + 128],
                            W[:, po + j * 256:po + j * 256 + 256], start=(j == 0), stop=False))
                    for j in range(8):
                        bd.emit(t, lambda e, bank=bank, j=j, sI=sI, po=poff: e.matmul(
                            PS[bank][:, 0:256],
                            W[:, O_WRT + j * 1024 + sI * 128:O_WRT + j * 1024 + sI * 128 + 128],
                            W[:, po + (32 + j) * 256:po + (32 + j) * 256 + 256],
                            start=False, stop=False))
                    mk_d = bd.emit(t, lambda e, bank=bank, po=poff: e.matmul(
                        PS[bank][:, 0:256], ones_r[:, :], W[0:1, po + 10240:po + 10496],
                        start=False, stop=True), inc="t")
                    bd.guard(f"wx{slot}", "t", mk_d)
                    bd.wait(v, "t", mk_d)
                    ob = 256 * (sI % 2)
                    bd.need(v, f"ob{sI % 2}")
                    mk_o = bd.emit(v, lambda e, bank=bank, ob=ob, sI=sI, op=op: e.tensor_add(
                        FA[:, ob:ob + 256], PS[bank][:, 0:256],
                        B[:, sI * 4096 + op * 256:sI * 4096 + op * 256 + 256]), inc="v")
                    bd.guard(f"ps{bank}", "v", mk_o)
                    bd.wait(x_, "v", mk_o)
                    t0 = tok0 + sI * 128
                    mk_y = bd.emit(x_, lambda e, ob=ob, t0=t0, op=op: e.dma_start(
                        y_d[t0:t0 + 128, op * 256:op * 256 + 256], FA[:, ob:ob + 256]),
                        inc=f"obs{sI % 2}", inck=16)
                    bd.guard(f"ob{sI % 2}", f"obs{sI % 2}", mk_y)
            end_t = bd.c["t"].n
            for key in ("war_lo", "xbf0", "xbf1", "wq0", "wq1", "wrhT", "bufA",
                        "kscr", "sqA"):
                bd.guard(key, "t", end_t)

        for sem_name in ("obs0", "obs1", "dmps"):
            if sem_name in bd.c:
                fin = bd.c[sem_name].n
                bd.wait(t, sem_name, fin)
                bd.wait(v, sem_name, fin)

        with nc.Block() as block:
            @block.gpsimd
            def _(e):
                for f in bd.q["g"]:
                    f(e)

            @block.sync
            def _(e):
                for f in bd.q["x"]:
                    f(e)

            @block.tensor
            def _(e):
                for f in bd.q["t"]:
                    f(e)

            @block.vector
            def _(e):
                for f in bd.q["v"]:
                    f(e)

            @block.scalar
            def _(e):
                for f in bd.q["s"]:
                    f(e)
    return nc


def _host_inputs(L0_attn_hidden_states, L0_attn_Wq, L0_attn_Wkv, L0_attn_Wo,
                 L0_ffn_base_w, L0_ffn_base_b, L0_ffn_A, L0_ffn_B, L0_ffn_router_w):
    f32 = np.float32
    wq = np.ascontiguousarray(L0_attn_Wq, dtype=f32)
    wkv = np.ascontiguousarray(L0_attn_Wkv, dtype=f32)
    wo = np.ascontiguousarray(L0_attn_Wo, dtype=f32)
    bwt = np.ascontiguousarray(np.asarray(L0_ffn_base_w, dtype=f32).T)
    bext = np.ascontiguousarray(
        np.asarray(L0_ffn_B, dtype=f32).transpose(0, 2, 1).reshape(8, 128, D))
    art = np.ascontiguousarray(np.concatenate(
        [np.asarray(L0_ffn_A, dtype=f32).T.reshape(32, 128, 64),
         np.asarray(L0_ffn_router_w, dtype=f32).T.reshape(32, 128, 16)], axis=2))
    bb = np.ascontiguousarray(np.asarray(L0_ffn_base_b, dtype=f32).reshape(1, D))
    p = np.arange(128)[:, None]
    j = np.arange(RELW)[None, :]
    u = (p - j + 511).astype(f32)
    relf = np.where(u <= 0, u, np.float32(MASKVAL)).astype(np.float16)
    ident = np.eye(128, dtype=f32)
    consts = np.zeros((128, 8), dtype=f32)
    consts[:, 0] = -SHIFT
    consts[:, 1] = -0.5
    consts[:, 3] = 1e-5
    shared = dict(wq=wq, wkv=wkv, wo=wo, bwt=bwt, bext=bext, art=art, bb=bb,
                  rel=relf, ident=ident, consts=consts)
    xs = np.ascontiguousarray(np.asarray(L0_attn_hidden_states, dtype=f32))
    return xs, shared


def kernel(**inputs):
    xs, shared = _host_inputs(**inputs)
    nc = build_nc()
    in_maps = [dict(shared, x=np.ascontiguousarray(xs[c])) for c in range(NC)]
    res = run_bass_kernel_spmd(nc, in_maps, core_ids=list(range(NC)))
    return np.stack([np.asarray(res.results[c]["y"], dtype=np.float32) for c in range(NC)],
                    axis=0)


# version 07
# speedup vs baseline: 1.1142x; 1.0394x over previous
import contextlib
import numpy as np
import concourse.bass as bass
import concourse.mybir as mybir
from concourse.bass_utils import run_bass_kernel_spmd

F32 = mybir.dt.float32
BF16 = mybir.dt.bfloat16
FP16 = mybir.dt.float16
AF = mybir.ActivationFunctionType
ALU = mybir.AluOpType
AX = mybir.AxisListType

S = 2048
D = 4096
NH = 32
NC = 8
HALF = 1024
SHIFT = 12.0
MASKVAL = -60000.0
RELW = 2560
WAR = 29184
SCALE = 1.0 / float(np.sqrt(128.0))

O_WKV = 0
O_XBF = 8192
O_SQA = 16384
O_KSC = 20480
O_WQ = 0
O_PRB = 0
O_WO = 0
O_SQ2 = 0
O_X2 = 8192
O_ART = 4096
O_WSC = 6656
O_WX = 0
O_WRT = 20992


def _alibi_slopes(n_heads=32):
    import math
    closest = 2 ** math.floor(math.log2(n_heads))
    base = 2.0 ** (-(2.0 ** (-(math.log2(closest) - 3))))
    slopes = [base ** i for i in range(1, closest + 1)]
    if closest != n_heads:
        extra = 2.0 ** (-(2.0 ** (-(math.log2(2 * closest) - 3))))
        n_extra = min(closest, n_heads - closest)
        slopes += [extra ** i for i in range(1, 2 * n_extra + 1, 2)]
    return [float(np.float32(s)) for s in slopes]

SLOPES = _alibi_slopes(NH)


class Ctr:
    def __init__(self, es, nc, name):
        self.h = es.enter_context(nc.semaphore(name))
        self.n = 0

    def bump(self, k=1):
        self.n += k
        return self.n


class Build:
    def __init__(self, nc, es, serial=False):
        self.nc = nc
        self.es = es
        self.serial = serial
        self.last = None
        self.q = {e: [] for e in ("t", "v", "s", "g", "x")}
        self.c = {}
        for e in ("t", "v", "s", "g", "x"):
            self.ctr(e)
        self.waited = {e: {} for e in self.q}
        self.guards = {}

    def ctr(self, name):
        if name not in self.c:
            self.c[name] = Ctr(self.es, self.nc, "sem_" + name)
        return self.c[name]

    def wait(self, eng, ctr_name, mark):
        if mark is None or mark <= 0 or mark <= self.waited[eng].get(ctr_name, 0):
            return
        self.waited[eng][ctr_name] = mark
        h = self.ctr(ctr_name).h
        self.q[eng].append(lambda e, h=h, m=mark: e.wait_ge(h, m))

    def emit(self, eng, fn, inc=None, inck=1):
        if self.serial:
            if self.last is not None:
                self.wait(eng, self.last[0], self.last[1])
            if inc is None:
                inc = {"t": "t", "v": "v", "s": "s", "g": "gp", "x": "gp"}[eng]
        if inc is None:
            self.q[eng].append(fn)
            return None
        ctr = self.ctr(inc)
        mark = ctr.bump(inck)
        h = ctr.h

        def run(e, fn=fn, h=h, k=inck):
            fn(e).then_inc(h, k)
        self.q[eng].append(run)
        self.last = (inc, mark)
        return mark

    def need(self, eng, *keys):
        for key in keys:
            g = self.guards.get(key)
            if g is not None:
                self.wait(eng, g[0], g[1])

    def guard(self, key, ctr_name, mark):
        old = self.guards.get(key)
        if old is None or old[0] != ctr_name or (mark is not None and mark > old[1]):
            if mark is not None:
                self.guards[key] = (ctr_name, mark)


DUMP_H = 0


def build_nc(dump=False, serial=False, serial_phases=()):
    nc = bass.Bass("TRN2", target_bir_lowering=False, debug=False)
    x_d = nc.dram_tensor("x", [S, D], F32, kind="ExternalInput").ap()
    wq_d = nc.dram_tensor("wq", [D, D], F32, kind="ExternalInput").ap()
    wkv_d = nc.dram_tensor("wkv", [D, 256], F32, kind="ExternalInput").ap()
    wo_d = nc.dram_tensor("wo", [D, D], F32, kind="ExternalInput").ap()
    bwt_d = nc.dram_tensor("bwt", [D, D], F32, kind="ExternalInput").ap()
    bext_d = nc.dram_tensor("bext", [8, 128, D], F32, kind="ExternalInput").ap()
    art_d = nc.dram_tensor("art", [32, 128, 80], F32, kind="ExternalInput").ap()
    bb_d = nc.dram_tensor("bb", [1, D], F32, kind="ExternalInput").ap()
    rel_d = nc.dram_tensor("rel", [128, RELW], FP16, kind="ExternalInput").ap()
    ident_d = nc.dram_tensor("ident", [128, 128], F32, kind="ExternalInput").ap()
    consts_d = nc.dram_tensor("consts", [128, 8], F32, kind="ExternalInput").ap()
    y_d = nc.dram_tensor("y", [S, D], F32, kind="ExternalOutput").ap()
    dmp = {}
    if dump:
        for nm, shp, dt in [("d_lnT", [128, 32768], BF16), ("d_kT", [128, 2048], BF16),
                            ("d_v", [128, 16 * 132], BF16), ("d_qT", [128, 32768], BF16),
                            ("d_attnT", [128, 32768], BF16), ("d_zr", [128, 8], F32),
                            ("d_asc", [128, 128], BF16), ("d_st", [128, 8], F32),
                            ("d_att0", [128, 512], BF16), ("d_ps", [128, 256], BF16), ("d_stA", [128, 8], F32),
                            ("d_x1", [128, 32768], BF16), ("d_ln2T", [128, 32768], BF16),
                            ("d_prb", [128, 8192], BF16)]:
            dmp[nm] = nc.dram_tensor(nm, shp, dt, kind="ExternalOutput").ap()

    es = contextlib.ExitStack()
    with es:
        sbt = lambda n, s, d: es.enter_context(nc.sbuf_tensor(n, s, d))
        pst = lambda n, s, d: es.enter_context(nc.psum_tensor(n, s, d))
        A = sbt("A", [128, 32 * 1024], BF16)
        B = sbt("Bb", [128, 32 * 1024], BF16)
        kT = sbt("kT", [128, 2048], BF16)
        v_ext = sbt("v_ext", [128, 16 * 132], BF16)
        W = sbt("W", [128, WAR], BF16)
        FA = sbt("FA", [128, 1024], F32)
        rel_sb = sbt("rel_sb", [128, RELW], FP16)
        ident_bf = sbt("ident_bf", [128, 128], BF16)
        consts = sbt("consts_sb", [128, 8], F32)
        ones_r = sbt("ones_r", [1, 128], BF16)
        ones_c = sbt("ones_c", [128, 1], BF16)
        zr = sbt("zr", [128, 8], F32)
        stat = sbt("stat", [128, 8], F32)
        PS = [pst(f"PS{i}", [128, 512], F32) for i in range(4)]
        PT = [pst("PTA", [128, 1024], BF16), pst("PTB", [128, 1024], BF16)]
        PSR = pst("PSR", [128, 512], F32)
        asc = sbt("asc", [128, 256], BF16)

        bd = Build(nc, es, serial=serial)
        t, v, s_, g, x_ = "t", "v", "s", "g", "x"

        bd.emit(g, lambda e: e.dma_start(ident_bf[:, :], ident_d), inc="ident", inck=16)
        bd.emit(g, lambda e: e.memset(ones_r[:, :], 1.0), inc="gp")
        bd.emit(g, lambda e: e.memset(ones_c[:, :], 1.0), inc="gp")
        bd.emit(g, lambda e: e.memset(W[:, O_WSC:O_WSC + 256], 0.0), inc="gp")
        bd.emit(g, lambda e: e.memset(v_ext[:, :], 1.0), inc="gp")
        bd.emit(x_, lambda e: e.dma_start(rel_sb[:, :], rel_d), inc="rel", inck=16)
        bd.emit(x_, lambda e: e.dma_start(consts[:, :], consts_d), inc="cst", inck=16)
        bd.wait(t, "ident", 16)
        bd.wait(t, "gp", 4)
        bd.wait(g, "gp", 4)
        bd.wait(v, "rel", 16)
        bd.wait(v, "cst", 16)
        bd.wait(s_, "cst", 16)

        def stats_ln(src_ap, sq_off, needkeys):
            mk0 = bd.emit(v, lambda e: e.reduce_sum(stat[:, 0:1], src_ap, axis=AX.X), inc="v")
            bd.need(s_, *needkeys)
            mk_s = bd.emit(s_, lambda e: e.activation(
                W[:, sq_off:sq_off + D], src_ap, AF.Square, accum_out=stat[:, 1:2]), inc="s")
            bd.wait(v, "v", mk0)
            bd.wait(v, "s", mk_s)
            mk = bd.emit(v, lambda e: e.tensor_scalar(stat[:, 2:3], stat[:, 0:1], 1.0 / D,
                                                      None, ALU.mult), inc="v")
            bd.wait(v, "v", mk)
            mk = bd.emit(v, lambda e: e.tensor_scalar(stat[:, 3:4], stat[:, 1:2], 1.0 / D,
                                                      None, ALU.mult), inc="v")
            bd.wait(v, "v", mk)
            mk = bd.emit(v, lambda e: e.tensor_mul(stat[:, 4:5], stat[:, 2:3], stat[:, 2:3]),
                         inc="v")
            bd.wait(v, "v", mk)
            mk_v = bd.emit(v, lambda e: e.scalar_tensor_tensor(
                stat[:, 5:6], stat[:, 4:5], -1.0, stat[:, 3:4], ALU.mult, ALU.add), inc="v")
            bd.wait(s_, "v", mk_v)
            mk_l = bd.emit(s_, lambda e: e.activation(stat[:, 6:7], stat[:, 5:6], AF.Ln,
                                                      bias=consts[:, 3:4]), inc="s")
            bd.wait(s_, "s", mk_l)
            mk_r = bd.emit(s_, lambda e: e.activation(stat[:, 7:8], stat[:, 6:7], AF.Exp,
                                                      scale=consts[:, 1:2]), inc="s")
            return mk_r

        def transpose_tile_to_A(xoff, sI):
            last_t = None
            for gI in range(8):
                pt = PT[gI % 2]
                bd.need(t, f"pt{gI % 2}")
                for j in range(4):
                    ic = gI * 4 + j
                    mk_t = bd.emit(t, lambda e, pt=pt, j=j, xo=xoff, ic=ic: e.transpose(
                        pt[:, j * 128:j * 128 + 128], W[:, xo + ic * 128:xo + ic * 128 + 128],
                        ident_bf[:, :]), inc=("t" if j == 3 else None))
                eng, enm = (v, "v") if gI % 2 == 0 else (s_, "s")
                bd.wait(eng, "t", mk_t)
                bd.need(eng, "bufA")
                if gI % 2 == 0:
                    mk_e = bd.emit(v, lambda e, pt=pt, gI=gI, sI=sI: e.tensor_copy(
                        A[:, :].rearrange("p (c n) -> p c n", n=1024)[:, gI * 4:gI * 4 + 4,
                                                                     sI * 128:sI * 128 + 128],
                        pt[:, 0:512].rearrange("p (c n) -> p c n", c=4)), inc="v")
                else:
                    mk_e = bd.emit(s_, lambda e, pt=pt, gI=gI, sI=sI: e.copy(
                        A[:, :].rearrange("p (c n) -> p c n", n=1024)[:, gI * 4:gI * 4 + 4,
                                                                     sI * 128:sI * 128 + 128],
                        pt[:, 0:512].rearrange("p (c n) -> p c n", c=4)), inc="s")
                bd.guard(f"pt{gI % 2}", enm, mk_e)
                last_t = mk_t
            return last_t, bd.c["v"].n, bd.c["s"].n

        qe_marks = {}
        for h in range(2):
            tok0 = h * HALF

            bd.serial = serial or ('A' in serial_phases)
            bd.need(g, "war_lo")
            mk_wkv = bd.emit(g, lambda e: e.dma_start(
                W[:, O_WKV:O_WKV + 8192].rearrange("p (c n) -> p c n", c=32),
                wkv_d.rearrange("(c p) n -> p c n", p=128)), inc="wkv", inck=16)
            for sI in range(8):
                t0 = tok0 + sI * 128
                gc = 8 * h + sI
                xoff = O_XBF + 4096 * (sI % 2)
                bd.need(g, f"xbf{sI % 2}")
                mk_x = bd.emit(g, lambda e, t0=t0, xo=xoff: e.dma_start(
                    W[:, xo:xo + D], x_d[t0:t0 + 128, :]), inc=f"xb{sI % 2}", inck=16)
                bd.wait(v, f"xb{sI % 2}", mk_x)
                bd.wait(s_, f"xb{sI % 2}", mk_x)
                mk_r = stats_ln(W[:, xoff:xoff + D], O_SQA, ("sqA", "wx1"))
                bd.guard("sqA", "s", bd.c["s"].n)
                bd.wait(v, "s", mk_r)
                if dump and h == DUMP_H and sI == 0:
                    mkst = bd.emit(v, lambda e: e.tensor_copy(FA[:, 896:904], stat[:, :]), inc="v")
                    bd.wait(x_, "v", mkst)
                    bd.emit(x_, lambda e: e.dma_start(dmp["d_stA"], FA[:, 896:904]),
                            inc="dmps", inck=16)
                mk_ln = bd.emit(v, lambda e, xo=xoff: e.tensor_scalar(
                    W[:, xo:xo + D], W[:, xo:xo + D], stat[:, 2:3], stat[:, 7:8],
                    ALU.subtract, ALU.mult), inc="v")
                bd.wait(t, "v", mk_ln)
                mk_t, lnT_v, lnT_s = transpose_tile_to_A(xoff, sI)
                bd.guard(f"xbf{sI % 2}", "t", mk_t)
                bd.wait(t, "v", lnT_v)
                bd.wait(t, "s", lnT_s)
                bd.wait(t, "wkv", mk_wkv)
                bd.need(t, "ps2")
                for ic in range(32):
                    mk_kv = bd.emit(t, lambda e, ic=ic, sI=sI: e.matmul(
                        PS[2][:, 0:256], A[:, ic * 1024 + sI * 128:ic * 1024 + sI * 128 + 128],
                        W[:, O_WKV + ic * 256:O_WKV + ic * 256 + 256],
                        start=(ic == 0), stop=(ic == 31)), inc=("t" if ic == 31 else None))
                bd.wait(v, "t", mk_kv)
                bd.need(v, "kscr")
                mk_k = bd.emit(v, lambda e: e.tensor_scalar(
                    W[:, O_KSC:O_KSC + 128], PS[2][:, 0:128], SCALE, None, ALU.mult), inc="v")
                bd.wait(v, "gp", 4)
                mk_vv = bd.emit(v, lambda e, gc=gc: e.tensor_copy(
                    v_ext[:, gc * 132:gc * 132 + 128], PS[2][:, 128:256]), inc="v")
                bd.guard("ps2", "v", mk_vv)
                bd.wait(t, "v", mk_k)
                bd.need(t, "pt0")
                mk_tk = bd.emit(t, lambda e: e.transpose(
                    PT[0][:, 0:128], W[:, O_KSC:O_KSC + 128], ident_bf[:, :]), inc="t")
                bd.guard("kscr", "t", mk_tk)
                bd.wait(v, "t", mk_tk)
                mk_kT = bd.emit(v, lambda e, gc=gc: e.tensor_copy(
                    kT[:, gc * 128:gc * 128 + 128], PT[0][:, 0:128]), inc="v")
                bd.guard("pt0", "v", mk_kT)
                last_kv_t = mk_tk
            bd.guard("wq0", "t", last_kv_t)
            bd.guard("wq1", "t", last_kv_t)
            phaseA_v = bd.c["v"].n
            phaseA_s = bd.c["s"].n
            if dump and h == DUMP_H:
                bd.wait(x_, "v", phaseA_v)
                bd.emit(x_, lambda e: e.dma_start(dmp["d_lnT"], A[:, :]), inc="dmps", inck=16)
                bd.emit(x_, lambda e: e.dma_start(dmp["d_kT"], kT[:, :]), inc="dmps", inck=16)
                mkd = bd.emit(x_, lambda e: e.dma_start(dmp["d_v"], v_ext[:, :]), inc="dmps", inck=16)
                bd.guard("bufA", "dmps", mkd)

            bd.serial = serial or ('Q' in serial_phases)
            for hd in range(NH):
                poff = O_WQ + 4096 * (hd % 2)
                bd.need(g, f"wq{hd % 2}")
                mk_wq = bd.emit(g, lambda e, hd=hd, po=poff: e.dma_start(
                    W[:, po:po + 4096].rearrange("p (c n) -> p c n", c=32),
                    wq_d.rearrange("(c p) n -> p c n", p=128)[:, :, hd * 128:hd * 128 + 128]),
                    inc=f"wqs{hd % 2}", inck=16)
                bd.wait(t, f"wqs{hd % 2}", mk_wq)
                bd.wait(t, "v", phaseA_v)
                for tb in range(2):
                    psq = PS[tb % 2]
                    bd.need(t, f"ps{tb % 2}")
                    for ic in range(32):
                        mk_q = bd.emit(t, lambda e, psq=psq, po=poff, ic=ic, tb=tb: e.matmul(
                            psq[:, :], W[:, po + ic * 128:po + ic * 128 + 128],
                            A[:, ic * 1024 + tb * 512:ic * 1024 + tb * 512 + 512],
                            start=(ic == 0), stop=(ic == 31)), inc=("t" if ic == 31 else None))
                    bd.guard(f"wq{hd % 2}", "t", mk_q)
                    bd.wait(v, "t", mk_q)
                    mk_qe = bd.emit(v, lambda e, psq=psq, hd=hd, tb=tb: e.tensor_copy(
                        B[:, hd * 1024 + tb * 512:hd * 1024 + tb * 512 + 512], psq[:, :]), inc="v")
                    bd.guard(f"ps{tb % 2}", "v", mk_qe)
                    qe_marks[(h, hd, tb)] = mk_qe
            phaseQ_t = bd.c["t"].n
            if dump and h == DUMP_H:
                bd.wait(x_, "v", bd.c["v"].n)
                mkd = bd.emit(x_, lambda e: e.dma_start(dmp["d_qT"], B[:, :]), inc="dmps", inck=16)
                bd.guard("bufB", "dmps", mkd)

            bd.serial = serial or ('ATTN' in serial_phases)
            bd.guard("probsT0", "t", phaseQ_t)
            bd.guard("probsT1", "t", phaseQ_t)
            bd.need(v, "ob0", "ob1")
            scidx = 0
            pvidx = 0

            def emit_scores(hd, tb):
                nonlocal scidx
                slope = SLOPES[hd]
                t0b = tok0 + tb * 512
                K_v = (t0b + 512) // 128
                seqp = (hd * 2 + tb) % 2
                pT_off = O_PRB + 8192 * seqp
                bd.need(s_, f"probsT{seqp}")
                bd.wait(t, "v", qe_marks[(h, hd, tb)])
                exp_last = None
                for kc in range(K_v):
                    bank = scidx % 2
                    scidx += 1
                    scf = 512 * bank
                    bd.need(t, f"ps{bank}")
                    mk_sc = bd.emit(t, lambda e, bank=bank, kc=kc, hd=hd, tb=tb: e.matmul(
                        PS[bank][:, :], kT[:, kc * 128:kc * 128 + 128],
                        B[:, hd * 1024 + tb * 512:hd * 1024 + tb * 512 + 512],
                        start=True, stop=True), inc="t")
                    bd.wait(v, "t", mk_sc)
                    bd.need(v, f"sc{bank}")
                    shift2 = 511 - kc * 128 + t0b
                    mk_stt = bd.emit(v, lambda e, bank=bank, scf=scf, sh=shift2, sl=slope:
                                     e.scalar_tensor_tensor(
                                         FA[:, scf:scf + 512], rel_sb[:, sh:sh + 512], sl,
                                         PS[bank][:, :], ALU.mult, ALU.add), inc="v")
                    bd.guard(f"ps{bank}", "v", mk_stt)
                    bd.wait(s_, "v", mk_stt)
                    mk_exp = bd.emit(s_, lambda e, scf=scf, po=pT_off, kc=kc: e.activation(
                        W[:, po + kc * 512:po + kc * 512 + 512], FA[:, scf:scf + 512],
                        AF.Exp, bias=consts[:, 0:1]), inc="s")
                    bd.guard(f"sc{bank}", "s", mk_exp)
                    exp_last = mk_exp
                return exp_last

            def emit_pv(hd, tb, exp_last):
                nonlocal pvidx
                t0b = tok0 + tb * 512
                K_v = (t0b + 512) // 128
                seqp = (hd * 2 + tb) % 2
                pT_off = O_PRB + 8192 * seqp
                for s4 in range(4):
                    pvb = 2 + (pvidx % 2)
                    pvidx += 1
                    bd.wait(t, "s", exp_last)
                    bd.need(t, f"ps{pvb}")
                    for kc in range(K_v):
                        mk_pv = bd.emit(t, lambda e, pvb=pvb, kc=kc, s4=s4, po=pT_off: e.matmul(
                            PS[pvb][:, 0:130],
                            W[:, po + kc * 512 + s4 * 128:po + kc * 512 + s4 * 128 + 128],
                            v_ext[:, kc * 132:kc * 132 + 130],
                            start=(kc == 0), stop=(kc == K_v - 1)),
                            inc=("t" if kc == K_v - 1 else None))
                    bd.guard(f"probsT{seqp}", "t", mk_pv)
                    bd.wait(v, "t", mk_pv)
                    mk_rc = bd.emit(v, lambda e, pvb=pvb, s4=s4: e.reciprocal(
                        stat[:, s4:s4 + 1], PS[pvb][:, 128:129]), inc="v")
                    bd.wait(v, "v", mk_rc)
                    bd.need(v, f"asc{s4 % 2}")
                    mk_sc2 = bd.emit(v, lambda e, pvb=pvb, s4=s4: e.tensor_scalar(
                        asc[:, (s4 % 2) * 128:(s4 % 2) * 128 + 128], PS[pvb][:, 0:128],
                        stat[:, s4:s4 + 1], None, ALU.mult), inc="v")
                    bd.guard(f"ps{pvb}", "v", mk_sc2)
                    bd.wait(t, "v", mk_sc2)
                    bd.need(t, f"pt{s4 % 2}")
                    mk_ta = bd.emit(t, lambda e, s4=s4: e.transpose(
                        PT[s4 % 2][:, 0:128], asc[:, (s4 % 2) * 128:(s4 % 2) * 128 + 128],
                        ident_bf[:, :]), inc="t")
                    bd.guard(f"asc{s4 % 2}", "t", mk_ta)
                    bd.wait(v, "t", mk_ta)
                    bd.need(v, "bufA")
                    mk_ae = bd.emit(v, lambda e, s4=s4, hd=hd, tb=tb: e.tensor_copy(
                        A[:, hd * 1024 + tb * 512 + s4 * 128:hd * 1024 + tb * 512 + s4 * 128 + 128],
                        PT[s4 % 2][:, 0:128]), inc="v")
                    bd.guard(f"pt{s4 % 2}", "v", mk_ae)

            prev = None
            for hd in range(NH):
                for tb in range(2):
                    el = emit_scores(hd, tb)
                    if prev is not None:
                        emit_pv(prev[0], prev[1], prev[2])
                    prev = (hd, tb, el)
            emit_pv(prev[0], prev[1], prev[2])
            attn_v = bd.c["v"].n
            if dump and h == DUMP_H:
                bd.wait(x_, "v", attn_v)
                bd.emit(x_, lambda e: e.dma_start(dmp["d_attnT"], A[:, :]), inc="dmps", inck=16)
                mkd = bd.emit(x_, lambda e: e.dma_start(dmp["d_zr"], zr[:, :]), inc="dmps", inck=16)
                bd.guard("bufA", "dmps", mkd)

            bd.serial = serial or ('C' in serial_phases)
            for op in range(16):
                poff = O_WO + 8192 * (op % 2)
                bd.need(g, f"wo{op % 2}", f"probsT{op % 2}")
                mk_wo = bd.emit(g, lambda e, op=op, po=poff: e.dma_start(
                    W[:, po:po + 8192].rearrange("p (c n) -> p c n", c=32),
                    wo_d.rearrange("(c p) n -> p c n", p=128)[:, :, op * 256:op * 256 + 256]),
                    inc=f"wos{op % 2}", inck=16)
                bd.wait(t, f"wos{op % 2}", mk_wo)
                bd.wait(t, "v", attn_v)
                for sI in range(8):
                    bank = sI % 2
                    bd.need(t, f"ps{bank}")
                    for dc in range(32):
                        mk_c = bd.emit(t, lambda e, bank=bank, dc=dc, sI=sI, po=poff: e.matmul(
                            PS[bank][:, 0:256],
                            A[:, dc * 1024 + sI * 128:dc * 1024 + sI * 128 + 128],
                            W[:, po + dc * 256:po + dc * 256 + 256],
                            start=(dc == 0), stop=(dc == 31)), inc=("t" if dc == 31 else None))
                    bd.guard(f"wo{op % 2}", "t", mk_c)
                    bd.wait(s_, "t", mk_c)
                    bd.need(s_, "bufB")
                    mk_cn = bd.emit(s_, lambda e, bank=bank, sI=sI, op=op: e.activation(
                        B[:, sI * 4096 + op * 256:sI * 4096 + op * 256 + 256],
                        PS[bank][:, 0:256], AF.Copy), inc="s")
                    bd.guard(f"ps{bank}", "s", mk_cn)
            phaseC_t = bd.c["t"].n
            phaseC_s = bd.c["s"].n

            bd.serial = serial or ('LN2' in serial_phases) or ('DP' in serial_phases)
            bd.need(g, "art", "wo0")
            mk_art = bd.emit(g, lambda e: e.dma_start(
                W[:, O_ART:O_ART + 2560].rearrange("p (c n) -> p c n", c=32),
                art_d.rearrange("c p n -> p c n")), inc="arts", inck=16)
            for sI in range(8):
                t0 = tok0 + sI * 128
                xoff = O_X2 + 4096 * (sI % 2)
                bd.need(g, f"x2bf{sI % 2}", "wo1")
                mk_x = bd.emit(g, lambda e, t0=t0, xo=xoff: e.dma_start(
                    W[:, xo:xo + D], x_d[t0:t0 + 128, :]), inc=f"x2b{sI % 2}", inck=16)
                bd.wait(v, f"x2b{sI % 2}", mk_x)
                bd.wait(v, "s", phaseC_s)
                mk_x1 = bd.emit(v, lambda e, sI=sI, xo=xoff: e.tensor_add(
                    B[:, sI * 4096:sI * 4096 + D], B[:, sI * 4096:sI * 4096 + D],
                    W[:, xo:xo + D]), inc="v")
                bd.wait(s_, "v", mk_x1)
                mk_r = stats_ln(B[:, sI * 4096:sI * 4096 + D], O_SQ2, ("sq2", "wo0"))
                bd.guard("sq2", "s", bd.c["s"].n)
                bd.wait(v, "s", mk_r)
                mk_ln = bd.emit(v, lambda e, sI=sI, xo=xoff: e.tensor_scalar(
                    W[:, xo:xo + D], B[:, sI * 4096:sI * 4096 + D], stat[:, 2:3], stat[:, 7:8],
                    ALU.subtract, ALU.mult), inc="v")
                bd.wait(t, "v", mk_ln)
                mk_t, _lv, _ls = transpose_tile_to_A(xoff, sI)
                bd.guard(f"x2bf{sI % 2}", "t", mk_t)
                ln2_done_mk = bd.c["v"].n
                ln2_done_s = bd.c["s"].n
                bd.wait(t, "arts", mk_art)
                bd.wait(t, "v", ln2_done_mk)
                bd.wait(t, "s", ln2_done_s)
                bd.need(t, "psr")
                for ic in range(32):
                    mk_rh = bd.emit(t, lambda e, ic=ic, sI=sI: e.matmul(
                        PSR[:, 0:80], A[:, ic * 1024 + sI * 128:ic * 1024 + sI * 128 + 128],
                        W[:, O_ART + ic * 80:O_ART + ic * 80 + 80],
                        start=(ic == 0), stop=(ic == 31)), inc=("t" if ic == 31 else None))
                bd.guard("art", "t", mk_rh)
                bd.wait(v, "t", mk_rh)
                bd.emit(v, lambda e: e.tensor_copy(FA[:, 48:112], PSR[:, 0:64]))
                mk_nm = bd.emit(v, lambda e: e.reduce_max(stat[:, 6:7], PSR[:, 64:80],
                                                          axis=AX.X, negate=True), inc="v")
                bd.wait(s_, "v", mk_nm)
                mk_el = bd.emit(s_, lambda e: e.activation(FA[:, 0:16], PSR[:, 64:80],
                                                           AF.Exp, bias=stat[:, 6:7]), inc="s")
                bd.guard("psr", "s", mk_el)
                bd.wait(v, "s", mk_el)
                tiny = []
                tiny.append(lambda e: e.reduce_sum(stat[:, 3:4], FA[:, 0:16], axis=AX.X))
                tiny.append(lambda e: e.reduce_max(stat[:, 0:1], FA[:, 0:16], axis=AX.X))
                tiny.append(lambda e: e.tensor_scalar(FA[:, 32:48], FA[:, 0:16], stat[:, 0:1],
                                                      None, ALU.is_lt))
                tiny.append(lambda e: e.tensor_mul(FA[:, 16:32], FA[:, 0:16], FA[:, 32:48]))
                tiny.append(lambda e: e.reduce_max(stat[:, 2:3], FA[:, 16:32], axis=AX.X))
                tiny.append(lambda e: e.tensor_scalar(stat[:, 3:4], stat[:, 3:4], 1e-6, None,
                                                      ALU.mult))
                tiny.append(lambda e: e.tensor_add(stat[:, 3:4], stat[:, 3:4], stat[:, 0:1]))
                tiny.append(lambda e: e.tensor_add(stat[:, 3:4], stat[:, 3:4], stat[:, 2:3]))
                tiny.append(lambda e: e.reciprocal(stat[:, 4:5], stat[:, 3:4]))
                tiny.append(lambda e: e.tensor_scalar(FA[:, 32:48], FA[:, 0:16], stat[:, 2:3],
                                                      None, ALU.is_ge))
                tiny.append(lambda e: e.tensor_scalar(FA[:, 16:32], FA[:, 0:16], stat[:, 4:5],
                                                      0.5, ALU.mult, ALU.mult))
                tiny.append(lambda e: e.tensor_mul(FA[:, 16:32], FA[:, 16:32], FA[:, 32:48]))
                for fn in tiny:
                    mk_cmb = bd.emit(v, fn, inc="v")
                    bd.wait(v, "v", mk_cmb)
                for eI in range(16):
                    slot = O_WSC + 128 * (eI % 2)
                    bd.need(v, f"wrh{eI % 2}", "wrhT")
                    mk_w = bd.emit(v, lambda e, sl=slot, eI=eI: e.tensor_scalar(
                        W[:, sl:sl + 64], FA[:, 48:112], FA[:, 16 + eI:17 + eI], None, ALU.mult),
                        inc="v")
                    bd.wait(t, "v", mk_w)
                    bd.need(t, f"pt{eI % 2}")
                    mk_tw = bd.emit(t, lambda e, sl=slot, eI=eI: e.transpose(
                        PT[eI % 2][:, 0:128], W[:, sl:sl + 128], ident_bf[:, :]), inc="t")
                    bd.guard(f"wrh{eI % 2}", "t", mk_tw)
                    bd.wait(v, "t", mk_tw)
                    half_p = 64 * (eI % 2)
                    dst = O_WRT + (eI // 2) * 1024 + sI * 128
                    mk_we = bd.emit(v, lambda e, eI=eI, hp=half_p, do=dst: e.tensor_copy(
                        W[hp:hp + 64, do:do + 128], PT[eI % 2][0:64, 0:128]), inc="v")
                    bd.guard(f"pt{eI % 2}", "v", mk_we)
            dpre_v = bd.c["v"].n

            bd.serial = serial or ('DM' in serial_phases)
            for op in range(16):
                slot = (op + 1) % 2
                poff = O_WX + 10496 * slot
                bd.need(g, f"wx{slot}")
                if op == 0:
                    bd.wait(g, "t", phaseC_t)
                    bd.need(g, "x2bf0", "x2bf1")
                if op == 1:
                    bd.wait(g, "v", dpre_v)
                mk_w1 = bd.emit(g, lambda e, op=op, po=poff: e.dma_start(
                    W[:, po:po + 8192].rearrange("p (c n) -> p c n", c=32),
                    bwt_d.rearrange("(c p) n -> p c n", p=128)[:, :, op * 256:op * 256 + 256]),
                    inc=f"wxs{slot}", inck=16)
                mk_w2 = bd.emit(g, lambda e, op=op, po=poff: e.dma_start(
                    W[:, po + 8192:po + 10240].rearrange("p (c n) -> p c n", c=8),
                    bext_d.rearrange("c p n -> p c n")[:, :, op * 256:op * 256 + 256]),
                    inc=f"wxs{slot}", inck=16)
                mk_w3 = bd.emit(g, lambda e, op=op, po=poff: e.dma_start(
                    W[0:1, po + 10240:po + 10496], bb_d[0:1, op * 256:op * 256 + 256]),
                    inc=f"wxs{slot}", inck=16)
                bd.wait(t, f"wxs{slot}", mk_w3)
                bd.wait(t, "v", dpre_v)
                for sI in range(8):
                    bank = sI % 2
                    bd.need(t, f"ps{bank}")
                    for j in range(32):
                        bd.emit(t, lambda e, bank=bank, j=j, sI=sI, po=poff: e.matmul(
                            PS[bank][:, 0:256],
                            A[:, j * 1024 + sI * 128:j * 1024 + sI * 128 + 128],
                            W[:, po + j * 256:po + j * 256 + 256], start=(j == 0), stop=False))
                    for j in range(8):
                        bd.emit(t, lambda e, bank=bank, j=j, sI=sI, po=poff: e.matmul(
                            PS[bank][:, 0:256],
                            W[:, O_WRT + j * 1024 + sI * 128:O_WRT + j * 1024 + sI * 128 + 128],
                            W[:, po + (32 + j) * 256:po + (32 + j) * 256 + 256],
                            start=False, stop=False))
                    mk_d = bd.emit(t, lambda e, bank=bank, po=poff: e.matmul(
                        PS[bank][:, 0:256], ones_r[:, :], W[0:1, po + 10240:po + 10496],
                        start=False, stop=True), inc="t")
                    bd.guard(f"wx{slot}", "t", mk_d)
                    bd.wait(v, "t", mk_d)
                    ob = 256 * (sI % 2)
                    bd.need(v, f"ob{sI % 2}")
                    mk_o = bd.emit(v, lambda e, bank=bank, ob=ob, sI=sI, op=op: e.tensor_add(
                        FA[:, ob:ob + 256], PS[bank][:, 0:256],
                        B[:, sI * 4096 + op * 256:sI * 4096 + op * 256 + 256]), inc="v")
                    bd.guard(f"ps{bank}", "v", mk_o)
                    bd.wait(x_, "v", mk_o)
                    t0 = tok0 + sI * 128
                    mk_y = bd.emit(x_, lambda e, ob=ob, t0=t0, op=op: e.dma_start(
                        y_d[t0:t0 + 128, op * 256:op * 256 + 256], FA[:, ob:ob + 256]),
                        inc=f"obs{sI % 2}", inck=16)
                    bd.guard(f"ob{sI % 2}", f"obs{sI % 2}", mk_y)
            end_t = bd.c["t"].n
            for key in ("war_lo", "xbf0", "xbf1", "wq0", "wq1", "wrhT", "bufA",
                        "kscr", "sqA"):
                bd.guard(key, "t", end_t)

        for sem_name in ("obs0", "obs1", "dmps"):
            if sem_name in bd.c:
                fin = bd.c[sem_name].n
                bd.wait(t, sem_name, fin)
                bd.wait(v, sem_name, fin)

        with nc.Block() as block:
            @block.gpsimd
            def _(e):
                for f in bd.q["g"]:
                    f(e)

            @block.sync
            def _(e):
                for f in bd.q["x"]:
                    f(e)

            @block.tensor
            def _(e):
                for f in bd.q["t"]:
                    f(e)

            @block.vector
            def _(e):
                for f in bd.q["v"]:
                    f(e)

            @block.scalar
            def _(e):
                for f in bd.q["s"]:
                    f(e)
    return nc


def _host_inputs(L0_attn_hidden_states, L0_attn_Wq, L0_attn_Wkv, L0_attn_Wo,
                 L0_ffn_base_w, L0_ffn_base_b, L0_ffn_A, L0_ffn_B, L0_ffn_router_w):
    f32 = np.float32
    wq = np.ascontiguousarray(L0_attn_Wq, dtype=f32)
    wkv = np.ascontiguousarray(L0_attn_Wkv, dtype=f32)
    wo = np.ascontiguousarray(L0_attn_Wo, dtype=f32)
    bwt = np.ascontiguousarray(np.asarray(L0_ffn_base_w, dtype=f32).T)
    bext = np.ascontiguousarray(
        np.asarray(L0_ffn_B, dtype=f32).transpose(0, 2, 1).reshape(8, 128, D))
    art = np.ascontiguousarray(np.concatenate(
        [np.asarray(L0_ffn_A, dtype=f32).T.reshape(32, 128, 64),
         np.asarray(L0_ffn_router_w, dtype=f32).T.reshape(32, 128, 16)], axis=2))
    bb = np.ascontiguousarray(np.asarray(L0_ffn_base_b, dtype=f32).reshape(1, D))
    p = np.arange(128)[:, None]
    j = np.arange(RELW)[None, :]
    u = (p - j + 511).astype(f32)
    relf = np.where(u <= 0, u, np.float32(MASKVAL)).astype(np.float16)
    ident = np.eye(128, dtype=f32)
    consts = np.zeros((128, 8), dtype=f32)
    consts[:, 0] = -SHIFT
    consts[:, 1] = -0.5
    consts[:, 3] = 1e-5
    shared = dict(wq=wq, wkv=wkv, wo=wo, bwt=bwt, bext=bext, art=art, bb=bb,
                  rel=relf, ident=ident, consts=consts)
    xs = np.ascontiguousarray(np.asarray(L0_attn_hidden_states, dtype=f32))
    return xs, shared


def kernel(**inputs):
    xs, shared = _host_inputs(**inputs)
    nc = build_nc()
    in_maps = [dict(shared, x=np.ascontiguousarray(xs[c])) for c in range(NC)]
    res = run_bass_kernel_spmd(nc, in_maps, core_ids=list(range(NC)))
    return np.stack([np.asarray(res.results[c]["y"], dtype=np.float32) for c in range(NC)],
                    axis=0)


# version 08
# speedup vs baseline: 1.1262x; 1.0108x over previous
import contextlib
import numpy as np
import concourse.bass as bass
import concourse.mybir as mybir
from concourse.bass_utils import run_bass_kernel_spmd

F32 = mybir.dt.float32
BF16 = mybir.dt.bfloat16
FP16 = mybir.dt.float16
AF = mybir.ActivationFunctionType
ALU = mybir.AluOpType
AX = mybir.AxisListType

S = 2048
D = 4096
NH = 32
NC = 8
HALF = 1024
SHIFT = 12.0
MASKVAL = -60000.0
RELW = 2560
WAR = 29184
SCALE = 1.0 / float(np.sqrt(128.0))

O_WKV = 0
O_XBF = 8192
O_SQA = 16384
O_KSC = 20480
O_WQ = 0
O_PRB = 0
O_WO = 0
O_SQ2 = 0
O_X2 = 8192
O_ART = 4096
O_WSC = 6656
O_WX = 0
O_WRT = 20992


def _alibi_slopes(n_heads=32):
    import math
    closest = 2 ** math.floor(math.log2(n_heads))
    base = 2.0 ** (-(2.0 ** (-(math.log2(closest) - 3))))
    slopes = [base ** i for i in range(1, closest + 1)]
    if closest != n_heads:
        extra = 2.0 ** (-(2.0 ** (-(math.log2(2 * closest) - 3))))
        n_extra = min(closest, n_heads - closest)
        slopes += [extra ** i for i in range(1, 2 * n_extra + 1, 2)]
    return [float(np.float32(s)) for s in slopes]

SLOPES = _alibi_slopes(NH)


class Ctr:
    def __init__(self, es, nc, name):
        self.h = es.enter_context(nc.semaphore(name))
        self.n = 0

    def bump(self, k=1):
        self.n += k
        return self.n


class Build:
    def __init__(self, nc, es, serial=False):
        self.nc = nc
        self.es = es
        self.serial = serial
        self.last = None
        self.q = {e: [] for e in ("t", "v", "s", "g", "x")}
        self.c = {}
        for e in ("t", "v", "s", "g", "x"):
            self.ctr(e)
        self.waited = {e: {} for e in self.q}
        self.guards = {}

    def ctr(self, name):
        if name not in self.c:
            self.c[name] = Ctr(self.es, self.nc, "sem_" + name)
        return self.c[name]

    def wait(self, eng, ctr_name, mark):
        if mark is None or mark <= 0 or mark <= self.waited[eng].get(ctr_name, 0):
            return
        self.waited[eng][ctr_name] = mark
        h = self.ctr(ctr_name).h
        self.q[eng].append(lambda e, h=h, m=mark: e.wait_ge(h, m))

    def emit(self, eng, fn, inc=None, inck=1):
        if self.serial:
            if self.last is not None:
                self.wait(eng, self.last[0], self.last[1])
            if inc is None:
                inc = {"t": "t", "v": "v", "s": "s", "g": "gp", "x": "gp"}[eng]
        if inc is None:
            self.q[eng].append(fn)
            return None
        ctr = self.ctr(inc)
        mark = ctr.bump(inck)
        h = ctr.h

        def run(e, fn=fn, h=h, k=inck):
            fn(e).then_inc(h, k)
        self.q[eng].append(run)
        self.last = (inc, mark)
        return mark

    def need(self, eng, *keys):
        for key in keys:
            g = self.guards.get(key)
            if g is not None:
                self.wait(eng, g[0], g[1])

    def guard(self, key, ctr_name, mark):
        old = self.guards.get(key)
        if old is None or old[0] != ctr_name or (mark is not None and mark > old[1]):
            if mark is not None:
                self.guards[key] = (ctr_name, mark)


DUMP_H = 0


def build_nc(dump=False, serial=False, serial_phases=()):
    nc = bass.Bass("TRN2", target_bir_lowering=False, debug=False)
    x_d = nc.dram_tensor("x", [S, D], F32, kind="ExternalInput").ap()
    wq_d = nc.dram_tensor("wq", [D, D], F32, kind="ExternalInput").ap()
    wkv_d = nc.dram_tensor("wkv", [D, 256], F32, kind="ExternalInput").ap()
    wo_d = nc.dram_tensor("wo", [D, D], F32, kind="ExternalInput").ap()
    bwt_d = nc.dram_tensor("bwt", [D, D], F32, kind="ExternalInput").ap()
    bext_d = nc.dram_tensor("bext", [8, 128, D], F32, kind="ExternalInput").ap()
    art_d = nc.dram_tensor("art", [32, 128, 80], F32, kind="ExternalInput").ap()
    bb_d = nc.dram_tensor("bb", [1, D], F32, kind="ExternalInput").ap()
    rel_d = nc.dram_tensor("rel", [128, RELW], FP16, kind="ExternalInput").ap()
    msk_d = nc.dram_tensor("msk", [128, 896], FP16, kind="ExternalInput").ap()
    ident_d = nc.dram_tensor("ident", [128, 128], F32, kind="ExternalInput").ap()
    consts_d = nc.dram_tensor("consts", [128, 8], F32, kind="ExternalInput").ap()
    y_d = nc.dram_tensor("y", [S, D], F32, kind="ExternalOutput").ap()
    dmp = {}
    if dump:
        for nm, shp, dt in [("d_lnT", [128, 32768], BF16), ("d_kT", [128, 2048], BF16),
                            ("d_v", [128, 16 * 132], BF16), ("d_qT", [128, 32768], BF16),
                            ("d_attnT", [128, 32768], BF16), ("d_zr", [128, 8], F32),
                            ("d_asc", [128, 128], BF16), ("d_st", [128, 8], F32),
                            ("d_att0", [128, 512], BF16), ("d_ps", [128, 256], BF16), ("d_stA", [128, 8], F32),
                            ("d_x1", [128, 32768], BF16), ("d_ln2T", [128, 32768], BF16),
                            ("d_prb", [128, 8192], BF16)]:
            dmp[nm] = nc.dram_tensor(nm, shp, dt, kind="ExternalOutput").ap()

    es = contextlib.ExitStack()
    with es:
        sbt = lambda n, s, d: es.enter_context(nc.sbuf_tensor(n, s, d))
        pst = lambda n, s, d: es.enter_context(nc.psum_tensor(n, s, d))
        A = sbt("A", [128, 32 * 1024], BF16)
        B = sbt("Bb", [128, 32 * 1024], BF16)
        kT = sbt("kT", [128, 2048], BF16)
        v_ext = sbt("v_ext", [128, 16 * 132], BF16)
        W = sbt("W", [128, WAR], BF16)
        FA = sbt("FA", [128, 1024], F32)
        rel_sb = sbt("rel_sb", [128, RELW], FP16)
        msk_sb = sbt("msk_sb", [128, 896], FP16)
        ident_bf = sbt("ident_bf", [128, 128], BF16)
        consts = sbt("consts_sb", [128, 8], F32)
        ones_r = sbt("ones_r", [1, 128], BF16)
        ones_c = sbt("ones_c", [128, 1], BF16)
        zr = sbt("zr", [128, 8], F32)
        stat = sbt("stat", [128, 8], F32)
        PS = [pst(f"PS{i}", [128, 512], F32) for i in range(5)]
        PT = [pst("PTA", [128, 1024], BF16), pst("PTB", [128, 1024], BF16)]
        PSR = pst("PSR", [128, 512], F32)
        asc = sbt("asc", [128, 256], BF16)

        bd = Build(nc, es, serial=serial)
        t, v, s_, g, x_ = "t", "v", "s", "g", "x"

        bd.emit(g, lambda e: e.dma_start(ident_bf[:, :], ident_d), inc="ident", inck=16)
        bd.emit(g, lambda e: e.memset(ones_r[:, :], 1.0), inc="gp")
        bd.emit(g, lambda e: e.memset(ones_c[:, :], 1.0), inc="gp")
        bd.emit(g, lambda e: e.memset(W[:, O_WSC:O_WSC + 256], 0.0), inc="gp")
        bd.emit(g, lambda e: e.memset(v_ext[:, :], 1.0), inc="gp")
        bd.emit(x_, lambda e: e.dma_start(rel_sb[:, :], rel_d), inc="rel", inck=16)
        bd.emit(x_, lambda e: e.dma_start(msk_sb[:, :], msk_d), inc="rel", inck=16)
        bd.emit(x_, lambda e: e.dma_start(consts[:, :], consts_d), inc="cst", inck=16)
        bd.wait(t, "ident", 16)
        bd.wait(t, "gp", 4)
        bd.wait(g, "gp", 4)
        bd.wait(v, "rel", 32)
        bd.wait(v, "cst", 16)
        bd.wait(s_, "cst", 16)

        def stats_ln(src_ap, sq_off, needkeys):
            mk0 = bd.emit(v, lambda e: e.reduce_sum(stat[:, 0:1], src_ap, axis=AX.X), inc="v")
            bd.need(s_, *needkeys)
            mk_s = bd.emit(s_, lambda e: e.activation(
                W[:, sq_off:sq_off + D], src_ap, AF.Square, accum_out=stat[:, 1:2]), inc="s")
            bd.wait(v, "v", mk0)
            bd.wait(v, "s", mk_s)
            mk = bd.emit(v, lambda e: e.tensor_scalar(stat[:, 2:3], stat[:, 0:1], 1.0 / D,
                                                      None, ALU.mult), inc="v")
            bd.wait(v, "v", mk)
            mk = bd.emit(v, lambda e: e.tensor_scalar(stat[:, 3:4], stat[:, 1:2], 1.0 / D,
                                                      None, ALU.mult), inc="v")
            bd.wait(v, "v", mk)
            mk = bd.emit(v, lambda e: e.tensor_mul(stat[:, 4:5], stat[:, 2:3], stat[:, 2:3]),
                         inc="v")
            bd.wait(v, "v", mk)
            mk_v = bd.emit(v, lambda e: e.scalar_tensor_tensor(
                stat[:, 5:6], stat[:, 4:5], -1.0, stat[:, 3:4], ALU.mult, ALU.add), inc="v")
            bd.wait(s_, "v", mk_v)
            mk_l = bd.emit(s_, lambda e: e.activation(stat[:, 6:7], stat[:, 5:6], AF.Ln,
                                                      bias=consts[:, 3:4]), inc="s")
            bd.wait(s_, "s", mk_l)
            mk_r = bd.emit(s_, lambda e: e.activation(stat[:, 7:8], stat[:, 6:7], AF.Exp,
                                                      scale=consts[:, 1:2]), inc="s")
            return mk_r

        def transpose_tile_to_A(xoff, sI):
            last_t = None
            for gI in range(8):
                pt = PT[gI % 2]
                bd.need(t, f"pt{gI % 2}")
                for j in range(4):
                    ic = gI * 4 + j
                    mk_t = bd.emit(t, lambda e, pt=pt, j=j, xo=xoff, ic=ic: e.transpose(
                        pt[:, j * 128:j * 128 + 128], W[:, xo + ic * 128:xo + ic * 128 + 128],
                        ident_bf[:, :]), inc=("t" if j == 3 else None))
                eng, enm = (v, "v") if gI % 2 == 0 else (s_, "s")
                bd.wait(eng, "t", mk_t)
                bd.need(eng, "bufA")
                if gI % 2 == 0:
                    mk_e = bd.emit(v, lambda e, pt=pt, gI=gI, sI=sI: e.tensor_copy(
                        A[:, :].rearrange("p (c n) -> p c n", n=1024)[:, gI * 4:gI * 4 + 4,
                                                                     sI * 128:sI * 128 + 128],
                        pt[:, 0:512].rearrange("p (c n) -> p c n", c=4)), inc="v")
                else:
                    mk_e = bd.emit(s_, lambda e, pt=pt, gI=gI, sI=sI: e.copy(
                        A[:, :].rearrange("p (c n) -> p c n", n=1024)[:, gI * 4:gI * 4 + 4,
                                                                     sI * 128:sI * 128 + 128],
                        pt[:, 0:512].rearrange("p (c n) -> p c n", c=4)), inc="s")
                bd.guard(f"pt{gI % 2}", enm, mk_e)
                last_t = mk_t
            return last_t, bd.c["v"].n, bd.c["s"].n

        qe_marks = {}
        for h in range(2):
            tok0 = h * HALF

            bd.serial = serial or ('A' in serial_phases)
            bd.need(g, "war_lo")
            mk_wkv = bd.emit(g, lambda e: e.dma_start(
                W[:, O_WKV:O_WKV + 8192].rearrange("p (c n) -> p c n", c=32),
                wkv_d.rearrange("(c p) n -> p c n", p=128)), inc="wkv", inck=16)
            for sI in range(8):
                t0 = tok0 + sI * 128
                gc = 8 * h + sI
                xoff = O_XBF + 4096 * (sI % 2)
                bd.need(g, f"xbf{sI % 2}")
                mk_x = bd.emit(g, lambda e, t0=t0, xo=xoff: e.dma_start(
                    W[:, xo:xo + D], x_d[t0:t0 + 128, :]), inc=f"xb{sI % 2}", inck=16)
                bd.wait(v, f"xb{sI % 2}", mk_x)
                bd.wait(s_, f"xb{sI % 2}", mk_x)
                mk_r = stats_ln(W[:, xoff:xoff + D], O_SQA, ("sqA", "wx1"))
                bd.guard("sqA", "s", bd.c["s"].n)
                bd.wait(v, "s", mk_r)
                if dump and h == DUMP_H and sI == 0:
                    mkst = bd.emit(v, lambda e: e.tensor_copy(FA[:, 896:904], stat[:, :]), inc="v")
                    bd.wait(x_, "v", mkst)
                    bd.emit(x_, lambda e: e.dma_start(dmp["d_stA"], FA[:, 896:904]),
                            inc="dmps", inck=16)
                mk_ln = bd.emit(v, lambda e, xo=xoff: e.tensor_scalar(
                    W[:, xo:xo + D], W[:, xo:xo + D], stat[:, 2:3], stat[:, 7:8],
                    ALU.subtract, ALU.mult), inc="v")
                bd.wait(t, "v", mk_ln)
                mk_t, lnT_v, lnT_s = transpose_tile_to_A(xoff, sI)
                bd.guard(f"xbf{sI % 2}", "t", mk_t)
                bd.wait(t, "v", lnT_v)
                bd.wait(t, "s", lnT_s)
                bd.wait(t, "wkv", mk_wkv)
                bd.need(t, "ps2")
                for ic in range(32):
                    mk_kv = bd.emit(t, lambda e, ic=ic, sI=sI: e.matmul(
                        PS[2][:, 0:256], A[:, ic * 1024 + sI * 128:ic * 1024 + sI * 128 + 128],
                        W[:, O_WKV + ic * 256:O_WKV + ic * 256 + 256],
                        start=(ic == 0), stop=(ic == 31)), inc=("t" if ic == 31 else None))
                bd.wait(v, "t", mk_kv)
                bd.need(v, "kscr")
                mk_k = bd.emit(v, lambda e: e.tensor_scalar(
                    W[:, O_KSC:O_KSC + 128], PS[2][:, 0:128], SCALE, None, ALU.mult), inc="v")
                bd.wait(v, "gp", 4)
                mk_vv = bd.emit(v, lambda e, gc=gc: e.tensor_copy(
                    v_ext[:, gc * 132:gc * 132 + 128], PS[2][:, 128:256]), inc="v")
                bd.guard("ps2", "v", mk_vv)
                bd.wait(t, "v", mk_k)
                bd.need(t, "pt0")
                mk_tk = bd.emit(t, lambda e: e.transpose(
                    PT[0][:, 0:128], W[:, O_KSC:O_KSC + 128], ident_bf[:, :]), inc="t")
                bd.guard("kscr", "t", mk_tk)
                bd.wait(v, "t", mk_tk)
                mk_kT = bd.emit(v, lambda e, gc=gc: e.tensor_copy(
                    kT[:, gc * 128:gc * 128 + 128], PT[0][:, 0:128]), inc="v")
                bd.guard("pt0", "v", mk_kT)
                last_kv_t = mk_tk
            bd.guard("wq0", "t", last_kv_t)
            bd.guard("wq1", "t", last_kv_t)
            phaseA_v = bd.c["v"].n
            phaseA_s = bd.c["s"].n
            if dump and h == DUMP_H:
                bd.wait(x_, "v", phaseA_v)
                bd.emit(x_, lambda e: e.dma_start(dmp["d_lnT"], A[:, :]), inc="dmps", inck=16)
                bd.emit(x_, lambda e: e.dma_start(dmp["d_kT"], kT[:, :]), inc="dmps", inck=16)
                mkd = bd.emit(x_, lambda e: e.dma_start(dmp["d_v"], v_ext[:, :]), inc="dmps", inck=16)
                bd.guard("bufA", "dmps", mkd)

            bd.serial = serial or ('Q' in serial_phases)
            for hd in range(NH):
                poff = O_WQ + 4096 * (hd % 2)
                bd.need(g, f"wq{hd % 2}")
                mk_wq = bd.emit(g, lambda e, hd=hd, po=poff: e.dma_start(
                    W[:, po:po + 4096].rearrange("p (c n) -> p c n", c=32),
                    wq_d.rearrange("(c p) n -> p c n", p=128)[:, :, hd * 128:hd * 128 + 128]),
                    inc=f"wqs{hd % 2}", inck=16)
                bd.wait(t, f"wqs{hd % 2}", mk_wq)
                bd.wait(t, "v", phaseA_v)
                for tb in range(2):
                    psq = PS[tb % 2]
                    bd.need(t, f"ps{tb % 2}")
                    for ic in range(32):
                        mk_q = bd.emit(t, lambda e, psq=psq, po=poff, ic=ic, tb=tb: e.matmul(
                            psq[:, :], W[:, po + ic * 128:po + ic * 128 + 128],
                            A[:, ic * 1024 + tb * 512:ic * 1024 + tb * 512 + 512],
                            start=(ic == 0), stop=(ic == 31)), inc=("t" if ic == 31 else None))
                    bd.guard(f"wq{hd % 2}", "t", mk_q)
                    bd.wait(v, "t", mk_q)
                    mk_qe = bd.emit(v, lambda e, psq=psq, hd=hd, tb=tb: e.tensor_copy(
                        B[:, hd * 1024 + tb * 512:hd * 1024 + tb * 512 + 512], psq[:, :]), inc="v")
                    bd.guard(f"ps{tb % 2}", "v", mk_qe)
                    qe_marks[(h, hd, tb)] = mk_qe
            phaseQ_t = bd.c["t"].n
            if dump and h == DUMP_H:
                bd.wait(x_, "v", bd.c["v"].n)
                mkd = bd.emit(x_, lambda e: e.dma_start(dmp["d_qT"], B[:, :]), inc="dmps", inck=16)
                bd.guard("bufB", "dmps", mkd)

            bd.serial = serial or ('ATTN' in serial_phases)
            bd.guard("probsT0", "t", phaseQ_t)
            bd.guard("probsT1", "t", phaseQ_t)
            bd.need(v, "ob0", "ob1")
            scidx = 0
            pvidx = 0
            faidx = 0
            bcidx = 0

            def emit_scores(hd, tb):
                nonlocal scidx, faidx, bcidx
                slope = SLOPES[hd]
                t0b = tok0 + tb * 512
                K_v = (t0b + 512) // 128
                seqp = (hd * 2 + tb) % 2
                pT_off = O_PRB + 8192 * seqp
                fast = hd >= 12
                n_full = (t0b // 128) if fast else 0
                bd.need(s_, f"probsT{seqp}")
                bd.wait(t, "v", qe_marks[(h, hd, tb)])
                exp_last = None
                for kc in range(K_v):
                    bank = (0, 1, 4)[scidx % 3]
                    scidx += 1
                    bd.need(t, f"ps{bank}")
                    mk_sc = bd.emit(t, lambda e, bank=bank, kc=kc, hd=hd, tb=tb: e.matmul(
                        PS[bank][:, :], kT[:, kc * 128:kc * 128 + 128],
                        B[:, hd * 1024 + tb * 512:hd * 1024 + tb * 512 + 512],
                        start=True, stop=True), inc="t")
                    if fast:
                        bc = 4 + (bcidx % 2)
                        bcidx += 1
                        bd.need(v, f"bcol{bc}")
                        off = float(slope) * (kc * 128 - t0b - 511) - SHIFT
                        mk_b = bd.emit(v, lambda e, bc=bc, sl=slope, off=off: e.tensor_scalar(
                            stat[:, bc:bc + 1], consts[:, 4:5], sl, off, ALU.mult, ALU.add),
                            inc="v")
                        if kc < n_full:
                            bd.wait(s_, "v", mk_b)
                            bd.wait(s_, "t", mk_sc)
                            mk_exp = bd.emit(s_, lambda e, bank=bank, po=pT_off, kc=kc, bc=bc:
                                             e.activation(
                                                 W[:, po + kc * 512:po + kc * 512 + 512],
                                                 PS[bank][:, :], AF.Exp, bias=stat[:, bc:bc + 1]),
                                             inc="s")
                            bd.guard(f"ps{bank}", "s", mk_exp)
                        else:
                            slot = faidx % 2
                            faidx += 1
                            scf = 512 * slot
                            shiftm = 384 - kc * 128 + t0b
                            bd.wait(v, "t", mk_sc)
                            bd.need(v, f"sc{slot}")
                            mk_stt = bd.emit(v, lambda e, bank=bank, scf=scf, sh=shiftm:
                                             e.scalar_tensor_tensor(
                                                 FA[:, scf:scf + 512], msk_sb[:, sh:sh + 512],
                                                 1.0, PS[bank][:, :], ALU.mult, ALU.add),
                                             inc="v")
                            bd.guard(f"ps{bank}", "v", mk_stt)
                            bd.wait(s_, "v", mk_stt)
                            mk_exp = bd.emit(s_, lambda e, scf=scf, po=pT_off, kc=kc, bc=bc:
                                             e.activation(
                                                 W[:, po + kc * 512:po + kc * 512 + 512],
                                                 FA[:, scf:scf + 512], AF.Exp,
                                                 bias=stat[:, bc:bc + 1]), inc="s")
                            bd.guard(f"sc{slot}", "s", mk_exp)
                        bd.guard(f"bcol{bc}", "s", mk_exp)
                    else:
                        slot = faidx % 2
                        faidx += 1
                        scf = 512 * slot
                        bd.wait(v, "t", mk_sc)
                        bd.need(v, f"sc{slot}")
                        shift2 = 511 - kc * 128 + t0b
                        mk_stt = bd.emit(v, lambda e, bank=bank, scf=scf, sh=shift2, sl=slope:
                                         e.scalar_tensor_tensor(
                                             FA[:, scf:scf + 512], rel_sb[:, sh:sh + 512], sl,
                                             PS[bank][:, :], ALU.mult, ALU.add), inc="v")
                        bd.guard(f"ps{bank}", "v", mk_stt)
                        bd.wait(s_, "v", mk_stt)
                        mk_exp = bd.emit(s_, lambda e, scf=scf, po=pT_off, kc=kc: e.activation(
                            W[:, po + kc * 512:po + kc * 512 + 512], FA[:, scf:scf + 512],
                            AF.Exp, bias=consts[:, 0:1]), inc="s")
                        bd.guard(f"sc{slot}", "s", mk_exp)
                    exp_last = mk_exp
                return exp_last

            def emit_pv(hd, tb, exp_last):
                nonlocal pvidx
                t0b = tok0 + tb * 512
                K_v = (t0b + 512) // 128
                seqp = (hd * 2 + tb) % 2
                pT_off = O_PRB + 8192 * seqp
                for s4 in range(4):
                    pvb = 2 + (pvidx % 2)
                    pvidx += 1
                    bd.wait(t, "s", exp_last)
                    bd.need(t, f"ps{pvb}")
                    for kc in range(K_v):
                        mk_pv = bd.emit(t, lambda e, pvb=pvb, kc=kc, s4=s4, po=pT_off: e.matmul(
                            PS[pvb][:, 0:130],
                            W[:, po + kc * 512 + s4 * 128:po + kc * 512 + s4 * 128 + 128],
                            v_ext[:, kc * 132:kc * 132 + 130],
                            start=(kc == 0), stop=(kc == K_v - 1)),
                            inc=("t" if kc == K_v - 1 else None))
                    bd.guard(f"probsT{seqp}", "t", mk_pv)
                    bd.wait(v, "t", mk_pv)
                    mk_rc = bd.emit(v, lambda e, pvb=pvb, s4=s4: e.reciprocal(
                        stat[:, s4:s4 + 1], PS[pvb][:, 128:129]), inc="v")
                    bd.wait(v, "v", mk_rc)
                    bd.need(v, f"asc{s4 % 2}")
                    mk_sc2 = bd.emit(v, lambda e, pvb=pvb, s4=s4: e.tensor_scalar(
                        asc[:, (s4 % 2) * 128:(s4 % 2) * 128 + 128], PS[pvb][:, 0:128],
                        stat[:, s4:s4 + 1], None, ALU.mult), inc="v")
                    bd.guard(f"ps{pvb}", "v", mk_sc2)
                    bd.wait(t, "v", mk_sc2)
                    bd.need(t, f"pt{s4 % 2}")
                    mk_ta = bd.emit(t, lambda e, s4=s4: e.transpose(
                        PT[s4 % 2][:, 0:128], asc[:, (s4 % 2) * 128:(s4 % 2) * 128 + 128],
                        ident_bf[:, :]), inc="t")
                    bd.guard(f"asc{s4 % 2}", "t", mk_ta)
                    bd.wait(v, "t", mk_ta)
                    bd.need(v, "bufA")
                    mk_ae = bd.emit(v, lambda e, s4=s4, hd=hd, tb=tb: e.tensor_copy(
                        A[:, hd * 1024 + tb * 512 + s4 * 128:hd * 1024 + tb * 512 + s4 * 128 + 128],
                        PT[s4 % 2][:, 0:128]), inc="v")
                    bd.guard(f"pt{s4 % 2}", "v", mk_ae)

            prev = None
            for hd in range(NH):
                for tb in range(2):
                    el = emit_scores(hd, tb)
                    if prev is not None:
                        emit_pv(prev[0], prev[1], prev[2])
                    prev = (hd, tb, el)
            emit_pv(prev[0], prev[1], prev[2])
            attn_v = bd.c["v"].n
            if dump and h == DUMP_H:
                bd.wait(x_, "v", attn_v)
                bd.emit(x_, lambda e: e.dma_start(dmp["d_attnT"], A[:, :]), inc="dmps", inck=16)
                mkd = bd.emit(x_, lambda e: e.dma_start(dmp["d_zr"], zr[:, :]), inc="dmps", inck=16)
                bd.guard("bufA", "dmps", mkd)

            bd.serial = serial or ('C' in serial_phases)
            for op in range(16):
                poff = O_WO + 8192 * (op % 2)
                bd.need(g, f"wo{op % 2}", f"probsT{op % 2}")
                mk_wo = bd.emit(g, lambda e, op=op, po=poff: e.dma_start(
                    W[:, po:po + 8192].rearrange("p (c n) -> p c n", c=32),
                    wo_d.rearrange("(c p) n -> p c n", p=128)[:, :, op * 256:op * 256 + 256]),
                    inc=f"wos{op % 2}", inck=16)
                bd.wait(t, f"wos{op % 2}", mk_wo)
                bd.wait(t, "v", attn_v)
                for sI in range(8):
                    bank = sI % 2
                    bd.need(t, f"ps{bank}")
                    for dc in range(32):
                        mk_c = bd.emit(t, lambda e, bank=bank, dc=dc, sI=sI, po=poff: e.matmul(
                            PS[bank][:, 0:256],
                            A[:, dc * 1024 + sI * 128:dc * 1024 + sI * 128 + 128],
                            W[:, po + dc * 256:po + dc * 256 + 256],
                            start=(dc == 0), stop=(dc == 31)), inc=("t" if dc == 31 else None))
                    bd.guard(f"wo{op % 2}", "t", mk_c)
                    bd.wait(s_, "t", mk_c)
                    bd.need(s_, "bufB")
                    mk_cn = bd.emit(s_, lambda e, bank=bank, sI=sI, op=op: e.activation(
                        B[:, sI * 4096 + op * 256:sI * 4096 + op * 256 + 256],
                        PS[bank][:, 0:256], AF.Copy), inc="s")
                    bd.guard(f"ps{bank}", "s", mk_cn)
            phaseC_t = bd.c["t"].n
            phaseC_s = bd.c["s"].n

            bd.serial = serial or ('LN2' in serial_phases) or ('DP' in serial_phases)
            bd.need(g, "art", "wo0")
            mk_art = bd.emit(g, lambda e: e.dma_start(
                W[:, O_ART:O_ART + 2560].rearrange("p (c n) -> p c n", c=32),
                art_d.rearrange("c p n -> p c n")), inc="arts", inck=16)
            for sI in range(8):
                t0 = tok0 + sI * 128
                xoff = O_X2 + 4096 * (sI % 2)
                bd.need(g, f"x2bf{sI % 2}", "wo1")
                mk_x = bd.emit(g, lambda e, t0=t0, xo=xoff: e.dma_start(
                    W[:, xo:xo + D], x_d[t0:t0 + 128, :]), inc=f"x2b{sI % 2}", inck=16)
                bd.wait(v, f"x2b{sI % 2}", mk_x)
                bd.wait(v, "s", phaseC_s)
                mk_x1 = bd.emit(v, lambda e, sI=sI, xo=xoff: e.tensor_add(
                    B[:, sI * 4096:sI * 4096 + D], B[:, sI * 4096:sI * 4096 + D],
                    W[:, xo:xo + D]), inc="v")
                bd.wait(s_, "v", mk_x1)
                mk_r = stats_ln(B[:, sI * 4096:sI * 4096 + D], O_SQ2, ("sq2", "wo0"))
                bd.guard("sq2", "s", bd.c["s"].n)
                bd.wait(v, "s", mk_r)
                mk_ln = bd.emit(v, lambda e, sI=sI, xo=xoff: e.tensor_scalar(
                    W[:, xo:xo + D], B[:, sI * 4096:sI * 4096 + D], stat[:, 2:3], stat[:, 7:8],
                    ALU.subtract, ALU.mult), inc="v")
                bd.wait(t, "v", mk_ln)
                mk_t, _lv, _ls = transpose_tile_to_A(xoff, sI)
                bd.guard(f"x2bf{sI % 2}", "t", mk_t)
                ln2_done_mk = bd.c["v"].n
                ln2_done_s = bd.c["s"].n
                bd.wait(t, "arts", mk_art)
                bd.wait(t, "v", ln2_done_mk)
                bd.wait(t, "s", ln2_done_s)
                bd.need(t, "psr")
                for ic in range(32):
                    mk_rh = bd.emit(t, lambda e, ic=ic, sI=sI: e.matmul(
                        PSR[:, 0:80], A[:, ic * 1024 + sI * 128:ic * 1024 + sI * 128 + 128],
                        W[:, O_ART + ic * 80:O_ART + ic * 80 + 80],
                        start=(ic == 0), stop=(ic == 31)), inc=("t" if ic == 31 else None))
                bd.guard("art", "t", mk_rh)
                bd.wait(v, "t", mk_rh)
                bd.emit(v, lambda e: e.tensor_copy(FA[:, 48:112], PSR[:, 0:64]))
                mk_nm = bd.emit(v, lambda e: e.reduce_max(stat[:, 6:7], PSR[:, 64:80],
                                                          axis=AX.X, negate=True), inc="v")
                bd.wait(s_, "v", mk_nm)
                mk_el = bd.emit(s_, lambda e: e.activation(FA[:, 0:16], PSR[:, 64:80],
                                                           AF.Exp, bias=stat[:, 6:7]), inc="s")
                bd.guard("psr", "s", mk_el)
                bd.wait(v, "s", mk_el)
                tiny = []
                tiny.append(lambda e: e.reduce_sum(stat[:, 3:4], FA[:, 0:16], axis=AX.X))
                tiny.append(lambda e: e.reduce_max(stat[:, 0:1], FA[:, 0:16], axis=AX.X))
                tiny.append(lambda e: e.tensor_scalar(FA[:, 32:48], FA[:, 0:16], stat[:, 0:1],
                                                      None, ALU.is_lt))
                tiny.append(lambda e: e.tensor_mul(FA[:, 16:32], FA[:, 0:16], FA[:, 32:48]))
                tiny.append(lambda e: e.reduce_max(stat[:, 2:3], FA[:, 16:32], axis=AX.X))
                tiny.append(lambda e: e.tensor_scalar(stat[:, 3:4], stat[:, 3:4], 1e-6, None,
                                                      ALU.mult))
                tiny.append(lambda e: e.tensor_add(stat[:, 3:4], stat[:, 3:4], stat[:, 0:1]))
                tiny.append(lambda e: e.tensor_add(stat[:, 3:4], stat[:, 3:4], stat[:, 2:3]))
                tiny.append(lambda e: e.reciprocal(stat[:, 4:5], stat[:, 3:4]))
                tiny.append(lambda e: e.tensor_scalar(FA[:, 32:48], FA[:, 0:16], stat[:, 2:3],
                                                      None, ALU.is_ge))
                tiny.append(lambda e: e.tensor_scalar(FA[:, 16:32], FA[:, 0:16], stat[:, 4:5],
                                                      0.5, ALU.mult, ALU.mult))
                tiny.append(lambda e: e.tensor_mul(FA[:, 16:32], FA[:, 16:32], FA[:, 32:48]))
                for fn in tiny:
                    mk_cmb = bd.emit(v, fn, inc="v")
                    bd.wait(v, "v", mk_cmb)
                for eI in range(16):
                    slot = O_WSC + 128 * (eI % 2)
                    bd.need(v, f"wrh{eI % 2}", "wrhT")
                    mk_w = bd.emit(v, lambda e, sl=slot, eI=eI: e.tensor_scalar(
                        W[:, sl:sl + 64], FA[:, 48:112], FA[:, 16 + eI:17 + eI], None, ALU.mult),
                        inc="v")
                    bd.wait(t, "v", mk_w)
                    bd.need(t, f"pt{eI % 2}")
                    mk_tw = bd.emit(t, lambda e, sl=slot, eI=eI: e.transpose(
                        PT[eI % 2][:, 0:128], W[:, sl:sl + 128], ident_bf[:, :]), inc="t")
                    bd.guard(f"wrh{eI % 2}", "t", mk_tw)
                    bd.wait(v, "t", mk_tw)
                    half_p = 64 * (eI % 2)
                    dst = O_WRT + (eI // 2) * 1024 + sI * 128
                    mk_we = bd.emit(v, lambda e, eI=eI, hp=half_p, do=dst: e.tensor_copy(
                        W[hp:hp + 64, do:do + 128], PT[eI % 2][0:64, 0:128]), inc="v")
                    bd.guard(f"pt{eI % 2}", "v", mk_we)
            dpre_v = bd.c["v"].n

            bd.serial = serial or ('DM' in serial_phases)
            for op in range(16):
                slot = (op + 1) % 2
                poff = O_WX + 10496 * slot
                bd.need(g, f"wx{slot}")
                if op == 0:
                    bd.wait(g, "t", phaseC_t)
                    bd.need(g, "x2bf0", "x2bf1")
                if op == 1:
                    bd.wait(g, "v", dpre_v)
                mk_w1 = bd.emit(g, lambda e, op=op, po=poff: e.dma_start(
                    W[:, po:po + 8192].rearrange("p (c n) -> p c n", c=32),
                    bwt_d.rearrange("(c p) n -> p c n", p=128)[:, :, op * 256:op * 256 + 256]),
                    inc=f"wxs{slot}", inck=16)
                mk_w2 = bd.emit(g, lambda e, op=op, po=poff: e.dma_start(
                    W[:, po + 8192:po + 10240].rearrange("p (c n) -> p c n", c=8),
                    bext_d.rearrange("c p n -> p c n")[:, :, op * 256:op * 256 + 256]),
                    inc=f"wxs{slot}", inck=16)
                mk_w3 = bd.emit(g, lambda e, op=op, po=poff: e.dma_start(
                    W[0:1, po + 10240:po + 10496], bb_d[0:1, op * 256:op * 256 + 256]),
                    inc=f"wxs{slot}", inck=16)
                bd.wait(t, f"wxs{slot}", mk_w3)
                bd.wait(t, "v", dpre_v)
                for sI in range(8):
                    bank = sI % 2
                    bd.need(t, f"ps{bank}")
                    for j in range(32):
                        bd.emit(t, lambda e, bank=bank, j=j, sI=sI, po=poff: e.matmul(
                            PS[bank][:, 0:256],
                            A[:, j * 1024 + sI * 128:j * 1024 + sI * 128 + 128],
                            W[:, po + j * 256:po + j * 256 + 256], start=(j == 0), stop=False))
                    for j in range(8):
                        bd.emit(t, lambda e, bank=bank, j=j, sI=sI, po=poff: e.matmul(
                            PS[bank][:, 0:256],
                            W[:, O_WRT + j * 1024 + sI * 128:O_WRT + j * 1024 + sI * 128 + 128],
                            W[:, po + (32 + j) * 256:po + (32 + j) * 256 + 256],
                            start=False, stop=False))
                    mk_d = bd.emit(t, lambda e, bank=bank, po=poff: e.matmul(
                        PS[bank][:, 0:256], ones_r[:, :], W[0:1, po + 10240:po + 10496],
                        start=False, stop=True), inc="t")
                    bd.guard(f"wx{slot}", "t", mk_d)
                    bd.wait(v, "t", mk_d)
                    ob = 256 * (sI % 2)
                    bd.need(v, f"ob{sI % 2}")
                    mk_o = bd.emit(v, lambda e, bank=bank, ob=ob, sI=sI, op=op: e.tensor_add(
                        FA[:, ob:ob + 256], PS[bank][:, 0:256],
                        B[:, sI * 4096 + op * 256:sI * 4096 + op * 256 + 256]), inc="v")
                    bd.guard(f"ps{bank}", "v", mk_o)
                    bd.wait(x_, "v", mk_o)
                    t0 = tok0 + sI * 128
                    mk_y = bd.emit(x_, lambda e, ob=ob, t0=t0, op=op: e.dma_start(
                        y_d[t0:t0 + 128, op * 256:op * 256 + 256], FA[:, ob:ob + 256]),
                        inc=f"obs{sI % 2}", inck=16)
                    bd.guard(f"ob{sI % 2}", f"obs{sI % 2}", mk_y)
            end_t = bd.c["t"].n
            for key in ("war_lo", "xbf0", "xbf1", "wq0", "wq1", "wrhT", "bufA",
                        "kscr", "sqA"):
                bd.guard(key, "t", end_t)

        for sem_name in ("obs0", "obs1", "dmps"):
            if sem_name in bd.c:
                fin = bd.c[sem_name].n
                bd.wait(t, sem_name, fin)
                bd.wait(v, sem_name, fin)

        with nc.Block() as block:
            @block.gpsimd
            def _(e):
                for f in bd.q["g"]:
                    f(e)

            @block.sync
            def _(e):
                for f in bd.q["x"]:
                    f(e)

            @block.tensor
            def _(e):
                for f in bd.q["t"]:
                    f(e)

            @block.vector
            def _(e):
                for f in bd.q["v"]:
                    f(e)

            @block.scalar
            def _(e):
                for f in bd.q["s"]:
                    f(e)
    return nc


def _host_inputs(L0_attn_hidden_states, L0_attn_Wq, L0_attn_Wkv, L0_attn_Wo,
                 L0_ffn_base_w, L0_ffn_base_b, L0_ffn_A, L0_ffn_B, L0_ffn_router_w):
    f32 = np.float32
    wq = np.ascontiguousarray(L0_attn_Wq, dtype=f32)
    wkv = np.ascontiguousarray(L0_attn_Wkv, dtype=f32)
    wo = np.ascontiguousarray(L0_attn_Wo, dtype=f32)
    bwt = np.ascontiguousarray(np.asarray(L0_ffn_base_w, dtype=f32).T)
    bext = np.ascontiguousarray(
        np.asarray(L0_ffn_B, dtype=f32).transpose(0, 2, 1).reshape(8, 128, D))
    art = np.ascontiguousarray(np.concatenate(
        [np.asarray(L0_ffn_A, dtype=f32).T.reshape(32, 128, 64),
         np.asarray(L0_ffn_router_w, dtype=f32).T.reshape(32, 128, 16)], axis=2))
    bb = np.ascontiguousarray(np.asarray(L0_ffn_base_b, dtype=f32).reshape(1, D))
    p = np.arange(128)[:, None]
    j = np.arange(RELW)[None, :]
    u = (p - j + 511).astype(f32)
    relf = np.where(u <= 0, u, np.float32(MASKVAL)).astype(np.float16)
    jm = np.arange(896)[None, :]
    um = (p - jm + 384).astype(f32)
    mskf = np.where(um <= 0, np.float32(0.0), np.float32(MASKVAL)).astype(np.float16)
    ident = np.eye(128, dtype=f32)
    consts = np.zeros((128, 8), dtype=f32)
    consts[:, 0] = -SHIFT
    consts[:, 1] = -0.5
    consts[:, 3] = 1e-5
    consts[:, 4] = np.arange(128, dtype=f32)
    shared = dict(wq=wq, wkv=wkv, wo=wo, bwt=bwt, bext=bext, art=art, bb=bb,
                  rel=relf, msk=mskf, ident=ident, consts=consts)
    xs = np.ascontiguousarray(np.asarray(L0_attn_hidden_states, dtype=f32))
    return xs, shared


def kernel(**inputs):
    xs, shared = _host_inputs(**inputs)
    nc = build_nc()
    in_maps = [dict(shared, x=np.ascontiguousarray(xs[c])) for c in range(NC)]
    res = run_bass_kernel_spmd(nc, in_maps, core_ids=list(range(NC)))
    return np.stack([np.asarray(res.results[c]["y"], dtype=np.float32) for c in range(NC)],
                    axis=0)


# version 10
# speedup vs baseline: 1.2543x; 1.1137x over previous
import contextlib
import numpy as np
import concourse.bass as bass
import concourse.mybir as mybir
from concourse.bass_utils import run_bass_kernel_spmd

F32 = mybir.dt.float32
BF16 = mybir.dt.bfloat16
FP16 = mybir.dt.float16
AF = mybir.ActivationFunctionType
ALU = mybir.AluOpType
AX = mybir.AxisListType

S = 2048
D = 4096
NH = 32
NC = 8
HALF = 1024
SHIFT = 12.0
MASKVAL = -60000.0
RELW = 2560
WAR = 29184
SCALE = 1.0 / float(np.sqrt(128.0))

O_WKV = 0
O_XBF = 8192
O_SQA = 16384
O_KSC = 20480
O_WQ = 0
O_PRB = 0
O_WO = 0
O_SQ2 = 0
O_X2 = 8192
O_ART = 4096
O_WSC = 6656
O_WX = 0
O_WRT = 20992


def _alibi_slopes(n_heads=32):
    import math
    closest = 2 ** math.floor(math.log2(n_heads))
    base = 2.0 ** (-(2.0 ** (-(math.log2(closest) - 3))))
    slopes = [base ** i for i in range(1, closest + 1)]
    if closest != n_heads:
        extra = 2.0 ** (-(2.0 ** (-(math.log2(2 * closest) - 3))))
        n_extra = min(closest, n_heads - closest)
        slopes += [extra ** i for i in range(1, 2 * n_extra + 1, 2)]
    return [float(np.float32(s)) for s in slopes]

SLOPES = _alibi_slopes(NH)


class Ctr:
    def __init__(self, es, nc, name):
        self.h = es.enter_context(nc.semaphore(name))
        self.n = 0

    def bump(self, k=1):
        self.n += k
        return self.n


class Build:
    def __init__(self, nc, es, serial=False):
        self.nc = nc
        self.es = es
        self.serial = serial
        self.last = None
        self.q = {e: [] for e in ("t", "v", "s", "g", "x")}
        self.c = {}
        for e in ("t", "v", "s", "g", "x"):
            self.ctr(e)
        self.waited = {e: {} for e in self.q}
        self.guards = {}

    def ctr(self, name):
        if name not in self.c:
            self.c[name] = Ctr(self.es, self.nc, "sem_" + name)
        return self.c[name]

    def wait(self, eng, ctr_name, mark):
        if mark is None or mark <= 0 or mark <= self.waited[eng].get(ctr_name, 0):
            return
        self.waited[eng][ctr_name] = mark
        h = self.ctr(ctr_name).h
        self.q[eng].append(lambda e, h=h, m=mark: e.wait_ge(h, m))

    def emit(self, eng, fn, inc=None, inck=1):
        if self.serial:
            if self.last is not None:
                self.wait(eng, self.last[0], self.last[1])
            if inc is None:
                inc = {"t": "t", "v": "v", "s": "s", "g": "gp", "x": "gp"}[eng]
        if inc is None:
            self.q[eng].append(fn)
            return None
        ctr = self.ctr(inc)
        mark = ctr.bump(inck)
        h = ctr.h

        def run(e, fn=fn, h=h, k=inck):
            fn(e).then_inc(h, k)
        self.q[eng].append(run)
        self.last = (inc, mark)
        return mark

    def need(self, eng, *keys):
        for key in keys:
            g = self.guards.get(key)
            if g is not None:
                self.wait(eng, g[0], g[1])

    def guard(self, key, ctr_name, mark):
        old = self.guards.get(key)
        if old is None or old[0] != ctr_name or (mark is not None and mark > old[1]):
            if mark is not None:
                self.guards[key] = (ctr_name, mark)


DUMP_H = 0


def build_nc(dump=False, serial=False, serial_phases=()):
    nc = bass.Bass("TRN2", target_bir_lowering=False, debug=False)
    x_d = nc.dram_tensor("x", [S, D], F32, kind="ExternalInput").ap()
    wq_d = nc.dram_tensor("wq", [D, D], F32, kind="ExternalInput").ap()
    wkv_d = nc.dram_tensor("wkv", [D, 256], F32, kind="ExternalInput").ap()
    wo_d = nc.dram_tensor("wo", [D, D], F32, kind="ExternalInput").ap()
    bwt_d = nc.dram_tensor("bwt", [D, D], F32, kind="ExternalInput").ap()
    bext_d = nc.dram_tensor("bext", [8, 128, D], F32, kind="ExternalInput").ap()
    art_d = nc.dram_tensor("art", [32, 128, 80], F32, kind="ExternalInput").ap()
    bb_d = nc.dram_tensor("bb", [1, D], F32, kind="ExternalInput").ap()
    rel_d = nc.dram_tensor("rel", [128, RELW], FP16, kind="ExternalInput").ap()
    msk_d = nc.dram_tensor("msk", [128, 896], FP16, kind="ExternalInput").ap()
    ident_d = nc.dram_tensor("ident", [128, 128], F32, kind="ExternalInput").ap()
    consts_d = nc.dram_tensor("consts", [128, 8], F32, kind="ExternalInput").ap()
    y_d = nc.dram_tensor("y", [S, D], F32, kind="ExternalOutput").ap()
    dmp = {}
    if dump:
        for nm, shp, dt in [("d_lnT", [128, 32768], BF16), ("d_kT", [128, 2048], BF16),
                            ("d_v", [128, 16 * 132], BF16), ("d_qT", [128, 32768], BF16),
                            ("d_attnT", [128, 32768], BF16), ("d_zr", [128, 8], F32),
                            ("d_asc", [128, 128], BF16), ("d_st", [128, 8], F32),
                            ("d_att0", [128, 512], BF16), ("d_ps", [128, 256], BF16), ("d_stA", [128, 8], F32),
                            ("d_x1", [128, 32768], BF16), ("d_ln2T", [128, 32768], BF16),
                            ("d_prb", [128, 8192], BF16)]:
            dmp[nm] = nc.dram_tensor(nm, shp, dt, kind="ExternalOutput").ap()

    es = contextlib.ExitStack()
    with es:
        sbt = lambda n, s, d: es.enter_context(nc.sbuf_tensor(n, s, d))
        pst = lambda n, s, d: es.enter_context(nc.psum_tensor(n, s, d))
        A = sbt("A", [128, 32 * 1024], BF16)
        B = sbt("Bb", [128, 32 * 1024], BF16)
        kT = sbt("kT", [128, 2048], BF16)
        v_ext = sbt("v_ext", [128, 16 * 132], BF16)
        W = sbt("W", [128, WAR], BF16)
        FA = sbt("FA", [128, 1024], F32)
        rel_sb = sbt("rel_sb", [128, RELW], FP16)
        msk_sb = sbt("msk_sb", [128, 896], FP16)
        ident_bf = sbt("ident_bf", [128, 128], BF16)
        consts = sbt("consts_sb", [128, 8], F32)
        ones_r = sbt("ones_r", [1, 128], BF16)
        ones_c = sbt("ones_c", [128, 1], BF16)
        zr = sbt("zr", [128, 8], F32)
        stat = sbt("stat", [128, 8], F32)
        PS = [pst(f"PS{i}", [128, 512], F32) for i in range(5)]
        PT = [pst("PTA", [128, 1024], BF16), pst("PTB", [128, 1024], BF16)]
        PSR = pst("PSR", [128, 512], F32)
        asc = sbt("asc", [128, 256], BF16)

        bd = Build(nc, es, serial=serial)
        t, v, s_, g, x_ = "t", "v", "s", "g", "x"

        bd.emit(g, lambda e: e.dma_start(ident_bf[:, :], ident_d), inc="ident", inck=16)
        bd.emit(g, lambda e: e.memset(ones_r[:, :], 1.0), inc="gp")
        bd.emit(g, lambda e: e.memset(ones_c[:, :], 1.0), inc="gp")
        bd.emit(g, lambda e: e.memset(W[:, O_WSC:O_WSC + 256], 0.0), inc="gp")
        bd.emit(g, lambda e: e.memset(v_ext[:, :], 1.0), inc="gp")
        bd.emit(x_, lambda e: e.dma_start(rel_sb[:, :], rel_d), inc="rel", inck=16)
        bd.emit(x_, lambda e: e.dma_start(msk_sb[:, :], msk_d), inc="rel", inck=16)
        bd.emit(x_, lambda e: e.dma_start(consts[:, :], consts_d), inc="cst", inck=16)
        bd.wait(t, "ident", 16)
        bd.wait(t, "gp", 4)
        bd.wait(g, "gp", 4)
        bd.wait(v, "rel", 32)
        bd.wait(v, "cst", 16)
        bd.wait(s_, "cst", 16)

        def stats_ln(src_ap, sq_off, needkeys):
            mk0 = bd.emit(v, lambda e: e.reduce_sum(stat[:, 0:1], src_ap, axis=AX.X), inc="v")
            bd.need(s_, *needkeys)
            mk_s = bd.emit(s_, lambda e: e.activation(
                W[:, sq_off:sq_off + D], src_ap, AF.Square, accum_out=stat[:, 1:2]), inc="s")
            bd.wait(v, "v", mk0)
            bd.wait(v, "s", mk_s)
            mk = bd.emit(v, lambda e: e.tensor_scalar(stat[:, 2:3], stat[:, 0:1], 1.0 / D,
                                                      None, ALU.mult), inc="v")
            bd.wait(v, "v", mk)
            mk = bd.emit(v, lambda e: e.tensor_scalar(stat[:, 3:4], stat[:, 1:2], 1.0 / D,
                                                      None, ALU.mult), inc="v")
            bd.wait(v, "v", mk)
            mk = bd.emit(v, lambda e: e.tensor_mul(stat[:, 4:5], stat[:, 2:3], stat[:, 2:3]),
                         inc="v")
            bd.wait(v, "v", mk)
            mk_v = bd.emit(v, lambda e: e.scalar_tensor_tensor(
                stat[:, 5:6], stat[:, 4:5], -1.0, stat[:, 3:4], ALU.mult, ALU.add), inc="v")
            bd.wait(s_, "v", mk_v)
            mk_l = bd.emit(s_, lambda e: e.activation(stat[:, 6:7], stat[:, 5:6], AF.Ln,
                                                      bias=consts[:, 3:4]), inc="s")
            bd.wait(s_, "s", mk_l)
            mk_r = bd.emit(s_, lambda e: e.activation(stat[:, 7:8], stat[:, 6:7], AF.Exp,
                                                      scale=consts[:, 1:2]), inc="s")
            return mk_r

        def transpose_tile_to_A(xoff, sI):
            last_t = None
            for gI in range(8):
                pt = PT[gI % 2]
                bd.need(t, f"pt{gI % 2}")
                for j in range(4):
                    ic = gI * 4 + j
                    mk_t = bd.emit(t, lambda e, pt=pt, j=j, xo=xoff, ic=ic: e.transpose(
                        pt[:, j * 128:j * 128 + 128], W[:, xo + ic * 128:xo + ic * 128 + 128],
                        ident_bf[:, :]), inc=("t" if j == 3 else None))
                eng, enm = (v, "v") if gI % 2 == 0 else (s_, "s")
                bd.wait(eng, "t", mk_t)
                bd.need(eng, "bufA")
                if gI % 2 == 0:
                    mk_e = bd.emit(v, lambda e, pt=pt, gI=gI, sI=sI: e.tensor_copy(
                        A[:, :].rearrange("p (c n) -> p c n", n=1024)[:, gI * 4:gI * 4 + 4,
                                                                     sI * 128:sI * 128 + 128],
                        pt[:, 0:512].rearrange("p (c n) -> p c n", c=4)), inc="v")
                else:
                    mk_e = bd.emit(s_, lambda e, pt=pt, gI=gI, sI=sI: e.copy(
                        A[:, :].rearrange("p (c n) -> p c n", n=1024)[:, gI * 4:gI * 4 + 4,
                                                                     sI * 128:sI * 128 + 128],
                        pt[:, 0:512].rearrange("p (c n) -> p c n", c=4)), inc="s")
                bd.guard(f"pt{gI % 2}", enm, mk_e)
                last_t = mk_t
            return last_t, bd.c["v"].n, bd.c["s"].n

        qe_marks = {}
        for h in range(2):
            tok0 = h * HALF

            bd.serial = serial or ('A' in serial_phases)
            bd.need(g, "war_lo")
            mk_wkv = bd.emit(g, lambda e: e.dma_start(
                W[:, O_WKV:O_WKV + 8192].rearrange("p (c n) -> p c n", c=32),
                wkv_d.rearrange("(c p) n -> p c n", p=128)), inc="wkv", inck=16)
            for sI in range(8):
                t0 = tok0 + sI * 128
                gc = 8 * h + sI
                xoff = O_XBF + 4096 * (sI % 2)
                bd.need(g, f"xbf{sI % 2}")
                mk_x = bd.emit(g, lambda e, t0=t0, xo=xoff: e.dma_start(
                    W[:, xo:xo + D], x_d[t0:t0 + 128, :]), inc=f"xb{sI % 2}", inck=16)
                bd.wait(v, f"xb{sI % 2}", mk_x)
                bd.wait(s_, f"xb{sI % 2}", mk_x)
                mk_r = stats_ln(W[:, xoff:xoff + D], O_SQA, ("sqA", "wx1"))
                bd.guard("sqA", "s", bd.c["s"].n)
                bd.wait(v, "s", mk_r)
                if dump and h == DUMP_H and sI == 0:
                    mkst = bd.emit(v, lambda e: e.tensor_copy(FA[:, 896:904], stat[:, :]), inc="v")
                    bd.wait(x_, "v", mkst)
                    bd.emit(x_, lambda e: e.dma_start(dmp["d_stA"], FA[:, 896:904]),
                            inc="dmps", inck=16)
                mk_ln = bd.emit(v, lambda e, xo=xoff: e.tensor_scalar(
                    W[:, xo:xo + D], W[:, xo:xo + D], stat[:, 2:3], stat[:, 7:8],
                    ALU.subtract, ALU.mult), inc="v")
                bd.wait(t, "v", mk_ln)
                mk_t, lnT_v, lnT_s = transpose_tile_to_A(xoff, sI)
                bd.guard(f"xbf{sI % 2}", "t", mk_t)
                bd.wait(t, "v", lnT_v)
                bd.wait(t, "s", lnT_s)
                bd.wait(t, "wkv", mk_wkv)
                bd.need(t, "ps2")
                for ic in range(32):
                    mk_kv = bd.emit(t, lambda e, ic=ic, sI=sI: e.matmul(
                        PS[2][:, 0:256], A[:, ic * 1024 + sI * 128:ic * 1024 + sI * 128 + 128],
                        W[:, O_WKV + ic * 256:O_WKV + ic * 256 + 256],
                        start=(ic == 0), stop=(ic == 31)), inc=("t" if ic == 31 else None))
                bd.wait(v, "t", mk_kv)
                bd.need(v, "kscr")
                mk_k = bd.emit(v, lambda e: e.tensor_scalar(
                    W[:, O_KSC:O_KSC + 128], PS[2][:, 0:128], SCALE, None, ALU.mult), inc="v")
                bd.wait(v, "gp", 4)
                mk_vv = bd.emit(v, lambda e, gc=gc: e.tensor_copy(
                    v_ext[:, gc * 132:gc * 132 + 128], PS[2][:, 128:256]), inc="v")
                bd.guard("ps2", "v", mk_vv)
                bd.wait(t, "v", mk_k)
                bd.need(t, "pt0")
                mk_tk = bd.emit(t, lambda e: e.transpose(
                    PT[0][:, 0:128], W[:, O_KSC:O_KSC + 128], ident_bf[:, :]), inc="t")
                bd.guard("kscr", "t", mk_tk)
                bd.wait(v, "t", mk_tk)
                mk_kT = bd.emit(v, lambda e, gc=gc: e.tensor_copy(
                    kT[:, gc * 128:gc * 128 + 128], PT[0][:, 0:128]), inc="v")
                bd.guard("pt0", "v", mk_kT)
                last_kv_t = mk_tk
            bd.guard("wq0", "t", last_kv_t)
            bd.guard("wq1", "t", last_kv_t)
            phaseA_v = bd.c["v"].n
            phaseA_s = bd.c["s"].n
            if dump and h == DUMP_H:
                bd.wait(x_, "v", phaseA_v)
                bd.emit(x_, lambda e: e.dma_start(dmp["d_lnT"], A[:, :]), inc="dmps", inck=16)
                bd.emit(x_, lambda e: e.dma_start(dmp["d_kT"], kT[:, :]), inc="dmps", inck=16)
                mkd = bd.emit(x_, lambda e: e.dma_start(dmp["d_v"], v_ext[:, :]), inc="dmps", inck=16)
                bd.guard("bufA", "dmps", mkd)

            bd.serial = serial or ('Q' in serial_phases)
            for hd in range(NH):
                poff = O_WQ + 4096 * (hd % 2)
                bd.need(g, f"wq{hd % 2}")
                mk_wq = bd.emit(g, lambda e, hd=hd, po=poff: e.dma_start(
                    W[:, po:po + 4096].rearrange("p (c n) -> p c n", c=32),
                    wq_d.rearrange("(c p) n -> p c n", p=128)[:, :, hd * 128:hd * 128 + 128]),
                    inc=f"wqs{hd % 2}", inck=16)
                bd.wait(t, f"wqs{hd % 2}", mk_wq)
                bd.wait(t, "v", phaseA_v)
                for tb in range(2):
                    psq = PS[tb % 2]
                    bd.need(t, f"ps{tb % 2}")
                    for ic in range(32):
                        mk_q = bd.emit(t, lambda e, psq=psq, po=poff, ic=ic, tb=tb: e.matmul(
                            psq[:, :], W[:, po + ic * 128:po + ic * 128 + 128],
                            A[:, ic * 1024 + tb * 512:ic * 1024 + tb * 512 + 512],
                            start=(ic == 0), stop=(ic == 31)), inc=("t" if ic == 31 else None))
                    bd.guard(f"wq{hd % 2}", "t", mk_q)
                    bd.wait(v, "t", mk_q)
                    mk_qe = bd.emit(v, lambda e, psq=psq, hd=hd, tb=tb: e.tensor_copy(
                        B[:, hd * 1024 + tb * 512:hd * 1024 + tb * 512 + 512], psq[:, :]), inc="v")
                    bd.guard(f"ps{tb % 2}", "v", mk_qe)
                    qe_marks[(h, hd, tb)] = mk_qe
            phaseQ_t = bd.c["t"].n
            if dump and h == DUMP_H:
                bd.wait(x_, "v", bd.c["v"].n)
                mkd = bd.emit(x_, lambda e: e.dma_start(dmp["d_qT"], B[:, :]), inc="dmps", inck=16)
                bd.guard("bufB", "dmps", mkd)

            bd.serial = serial or ('ATTN' in serial_phases)
            bd.guard("probsT0", "t", phaseQ_t)
            bd.guard("probsT1", "t", phaseQ_t)
            bd.need(v, "ob0", "ob1")
            scidx = 0
            pvidx = 0
            faidx = 0
            bcidx = 0

            def emit_scores(hd, tb):
                nonlocal scidx, faidx, bcidx
                slope = SLOPES[hd]
                t0b = tok0 + tb * 512
                K_v = (t0b + 512) // 128
                seqp = (hd * 2 + tb) % 2
                pT_off = O_PRB + 8192 * seqp
                fast = hd >= 12
                n_full = (t0b // 128) if fast else 0
                bd.need(s_, f"probsT{seqp}")
                bd.wait(t, "v", qe_marks[(h, hd, tb)])
                exp_last = None
                for kc in range(K_v):
                    bank = (0, 1, 4)[scidx % 3]
                    scidx += 1
                    bd.need(t, f"ps{bank}")
                    mk_sc = bd.emit(t, lambda e, bank=bank, kc=kc, hd=hd, tb=tb: e.matmul(
                        PS[bank][:, :], kT[:, kc * 128:kc * 128 + 128],
                        B[:, hd * 1024 + tb * 512:hd * 1024 + tb * 512 + 512],
                        start=True, stop=True), inc="t")
                    if fast:
                        bc = 4 + (bcidx % 2)
                        bcidx += 1
                        bd.need(v, f"bcol{bc}")
                        off = float(slope) * (kc * 128 - t0b - 511) - SHIFT
                        mk_b = bd.emit(v, lambda e, bc=bc, sl=slope, off=off: e.tensor_scalar(
                            stat[:, bc:bc + 1], consts[:, 4:5], sl, off, ALU.mult, ALU.add),
                            inc="v")
                        if kc < n_full:
                            bd.wait(s_, "v", mk_b)
                            bd.wait(s_, "t", mk_sc)
                            mk_exp = bd.emit(s_, lambda e, bank=bank, po=pT_off, kc=kc, bc=bc:
                                             e.activation(
                                                 W[:, po + kc * 512:po + kc * 512 + 512],
                                                 PS[bank][:, :], AF.Exp, bias=stat[:, bc:bc + 1]),
                                             inc="s")
                            bd.guard(f"ps{bank}", "s", mk_exp)
                        else:
                            slot = faidx % 2
                            faidx += 1
                            scf = 512 * slot
                            shiftm = 384 - kc * 128 + t0b
                            bd.wait(v, "t", mk_sc)
                            bd.need(v, f"sc{slot}")
                            mk_stt = bd.emit(v, lambda e, bank=bank, scf=scf, sh=shiftm:
                                             e.scalar_tensor_tensor(
                                                 FA[:, scf:scf + 512], msk_sb[:, sh:sh + 512],
                                                 1.0, PS[bank][:, :], ALU.mult, ALU.add),
                                             inc="v")
                            bd.guard(f"ps{bank}", "v", mk_stt)
                            bd.wait(s_, "v", mk_stt)
                            mk_exp = bd.emit(s_, lambda e, scf=scf, po=pT_off, kc=kc, bc=bc:
                                             e.activation(
                                                 W[:, po + kc * 512:po + kc * 512 + 512],
                                                 FA[:, scf:scf + 512], AF.Exp,
                                                 bias=stat[:, bc:bc + 1]), inc="s")
                            bd.guard(f"sc{slot}", "s", mk_exp)
                        bd.guard(f"bcol{bc}", "s", mk_exp)
                    else:
                        slot = faidx % 2
                        faidx += 1
                        scf = 512 * slot
                        bd.wait(v, "t", mk_sc)
                        bd.need(v, f"sc{slot}")
                        shift2 = 511 - kc * 128 + t0b
                        mk_stt = bd.emit(v, lambda e, bank=bank, scf=scf, sh=shift2, sl=slope:
                                         e.scalar_tensor_tensor(
                                             FA[:, scf:scf + 512], rel_sb[:, sh:sh + 512], sl,
                                             PS[bank][:, :], ALU.mult, ALU.add), inc="v")
                        bd.guard(f"ps{bank}", "v", mk_stt)
                        bd.wait(s_, "v", mk_stt)
                        mk_exp = bd.emit(s_, lambda e, scf=scf, po=pT_off, kc=kc: e.activation(
                            W[:, po + kc * 512:po + kc * 512 + 512], FA[:, scf:scf + 512],
                            AF.Exp, bias=consts[:, 0:1]), inc="s")
                        bd.guard(f"sc{slot}", "s", mk_exp)
                    exp_last = mk_exp
                return exp_last

            def emit_pv(hd, tb, exp_last):
                nonlocal pvidx
                t0b = tok0 + tb * 512
                K_v = (t0b + 512) // 128
                seqp = (hd * 2 + tb) % 2
                pT_off = O_PRB + 8192 * seqp
                pend = []

                def flush_one():
                    s4, mk_sc2 = pend.pop(0)
                    bd.wait(t, "v", mk_sc2)
                    bd.need(t, f"pt{s4 % 2}")
                    mk_ta = bd.emit(t, lambda e, s4=s4: e.transpose(
                        PT[s4 % 2][:, 0:128], asc[:, (s4 % 2) * 128:(s4 % 2) * 128 + 128],
                        ident_bf[:, :]), inc="t")
                    bd.guard(f"asc{s4 % 2}", "t", mk_ta)
                    bd.wait(v, "t", mk_ta)
                    bd.need(v, "bufA")
                    mk_ae = bd.emit(v, lambda e, s4=s4, hd=hd, tb=tb: e.tensor_copy(
                        A[:, hd * 1024 + tb * 512 + s4 * 128:hd * 1024 + tb * 512 + s4 * 128 + 128],
                        PT[s4 % 2][:, 0:128]), inc="v")
                    bd.guard(f"pt{s4 % 2}", "v", mk_ae)

                for s4 in range(4):
                    pvb = 2 + (pvidx % 2)
                    pvidx += 1
                    bd.wait(t, "s", exp_last)
                    bd.need(t, f"ps{pvb}")
                    for kc in range(K_v):
                        mk_pv = bd.emit(t, lambda e, pvb=pvb, kc=kc, s4=s4, po=pT_off: e.matmul(
                            PS[pvb][:, 0:130],
                            W[:, po + kc * 512 + s4 * 128:po + kc * 512 + s4 * 128 + 128],
                            v_ext[:, kc * 132:kc * 132 + 130],
                            start=(kc == 0), stop=(kc == K_v - 1)),
                            inc=("t" if kc == K_v - 1 else None))
                    bd.guard(f"probsT{seqp}", "t", mk_pv)
                    bd.wait(v, "t", mk_pv)
                    mk_rc = bd.emit(v, lambda e, pvb=pvb, s4=s4: e.reciprocal(
                        stat[:, s4:s4 + 1], PS[pvb][:, 128:129]), inc="v")
                    bd.wait(v, "v", mk_rc)
                    bd.need(v, f"asc{s4 % 2}")
                    mk_sc2 = bd.emit(v, lambda e, pvb=pvb, s4=s4: e.tensor_scalar(
                        asc[:, (s4 % 2) * 128:(s4 % 2) * 128 + 128], PS[pvb][:, 0:128],
                        stat[:, s4:s4 + 1], None, ALU.mult), inc="v")
                    bd.guard(f"ps{pvb}", "v", mk_sc2)
                    pend.append((s4, mk_sc2))
                    if len(pend) > 1:
                        flush_one()
                while pend:
                    flush_one()

            prev = None
            for hd in range(NH):
                for tb in range(2):
                    el = emit_scores(hd, tb)
                    if prev is not None:
                        emit_pv(prev[0], prev[1], prev[2])
                    prev = (hd, tb, el)
            emit_pv(prev[0], prev[1], prev[2])
            attn_v = bd.c["v"].n
            if dump and h == DUMP_H:
                bd.wait(x_, "v", attn_v)
                bd.emit(x_, lambda e: e.dma_start(dmp["d_attnT"], A[:, :]), inc="dmps", inck=16)
                mkd = bd.emit(x_, lambda e: e.dma_start(dmp["d_zr"], zr[:, :]), inc="dmps", inck=16)
                bd.guard("bufA", "dmps", mkd)

            bd.serial = serial or ('C' in serial_phases)
            for op in range(16):
                poff = O_WO + 8192 * (op % 2)
                bd.need(g, f"wo{op % 2}", f"probsT{op % 2}")
                mk_wo = bd.emit(g, lambda e, op=op, po=poff: e.dma_start(
                    W[:, po:po + 8192].rearrange("p (c n) -> p c n", c=32),
                    wo_d.rearrange("(c p) n -> p c n", p=128)[:, :, op * 256:op * 256 + 256]),
                    inc=f"wos{op % 2}", inck=16)
                bd.wait(t, f"wos{op % 2}", mk_wo)
                bd.wait(t, "v", attn_v)
                for sI in range(8):
                    bank = sI % 2
                    bd.need(t, f"ps{bank}")
                    for dc in range(32):
                        mk_c = bd.emit(t, lambda e, bank=bank, dc=dc, sI=sI, po=poff: e.matmul(
                            PS[bank][:, 0:256],
                            A[:, dc * 1024 + sI * 128:dc * 1024 + sI * 128 + 128],
                            W[:, po + dc * 256:po + dc * 256 + 256],
                            start=(dc == 0), stop=(dc == 31)), inc=("t" if dc == 31 else None))
                    bd.guard(f"wo{op % 2}", "t", mk_c)
                    bd.wait(s_, "t", mk_c)
                    bd.need(s_, "bufB")
                    mk_cn = bd.emit(s_, lambda e, bank=bank, sI=sI, op=op: e.activation(
                        B[:, sI * 4096 + op * 256:sI * 4096 + op * 256 + 256],
                        PS[bank][:, 0:256], AF.Copy), inc="s")
                    bd.guard(f"ps{bank}", "s", mk_cn)
            phaseC_t = bd.c["t"].n
            phaseC_s = bd.c["s"].n

            bd.serial = serial or ('LN2' in serial_phases) or ('DP' in serial_phases)
            bd.need(g, "art", "wo0")
            mk_art = bd.emit(g, lambda e: e.dma_start(
                W[:, O_ART:O_ART + 2560].rearrange("p (c n) -> p c n", c=32),
                art_d.rearrange("c p n -> p c n")), inc="arts", inck=16)
            for sI in range(8):
                t0 = tok0 + sI * 128
                xoff = O_X2 + 4096 * (sI % 2)
                bd.need(g, f"x2bf{sI % 2}", "wo1")
                mk_x = bd.emit(g, lambda e, t0=t0, xo=xoff: e.dma_start(
                    W[:, xo:xo + D], x_d[t0:t0 + 128, :]), inc=f"x2b{sI % 2}", inck=16)
                bd.wait(v, f"x2b{sI % 2}", mk_x)
                bd.wait(v, "s", phaseC_s)
                mk_x1 = bd.emit(v, lambda e, sI=sI, xo=xoff: e.tensor_add(
                    B[:, sI * 4096:sI * 4096 + D], B[:, sI * 4096:sI * 4096 + D],
                    W[:, xo:xo + D]), inc="v")
                bd.wait(s_, "v", mk_x1)
                mk_r = stats_ln(B[:, sI * 4096:sI * 4096 + D], O_SQ2, ("sq2", "wo0"))
                bd.guard("sq2", "s", bd.c["s"].n)
                bd.wait(v, "s", mk_r)
                mk_ln = bd.emit(v, lambda e, sI=sI, xo=xoff: e.tensor_scalar(
                    W[:, xo:xo + D], B[:, sI * 4096:sI * 4096 + D], stat[:, 2:3], stat[:, 7:8],
                    ALU.subtract, ALU.mult), inc="v")
                bd.wait(t, "v", mk_ln)
                mk_t, _lv, _ls = transpose_tile_to_A(xoff, sI)
                bd.guard(f"x2bf{sI % 2}", "t", mk_t)
                ln2_done_mk = bd.c["v"].n
                ln2_done_s = bd.c["s"].n
                bd.wait(t, "arts", mk_art)
                bd.wait(t, "v", ln2_done_mk)
                bd.wait(t, "s", ln2_done_s)
                bd.need(t, "psr")
                for ic in range(32):
                    mk_rh = bd.emit(t, lambda e, ic=ic, sI=sI: e.matmul(
                        PSR[:, 0:80], A[:, ic * 1024 + sI * 128:ic * 1024 + sI * 128 + 128],
                        W[:, O_ART + ic * 80:O_ART + ic * 80 + 80],
                        start=(ic == 0), stop=(ic == 31)), inc=("t" if ic == 31 else None))
                bd.guard("art", "t", mk_rh)
                bd.wait(v, "t", mk_rh)
                bd.emit(v, lambda e: e.tensor_copy(FA[:, 48:112], PSR[:, 0:64]))
                mk_nm = bd.emit(v, lambda e: e.reduce_max(stat[:, 6:7], PSR[:, 64:80],
                                                          axis=AX.X, negate=True), inc="v")
                bd.wait(s_, "v", mk_nm)
                mk_el = bd.emit(s_, lambda e: e.activation(FA[:, 0:16], PSR[:, 64:80],
                                                           AF.Exp, bias=stat[:, 6:7]), inc="s")
                bd.guard("psr", "s", mk_el)
                bd.wait(v, "s", mk_el)
                tiny = []
                tiny.append(lambda e: e.reduce_sum(stat[:, 3:4], FA[:, 0:16], axis=AX.X))
                tiny.append(lambda e: e.reduce_max(stat[:, 0:1], FA[:, 0:16], axis=AX.X))
                tiny.append(lambda e: e.tensor_scalar(FA[:, 32:48], FA[:, 0:16], stat[:, 0:1],
                                                      None, ALU.is_lt))
                tiny.append(lambda e: e.tensor_mul(FA[:, 16:32], FA[:, 0:16], FA[:, 32:48]))
                tiny.append(lambda e: e.reduce_max(stat[:, 2:3], FA[:, 16:32], axis=AX.X))
                tiny.append(lambda e: e.tensor_scalar(stat[:, 3:4], stat[:, 3:4], 1e-6, None,
                                                      ALU.mult))
                tiny.append(lambda e: e.tensor_add(stat[:, 3:4], stat[:, 3:4], stat[:, 0:1]))
                tiny.append(lambda e: e.tensor_add(stat[:, 3:4], stat[:, 3:4], stat[:, 2:3]))
                tiny.append(lambda e: e.reciprocal(stat[:, 4:5], stat[:, 3:4]))
                tiny.append(lambda e: e.tensor_scalar(FA[:, 32:48], FA[:, 0:16], stat[:, 2:3],
                                                      None, ALU.is_ge))
                tiny.append(lambda e: e.tensor_scalar(FA[:, 16:32], FA[:, 0:16], stat[:, 4:5],
                                                      0.5, ALU.mult, ALU.mult))
                tiny.append(lambda e: e.tensor_mul(FA[:, 16:32], FA[:, 16:32], FA[:, 32:48]))
                for fn in tiny:
                    mk_cmb = bd.emit(v, fn, inc="v")
                    bd.wait(v, "v", mk_cmb)
                wpend = []

                def wrh_flush():
                    eI, mk_w = wpend.pop(0)
                    slot = O_WSC + 128 * (eI % 2)
                    bd.wait(t, "v", mk_w)
                    bd.need(t, f"pt{eI % 2}")
                    mk_tw = bd.emit(t, lambda e, sl=slot, eI=eI: e.transpose(
                        PT[eI % 2][:, 0:128], W[:, sl:sl + 128], ident_bf[:, :]), inc="t")
                    bd.guard(f"wrh{eI % 2}", "t", mk_tw)
                    bd.wait(v, "t", mk_tw)
                    half_p = 64 * (eI % 2)
                    dst = O_WRT + (eI // 2) * 1024 + sI * 128
                    mk_we = bd.emit(v, lambda e, eI=eI, hp=half_p, do=dst: e.tensor_copy(
                        W[hp:hp + 64, do:do + 128], PT[eI % 2][0:64, 0:128]), inc="v")
                    bd.guard(f"pt{eI % 2}", "v", mk_we)

                for eI in range(16):
                    slot = O_WSC + 128 * (eI % 2)
                    bd.need(v, f"wrh{eI % 2}", "wrhT")
                    mk_w = bd.emit(v, lambda e, sl=slot, eI=eI: e.tensor_scalar(
                        W[:, sl:sl + 64], FA[:, 48:112], FA[:, 16 + eI:17 + eI], None, ALU.mult),
                        inc="v")
                    wpend.append((eI, mk_w))
                    if len(wpend) > 1:
                        wrh_flush()
                while wpend:
                    wrh_flush()
            dpre_v = bd.c["v"].n

            bd.serial = serial or ('DM' in serial_phases)
            for op in range(16):
                slot = (op + 1) % 2
                poff = O_WX + 10496 * slot
                bd.need(g, f"wx{slot}")
                if op == 0:
                    bd.wait(g, "t", phaseC_t)
                    bd.need(g, "x2bf0", "x2bf1")
                if op == 1:
                    bd.wait(g, "v", dpre_v)
                mk_w1 = bd.emit(g, lambda e, op=op, po=poff: e.dma_start(
                    W[:, po:po + 8192].rearrange("p (c n) -> p c n", c=32),
                    bwt_d.rearrange("(c p) n -> p c n", p=128)[:, :, op * 256:op * 256 + 256]),
                    inc=f"wxs{slot}", inck=16)
                mk_w2 = bd.emit(g, lambda e, op=op, po=poff: e.dma_start(
                    W[:, po + 8192:po + 10240].rearrange("p (c n) -> p c n", c=8),
                    bext_d.rearrange("c p n -> p c n")[:, :, op * 256:op * 256 + 256]),
                    inc=f"wxs{slot}", inck=16)
                mk_w3 = bd.emit(g, lambda e, op=op, po=poff: e.dma_start(
                    W[0:1, po + 10240:po + 10496], bb_d[0:1, op * 256:op * 256 + 256]),
                    inc=f"wxs{slot}", inck=16)
                bd.wait(t, f"wxs{slot}", mk_w3)
                bd.wait(t, "v", dpre_v)
                for sI in range(8):
                    bank = sI % 2
                    bd.need(t, f"ps{bank}")
                    for j in range(32):
                        bd.emit(t, lambda e, bank=bank, j=j, sI=sI, po=poff: e.matmul(
                            PS[bank][:, 0:256],
                            A[:, j * 1024 + sI * 128:j * 1024 + sI * 128 + 128],
                            W[:, po + j * 256:po + j * 256 + 256], start=(j == 0), stop=False))
                    for j in range(8):
                        bd.emit(t, lambda e, bank=bank, j=j, sI=sI, po=poff: e.matmul(
                            PS[bank][:, 0:256],
                            W[:, O_WRT + j * 1024 + sI * 128:O_WRT + j * 1024 + sI * 128 + 128],
                            W[:, po + (32 + j) * 256:po + (32 + j) * 256 + 256],
                            start=False, stop=False))
                    mk_d = bd.emit(t, lambda e, bank=bank, po=poff: e.matmul(
                        PS[bank][:, 0:256], ones_r[:, :], W[0:1, po + 10240:po + 10496],
                        start=False, stop=True), inc="t")
                    bd.guard(f"wx{slot}", "t", mk_d)
                    bd.wait(v, "t", mk_d)
                    ob = 256 * (sI % 2)
                    bd.need(v, f"ob{sI % 2}")
                    mk_o = bd.emit(v, lambda e, bank=bank, ob=ob, sI=sI, op=op: e.tensor_add(
                        FA[:, ob:ob + 256], PS[bank][:, 0:256],
                        B[:, sI * 4096 + op * 256:sI * 4096 + op * 256 + 256]), inc="v")
                    bd.guard(f"ps{bank}", "v", mk_o)
                    bd.wait(x_, "v", mk_o)
                    t0 = tok0 + sI * 128
                    mk_y = bd.emit(x_, lambda e, ob=ob, t0=t0, op=op: e.dma_start(
                        y_d[t0:t0 + 128, op * 256:op * 256 + 256], FA[:, ob:ob + 256]),
                        inc=f"obs{sI % 2}", inck=16)
                    bd.guard(f"ob{sI % 2}", f"obs{sI % 2}", mk_y)
            end_t = bd.c["t"].n
            for key in ("war_lo", "xbf0", "xbf1", "wq0", "wq1", "wrhT", "bufA",
                        "kscr", "sqA"):
                bd.guard(key, "t", end_t)

        for sem_name in ("obs0", "obs1", "dmps"):
            if sem_name in bd.c:
                fin = bd.c[sem_name].n
                bd.wait(t, sem_name, fin)
                bd.wait(v, sem_name, fin)

        with nc.Block() as block:
            @block.gpsimd
            def _(e):
                for f in bd.q["g"]:
                    f(e)

            @block.sync
            def _(e):
                for f in bd.q["x"]:
                    f(e)

            @block.tensor
            def _(e):
                for f in bd.q["t"]:
                    f(e)

            @block.vector
            def _(e):
                for f in bd.q["v"]:
                    f(e)

            @block.scalar
            def _(e):
                for f in bd.q["s"]:
                    f(e)
    return nc


def _host_inputs(L0_attn_hidden_states, L0_attn_Wq, L0_attn_Wkv, L0_attn_Wo,
                 L0_ffn_base_w, L0_ffn_base_b, L0_ffn_A, L0_ffn_B, L0_ffn_router_w):
    f32 = np.float32
    wq = np.ascontiguousarray(L0_attn_Wq, dtype=f32)
    wkv = np.ascontiguousarray(L0_attn_Wkv, dtype=f32)
    wo = np.ascontiguousarray(L0_attn_Wo, dtype=f32)
    bwt = np.ascontiguousarray(np.asarray(L0_ffn_base_w, dtype=f32).T)
    bext = np.ascontiguousarray(
        np.asarray(L0_ffn_B, dtype=f32).transpose(0, 2, 1).reshape(8, 128, D))
    art = np.ascontiguousarray(np.concatenate(
        [np.asarray(L0_ffn_A, dtype=f32).T.reshape(32, 128, 64),
         np.asarray(L0_ffn_router_w, dtype=f32).T.reshape(32, 128, 16)], axis=2))
    bb = np.ascontiguousarray(np.asarray(L0_ffn_base_b, dtype=f32).reshape(1, D))
    p = np.arange(128)[:, None]
    j = np.arange(RELW)[None, :]
    u = (p - j + 511).astype(f32)
    relf = np.where(u <= 0, u, np.float32(MASKVAL)).astype(np.float16)
    jm = np.arange(896)[None, :]
    um = (p - jm + 384).astype(f32)
    mskf = np.where(um <= 0, np.float32(0.0), np.float32(MASKVAL)).astype(np.float16)
    ident = np.eye(128, dtype=f32)
    consts = np.zeros((128, 8), dtype=f32)
    consts[:, 0] = -SHIFT
    consts[:, 1] = -0.5
    consts[:, 3] = 1e-5
    consts[:, 4] = np.arange(128, dtype=f32)
    shared = dict(wq=wq, wkv=wkv, wo=wo, bwt=bwt, bext=bext, art=art, bb=bb,
                  rel=relf, msk=mskf, ident=ident, consts=consts)
    xs = np.ascontiguousarray(np.asarray(L0_attn_hidden_states, dtype=f32))
    return xs, shared


def kernel(**inputs):
    xs, shared = _host_inputs(**inputs)
    nc = build_nc()
    in_maps = [dict(shared, x=np.ascontiguousarray(xs[c])) for c in range(NC)]
    res = run_bass_kernel_spmd(nc, in_maps, core_ids=list(range(NC)))
    return np.stack([np.asarray(res.results[c]["y"], dtype=np.float32) for c in range(NC)],
                    axis=0)


# version 11
# speedup vs baseline: 1.3025x; 1.0384x over previous
import contextlib
import numpy as np
import concourse.bass as bass
import concourse.mybir as mybir
from concourse.bass_utils import run_bass_kernel_spmd

F32 = mybir.dt.float32
BF16 = mybir.dt.bfloat16
FP16 = mybir.dt.float16
AF = mybir.ActivationFunctionType
ALU = mybir.AluOpType
AX = mybir.AxisListType

S = 2048
D = 4096
NH = 32
NC = 8
HALF = 1024
SHIFT = 12.0
MASKVAL = -60000.0
RELW = 2560
WAR = 29184
SCALE = 1.0 / float(np.sqrt(128.0))

O_WKV = 0
O_XBF = 8192
O_SQA = 16384
O_KSC = 20480
O_WQ = 0
O_PRB = 0
O_WO = 0
O_SQ2 = 0
O_X2 = 8192
O_ART = 4096
O_WSC = 6656
O_WX = 0
O_WRT = 20992


def _alibi_slopes(n_heads=32):
    import math
    closest = 2 ** math.floor(math.log2(n_heads))
    base = 2.0 ** (-(2.0 ** (-(math.log2(closest) - 3))))
    slopes = [base ** i for i in range(1, closest + 1)]
    if closest != n_heads:
        extra = 2.0 ** (-(2.0 ** (-(math.log2(2 * closest) - 3))))
        n_extra = min(closest, n_heads - closest)
        slopes += [extra ** i for i in range(1, 2 * n_extra + 1, 2)]
    return [float(np.float32(s)) for s in slopes]

SLOPES = _alibi_slopes(NH)


class Ctr:
    def __init__(self, es, nc, name):
        self.h = es.enter_context(nc.semaphore(name))
        self.n = 0

    def bump(self, k=1):
        self.n += k
        return self.n


class Build:
    def __init__(self, nc, es, serial=False):
        self.nc = nc
        self.es = es
        self.serial = serial
        self.last = None
        self.q = {e: [] for e in ("t", "v", "s", "g", "x")}
        self.c = {}
        for e in ("t", "v", "s", "g", "x"):
            self.ctr(e)
        self.waited = {e: {} for e in self.q}
        self.guards = {}

    def ctr(self, name):
        if name not in self.c:
            self.c[name] = Ctr(self.es, self.nc, "sem_" + name)
        return self.c[name]

    def wait(self, eng, ctr_name, mark):
        if mark is None or mark <= 0 or mark <= self.waited[eng].get(ctr_name, 0):
            return
        self.waited[eng][ctr_name] = mark
        h = self.ctr(ctr_name).h
        self.q[eng].append(lambda e, h=h, m=mark: e.wait_ge(h, m))

    def emit(self, eng, fn, inc=None, inck=1):
        if self.serial:
            if self.last is not None:
                self.wait(eng, self.last[0], self.last[1])
            if inc is None:
                inc = {"t": "t", "v": "v", "s": "s", "g": "gp", "x": "gp"}[eng]
        if inc is None:
            self.q[eng].append(fn)
            return None
        ctr = self.ctr(inc)
        mark = ctr.bump(inck)
        h = ctr.h

        def run(e, fn=fn, h=h, k=inck):
            fn(e).then_inc(h, k)
        self.q[eng].append(run)
        self.last = (inc, mark)
        return mark

    def need(self, eng, *keys):
        for key in keys:
            g = self.guards.get(key)
            if g is not None:
                self.wait(eng, g[0], g[1])

    def guard(self, key, ctr_name, mark):
        old = self.guards.get(key)
        if old is None or old[0] != ctr_name or (mark is not None and mark > old[1]):
            if mark is not None:
                self.guards[key] = (ctr_name, mark)


DUMP_H = 0


def build_nc(dump=False, serial=False, serial_phases=()):
    nc = bass.Bass("TRN2", target_bir_lowering=False, debug=False)
    x_d = nc.dram_tensor("x", [S, D], F32, kind="ExternalInput").ap()
    wq_d = nc.dram_tensor("wq", [D, D], F32, kind="ExternalInput").ap()
    wkv_d = nc.dram_tensor("wkv", [D, 256], F32, kind="ExternalInput").ap()
    wo_d = nc.dram_tensor("wo", [D, D], F32, kind="ExternalInput").ap()
    bwt_d = nc.dram_tensor("bwt", [D, D], F32, kind="ExternalInput").ap()
    bext_d = nc.dram_tensor("bext", [8, 128, D], F32, kind="ExternalInput").ap()
    art_d = nc.dram_tensor("art", [32, 128, 80], F32, kind="ExternalInput").ap()
    bb_d = nc.dram_tensor("bb", [1, D], F32, kind="ExternalInput").ap()
    rel_d = nc.dram_tensor("rel", [128, RELW], FP16, kind="ExternalInput").ap()
    msk_d = nc.dram_tensor("msk", [128, 896], FP16, kind="ExternalInput").ap()
    ident_d = nc.dram_tensor("ident", [128, 128], F32, kind="ExternalInput").ap()
    consts_d = nc.dram_tensor("consts", [128, 8], F32, kind="ExternalInput").ap()
    y_d = nc.dram_tensor("y", [S, D], F32, kind="ExternalOutput").ap()
    dmp = {}
    if dump:
        for nm, shp, dt in [("d_lnT", [128, 32768], BF16), ("d_kT", [128, 2048], BF16),
                            ("d_v", [128, 16 * 132], BF16), ("d_qT", [128, 32768], BF16),
                            ("d_attnT", [128, 32768], BF16), ("d_zr", [128, 8], F32),
                            ("d_asc", [128, 128], BF16), ("d_st", [128, 8], F32),
                            ("d_att0", [128, 512], BF16), ("d_ps", [128, 256], BF16), ("d_stA", [128, 8], F32),
                            ("d_x1", [128, 32768], BF16), ("d_ln2T", [128, 32768], BF16),
                            ("d_prb", [128, 8192], BF16)]:
            dmp[nm] = nc.dram_tensor(nm, shp, dt, kind="ExternalOutput").ap()

    es = contextlib.ExitStack()
    with es:
        sbt = lambda n, s, d: es.enter_context(nc.sbuf_tensor(n, s, d))
        pst = lambda n, s, d: es.enter_context(nc.psum_tensor(n, s, d))
        A = sbt("A", [128, 32 * 1024], BF16)
        B = sbt("Bb", [128, 32 * 1024], BF16)
        kT = sbt("kT", [128, 2048], BF16)
        v_ext = sbt("v_ext", [128, 16 * 132], BF16)
        W = sbt("W", [128, WAR], BF16)
        FA = sbt("FA", [128, 1536], F32)
        rel_sb = sbt("rel_sb", [128, RELW], FP16)
        msk_sb = sbt("msk_sb", [128, 896], FP16)
        ident_bf = sbt("ident_bf", [128, 128], BF16)
        consts = sbt("consts_sb", [128, 8], F32)
        ones_r = sbt("ones_r", [1, 128], BF16)
        ones_c = sbt("ones_c", [128, 1], BF16)
        zr = sbt("zr", [128, 8], F32)
        stat = sbt("stat", [128, 8], F32)
        PS = [pst(f"PS{i}", [128, 512], F32) for i in range(5)]
        PT = [pst("PTA", [128, 1024], BF16), pst("PTB", [128, 1024], BF16)]
        PSR = pst("PSR", [128, 512], F32)
        asc = sbt("asc", [128, 256], BF16)

        bd = Build(nc, es, serial=serial)
        t, v, s_, g, x_ = "t", "v", "s", "g", "x"

        bd.emit(g, lambda e: e.dma_start(ident_bf[:, :], ident_d), inc="ident", inck=16)
        bd.emit(g, lambda e: e.memset(ones_r[:, :], 1.0), inc="gp")
        bd.emit(g, lambda e: e.memset(ones_c[:, :], 1.0), inc="gp")
        bd.emit(g, lambda e: e.memset(W[:, O_WSC:O_WSC + 256], 0.0), inc="gp")
        bd.emit(g, lambda e: e.memset(v_ext[:, :], 1.0), inc="gp")
        bd.emit(x_, lambda e: e.dma_start(rel_sb[:, :], rel_d), inc="rel", inck=16)
        bd.emit(x_, lambda e: e.dma_start(msk_sb[:, :], msk_d), inc="rel", inck=16)
        bd.emit(x_, lambda e: e.dma_start(consts[:, :], consts_d), inc="cst", inck=16)
        bd.wait(t, "ident", 16)
        bd.wait(t, "gp", 4)
        bd.wait(g, "gp", 4)
        bd.wait(v, "rel", 32)
        bd.wait(v, "cst", 16)
        bd.wait(s_, "cst", 16)

        def stats_ln(src_ap, sq_off, needkeys):
            mk0 = bd.emit(v, lambda e: e.reduce_sum(stat[:, 0:1], src_ap, axis=AX.X), inc="v")
            bd.need(s_, *needkeys)
            mk_s = bd.emit(s_, lambda e: e.activation(
                W[:, sq_off:sq_off + D], src_ap, AF.Square, accum_out=stat[:, 1:2]), inc="s")
            bd.wait(v, "v", mk0)
            bd.wait(v, "s", mk_s)
            mk = bd.emit(v, lambda e: e.tensor_scalar(stat[:, 2:3], stat[:, 0:1], 1.0 / D,
                                                      None, ALU.mult), inc="v")
            bd.wait(v, "v", mk)
            mk = bd.emit(v, lambda e: e.tensor_scalar(stat[:, 3:4], stat[:, 1:2], 1.0 / D,
                                                      None, ALU.mult), inc="v")
            bd.wait(v, "v", mk)
            mk = bd.emit(v, lambda e: e.tensor_mul(stat[:, 4:5], stat[:, 2:3], stat[:, 2:3]),
                         inc="v")
            bd.wait(v, "v", mk)
            mk_v = bd.emit(v, lambda e: e.scalar_tensor_tensor(
                stat[:, 5:6], stat[:, 4:5], -1.0, stat[:, 3:4], ALU.mult, ALU.add), inc="v")
            bd.wait(s_, "v", mk_v)
            mk_l = bd.emit(s_, lambda e: e.activation(stat[:, 6:7], stat[:, 5:6], AF.Ln,
                                                      bias=consts[:, 3:4]), inc="s")
            bd.wait(s_, "s", mk_l)
            mk_r = bd.emit(s_, lambda e: e.activation(stat[:, 7:8], stat[:, 6:7], AF.Exp,
                                                      scale=consts[:, 1:2]), inc="s")
            return mk_r

        def transpose_tile_to_A(xoff, sI):
            last_t = None
            for gI in range(8):
                pt = PT[gI % 2]
                bd.need(t, f"pt{gI % 2}")
                for j in range(4):
                    ic = gI * 4 + j
                    mk_t = bd.emit(t, lambda e, pt=pt, j=j, xo=xoff, ic=ic: e.transpose(
                        pt[:, j * 128:j * 128 + 128], W[:, xo + ic * 128:xo + ic * 128 + 128],
                        ident_bf[:, :]), inc=("t" if j == 3 else None))
                eng, enm = (v, "v") if gI % 2 == 0 else (s_, "s")
                bd.wait(eng, "t", mk_t)
                bd.need(eng, "bufA")
                if gI % 2 == 0:
                    mk_e = bd.emit(v, lambda e, pt=pt, gI=gI, sI=sI: e.tensor_copy(
                        A[:, :].rearrange("p (c n) -> p c n", n=1024)[:, gI * 4:gI * 4 + 4,
                                                                     sI * 128:sI * 128 + 128],
                        pt[:, 0:512].rearrange("p (c n) -> p c n", c=4)), inc="v")
                else:
                    mk_e = bd.emit(s_, lambda e, pt=pt, gI=gI, sI=sI: e.copy(
                        A[:, :].rearrange("p (c n) -> p c n", n=1024)[:, gI * 4:gI * 4 + 4,
                                                                     sI * 128:sI * 128 + 128],
                        pt[:, 0:512].rearrange("p (c n) -> p c n", c=4)), inc="s")
                bd.guard(f"pt{gI % 2}", enm, mk_e)
                last_t = mk_t
            return last_t, bd.c["v"].n, bd.c["s"].n

        qe_marks = {}
        for h in range(2):
            tok0 = h * HALF

            bd.serial = serial or ('A' in serial_phases)
            bd.need(g, "war_lo")
            mk_wkv = bd.emit(g, lambda e: e.dma_start(
                W[:, O_WKV:O_WKV + 8192].rearrange("p (c n) -> p c n", c=32),
                wkv_d.rearrange("(c p) n -> p c n", p=128)), inc="wkv", inck=16)
            for sI in range(8):
                t0 = tok0 + sI * 128
                gc = 8 * h + sI
                xoff = O_XBF + 4096 * (sI % 2)
                bd.need(g, f"xbf{sI % 2}")
                mk_x = bd.emit(g, lambda e, t0=t0, xo=xoff: e.dma_start(
                    W[:, xo:xo + D], x_d[t0:t0 + 128, :]), inc=f"xb{sI % 2}", inck=16)
                bd.wait(v, f"xb{sI % 2}", mk_x)
                bd.wait(s_, f"xb{sI % 2}", mk_x)
                mk_r = stats_ln(W[:, xoff:xoff + D], O_SQA, ("sqA", "wx1"))
                bd.guard("sqA", "s", bd.c["s"].n)
                bd.wait(v, "s", mk_r)
                if dump and h == DUMP_H and sI == 0:
                    mkst = bd.emit(v, lambda e: e.tensor_copy(FA[:, 896:904], stat[:, :]), inc="v")
                    bd.wait(x_, "v", mkst)
                    bd.emit(x_, lambda e: e.dma_start(dmp["d_stA"], FA[:, 896:904]),
                            inc="dmps", inck=16)
                mk_ln = bd.emit(v, lambda e, xo=xoff: e.tensor_scalar(
                    W[:, xo:xo + D], W[:, xo:xo + D], stat[:, 2:3], stat[:, 7:8],
                    ALU.subtract, ALU.mult), inc="v")
                bd.wait(t, "v", mk_ln)
                mk_t, lnT_v, lnT_s = transpose_tile_to_A(xoff, sI)
                bd.guard(f"xbf{sI % 2}", "t", mk_t)
                bd.wait(t, "v", lnT_v)
                bd.wait(t, "s", lnT_s)
                bd.wait(t, "wkv", mk_wkv)
                bd.need(t, "ps2")
                for ic in range(32):
                    mk_kv = bd.emit(t, lambda e, ic=ic, sI=sI: e.matmul(
                        PS[2][:, 0:256], A[:, ic * 1024 + sI * 128:ic * 1024 + sI * 128 + 128],
                        W[:, O_WKV + ic * 256:O_WKV + ic * 256 + 256],
                        start=(ic == 0), stop=(ic == 31)), inc=("t" if ic == 31 else None))
                bd.wait(v, "t", mk_kv)
                bd.need(v, "kscr")
                mk_k = bd.emit(v, lambda e: e.tensor_scalar(
                    W[:, O_KSC:O_KSC + 128], PS[2][:, 0:128], SCALE, None, ALU.mult), inc="v")
                bd.wait(v, "gp", 4)
                mk_vv = bd.emit(v, lambda e, gc=gc: e.tensor_copy(
                    v_ext[:, gc * 132:gc * 132 + 128], PS[2][:, 128:256]), inc="v")
                bd.guard("ps2", "v", mk_vv)
                bd.wait(t, "v", mk_k)
                bd.need(t, "pt0")
                mk_tk = bd.emit(t, lambda e: e.transpose(
                    PT[0][:, 0:128], W[:, O_KSC:O_KSC + 128], ident_bf[:, :]), inc="t")
                bd.guard("kscr", "t", mk_tk)
                bd.wait(v, "t", mk_tk)
                mk_kT = bd.emit(v, lambda e, gc=gc: e.tensor_copy(
                    kT[:, gc * 128:gc * 128 + 128], PT[0][:, 0:128]), inc="v")
                bd.guard("pt0", "v", mk_kT)
                last_kv_t = mk_tk
            bd.guard("wq0", "t", last_kv_t)
            bd.guard("wq1", "t", last_kv_t)
            phaseA_v = bd.c["v"].n
            phaseA_s = bd.c["s"].n
            if dump and h == DUMP_H:
                bd.wait(x_, "v", phaseA_v)
                bd.emit(x_, lambda e: e.dma_start(dmp["d_lnT"], A[:, :]), inc="dmps", inck=16)
                bd.emit(x_, lambda e: e.dma_start(dmp["d_kT"], kT[:, :]), inc="dmps", inck=16)
                mkd = bd.emit(x_, lambda e: e.dma_start(dmp["d_v"], v_ext[:, :]), inc="dmps", inck=16)
                bd.guard("bufA", "dmps", mkd)

            bd.serial = serial or ('Q' in serial_phases)
            for hd in range(NH):
                poff = O_WQ + 4096 * (hd % 2)
                bd.need(g, f"wq{hd % 2}")
                mk_wq = bd.emit(g, lambda e, hd=hd, po=poff: e.dma_start(
                    W[:, po:po + 4096].rearrange("p (c n) -> p c n", c=32),
                    wq_d.rearrange("(c p) n -> p c n", p=128)[:, :, hd * 128:hd * 128 + 128]),
                    inc=f"wqs{hd % 2}", inck=16)
                bd.wait(t, f"wqs{hd % 2}", mk_wq)
                bd.wait(t, "v", phaseA_v)
                for tb in range(2):
                    psq = PS[tb % 2]
                    bd.need(t, f"ps{tb % 2}")
                    for ic in range(32):
                        mk_q = bd.emit(t, lambda e, psq=psq, po=poff, ic=ic, tb=tb: e.matmul(
                            psq[:, :], W[:, po + ic * 128:po + ic * 128 + 128],
                            A[:, ic * 1024 + tb * 512:ic * 1024 + tb * 512 + 512],
                            start=(ic == 0), stop=(ic == 31)), inc=("t" if ic == 31 else None))
                    bd.guard(f"wq{hd % 2}", "t", mk_q)
                    bd.wait(v, "t", mk_q)
                    mk_qe = bd.emit(v, lambda e, psq=psq, hd=hd, tb=tb: e.tensor_copy(
                        B[:, hd * 1024 + tb * 512:hd * 1024 + tb * 512 + 512], psq[:, :]), inc="v")
                    bd.guard(f"ps{tb % 2}", "v", mk_qe)
                    qe_marks[(h, hd, tb)] = mk_qe
            phaseQ_t = bd.c["t"].n
            if dump and h == DUMP_H:
                bd.wait(x_, "v", bd.c["v"].n)
                mkd = bd.emit(x_, lambda e: e.dma_start(dmp["d_qT"], B[:, :]), inc="dmps", inck=16)
                bd.guard("bufB", "dmps", mkd)

            bd.serial = serial or ('ATTN' in serial_phases)
            bd.guard("probsT0", "t", phaseQ_t)
            bd.guard("probsT1", "t", phaseQ_t)
            bd.need(v, "ob0", "ob1")
            scidx = 0
            pvidx = 0
            faidx = 0
            bcidx = 0

            def emit_scores(hd, tb):
                nonlocal scidx, faidx, bcidx
                slope = SLOPES[hd]
                t0b = tok0 + tb * 512
                K_v = (t0b + 512) // 128
                seqp = (hd * 2 + tb) % 2
                pT_off = O_PRB + 8192 * seqp
                fast = hd >= 12
                n_full = (t0b // 128) if fast else 0
                bd.need(s_, f"probsT{seqp}")
                bd.wait(t, "v", qe_marks[(h, hd, tb)])
                exp_last = None
                for kc in range(K_v):
                    bank = (0, 1, 4)[scidx % 3]
                    scidx += 1
                    bd.need(t, f"ps{bank}")
                    mk_sc = bd.emit(t, lambda e, bank=bank, kc=kc, hd=hd, tb=tb: e.matmul(
                        PS[bank][:, :], kT[:, kc * 128:kc * 128 + 128],
                        B[:, hd * 1024 + tb * 512:hd * 1024 + tb * 512 + 512],
                        start=True, stop=True), inc="t")
                    if fast:
                        bc = 4 + (bcidx % 2)
                        bcidx += 1
                        bd.need(v, f"bcol{bc}")
                        off = float(slope) * (kc * 128 - t0b - 511) - SHIFT
                        mk_b = bd.emit(v, lambda e, bc=bc, sl=slope, off=off: e.tensor_scalar(
                            stat[:, bc:bc + 1], consts[:, 4:5], sl, off, ALU.mult, ALU.add),
                            inc="v")
                        if kc < n_full:
                            bd.wait(s_, "v", mk_b)
                            bd.wait(s_, "t", mk_sc)
                            mk_exp = bd.emit(s_, lambda e, bank=bank, po=pT_off, kc=kc, bc=bc:
                                             e.activation(
                                                 W[:, po + kc * 512:po + kc * 512 + 512],
                                                 PS[bank][:, :], AF.Exp, bias=stat[:, bc:bc + 1]),
                                             inc="s")
                            bd.guard(f"ps{bank}", "s", mk_exp)
                        else:
                            slot = faidx % 3
                            faidx += 1
                            scf = 512 * slot
                            shiftm = 384 - kc * 128 + t0b
                            bd.wait(v, "t", mk_sc)
                            bd.need(v, f"sc{slot}")
                            mk_stt = bd.emit(v, lambda e, bank=bank, scf=scf, sh=shiftm:
                                             e.scalar_tensor_tensor(
                                                 FA[:, scf:scf + 512], msk_sb[:, sh:sh + 512],
                                                 1.0, PS[bank][:, :], ALU.mult, ALU.add),
                                             inc="v")
                            bd.guard(f"ps{bank}", "v", mk_stt)
                            bd.wait(s_, "v", mk_stt)
                            mk_exp = bd.emit(s_, lambda e, scf=scf, po=pT_off, kc=kc, bc=bc:
                                             e.activation(
                                                 W[:, po + kc * 512:po + kc * 512 + 512],
                                                 FA[:, scf:scf + 512], AF.Exp,
                                                 bias=stat[:, bc:bc + 1]), inc="s")
                            bd.guard(f"sc{slot}", "s", mk_exp)
                        bd.guard(f"bcol{bc}", "s", mk_exp)
                    else:
                        slot = faidx % 3
                        faidx += 1
                        scf = 512 * slot
                        bd.wait(v, "t", mk_sc)
                        bd.need(v, f"sc{slot}")
                        shift2 = 511 - kc * 128 + t0b
                        mk_stt = bd.emit(v, lambda e, bank=bank, scf=scf, sh=shift2, sl=slope:
                                         e.scalar_tensor_tensor(
                                             FA[:, scf:scf + 512], rel_sb[:, sh:sh + 512], sl,
                                             PS[bank][:, :], ALU.mult, ALU.add), inc="v")
                        bd.guard(f"ps{bank}", "v", mk_stt)
                        bd.wait(s_, "v", mk_stt)
                        mk_exp = bd.emit(s_, lambda e, scf=scf, po=pT_off, kc=kc: e.activation(
                            W[:, po + kc * 512:po + kc * 512 + 512], FA[:, scf:scf + 512],
                            AF.Exp, bias=consts[:, 0:1]), inc="s")
                        bd.guard(f"sc{slot}", "s", mk_exp)
                    exp_last = mk_exp
                return exp_last

            def emit_pv(hd, tb, exp_last):
                nonlocal pvidx
                t0b = tok0 + tb * 512
                K_v = (t0b + 512) // 128
                seqp = (hd * 2 + tb) % 2
                pT_off = O_PRB + 8192 * seqp
                pend = []

                def flush_one():
                    s4, mk_sc2 = pend.pop(0)
                    bd.wait(t, "v", mk_sc2)
                    bd.need(t, f"pt{s4 % 2}")
                    mk_ta = bd.emit(t, lambda e, s4=s4: e.transpose(
                        PT[s4 % 2][:, 0:128], asc[:, (s4 % 2) * 128:(s4 % 2) * 128 + 128],
                        ident_bf[:, :]), inc="t")
                    bd.guard(f"asc{s4 % 2}", "t", mk_ta)
                    bd.wait(v, "t", mk_ta)
                    bd.need(v, "bufA")
                    mk_ae = bd.emit(v, lambda e, s4=s4, hd=hd, tb=tb: e.tensor_copy(
                        A[:, hd * 1024 + tb * 512 + s4 * 128:hd * 1024 + tb * 512 + s4 * 128 + 128],
                        PT[s4 % 2][:, 0:128]), inc="v")
                    bd.guard(f"pt{s4 % 2}", "v", mk_ae)

                for s4 in range(4):
                    pvb = 2 + (pvidx % 2)
                    pvidx += 1
                    bd.wait(t, "s", exp_last)
                    bd.need(t, f"ps{pvb}")
                    for kc in range(K_v):
                        mk_pv = bd.emit(t, lambda e, pvb=pvb, kc=kc, s4=s4, po=pT_off: e.matmul(
                            PS[pvb][:, 0:130],
                            W[:, po + kc * 512 + s4 * 128:po + kc * 512 + s4 * 128 + 128],
                            v_ext[:, kc * 132:kc * 132 + 130],
                            start=(kc == 0), stop=(kc == K_v - 1)),
                            inc=("t" if kc == K_v - 1 else None))
                    bd.guard(f"probsT{seqp}", "t", mk_pv)
                    bd.wait(v, "t", mk_pv)
                    mk_rc = bd.emit(v, lambda e, pvb=pvb, s4=s4: e.reciprocal(
                        stat[:, s4:s4 + 1], PS[pvb][:, 128:129]), inc="v")
                    bd.wait(v, "v", mk_rc)
                    bd.need(v, f"asc{s4 % 2}")
                    mk_sc2 = bd.emit(v, lambda e, pvb=pvb, s4=s4: e.tensor_scalar(
                        asc[:, (s4 % 2) * 128:(s4 % 2) * 128 + 128], PS[pvb][:, 0:128],
                        stat[:, s4:s4 + 1], None, ALU.mult), inc="v")
                    bd.guard(f"ps{pvb}", "v", mk_sc2)
                    pend.append((s4, mk_sc2))
                    if len(pend) > 1:
                        flush_one()
                while pend:
                    flush_one()

            prev = None
            for hd in range(NH):
                for tb in range(2):
                    el = emit_scores(hd, tb)
                    if prev is not None:
                        emit_pv(prev[0], prev[1], prev[2])
                    prev = (hd, tb, el)
            emit_pv(prev[0], prev[1], prev[2])
            attn_v = bd.c["v"].n
            if dump and h == DUMP_H:
                bd.wait(x_, "v", attn_v)
                bd.emit(x_, lambda e: e.dma_start(dmp["d_attnT"], A[:, :]), inc="dmps", inck=16)
                mkd = bd.emit(x_, lambda e: e.dma_start(dmp["d_zr"], zr[:, :]), inc="dmps", inck=16)
                bd.guard("bufA", "dmps", mkd)

            bd.serial = serial or ('C' in serial_phases)
            for op in range(16):
                poff = O_WO + 8192 * (op % 2)
                bd.need(g, f"wo{op % 2}", f"probsT{op % 2}")
                mk_wo = bd.emit(g, lambda e, op=op, po=poff: e.dma_start(
                    W[:, po:po + 8192].rearrange("p (c n) -> p c n", c=32),
                    wo_d.rearrange("(c p) n -> p c n", p=128)[:, :, op * 256:op * 256 + 256]),
                    inc=f"wos{op % 2}", inck=16)
                bd.wait(t, f"wos{op % 2}", mk_wo)
                bd.wait(t, "v", attn_v)
                for sI in range(8):
                    bank = sI % 2
                    bd.need(t, f"ps{bank}")
                    for dc in range(32):
                        mk_c = bd.emit(t, lambda e, bank=bank, dc=dc, sI=sI, po=poff: e.matmul(
                            PS[bank][:, 0:256],
                            A[:, dc * 1024 + sI * 128:dc * 1024 + sI * 128 + 128],
                            W[:, po + dc * 256:po + dc * 256 + 256],
                            start=(dc == 0), stop=(dc == 31)), inc=("t" if dc == 31 else None))
                    bd.guard(f"wo{op % 2}", "t", mk_c)
                    bd.wait(s_, "t", mk_c)
                    bd.need(s_, "bufB")
                    mk_cn = bd.emit(s_, lambda e, bank=bank, sI=sI, op=op: e.activation(
                        B[:, sI * 4096 + op * 256:sI * 4096 + op * 256 + 256],
                        PS[bank][:, 0:256], AF.Copy), inc="s")
                    bd.guard(f"ps{bank}", "s", mk_cn)
            phaseC_t = bd.c["t"].n
            phaseC_s = bd.c["s"].n

            bd.serial = serial or ('LN2' in serial_phases) or ('DP' in serial_phases)
            bd.need(g, "art", "wo0")
            mk_art = bd.emit(g, lambda e: e.dma_start(
                W[:, O_ART:O_ART + 2560].rearrange("p (c n) -> p c n", c=32),
                art_d.rearrange("c p n -> p c n")), inc="arts", inck=16)
            for sI in range(8):
                t0 = tok0 + sI * 128
                xoff = O_X2 + 4096 * (sI % 2)
                bd.need(g, f"x2bf{sI % 2}", "wo1")
                mk_x = bd.emit(g, lambda e, t0=t0, xo=xoff: e.dma_start(
                    W[:, xo:xo + D], x_d[t0:t0 + 128, :]), inc=f"x2b{sI % 2}", inck=16)
                bd.wait(v, f"x2b{sI % 2}", mk_x)
                bd.wait(v, "s", phaseC_s)
                mk_x1 = bd.emit(v, lambda e, sI=sI, xo=xoff: e.tensor_add(
                    B[:, sI * 4096:sI * 4096 + D], B[:, sI * 4096:sI * 4096 + D],
                    W[:, xo:xo + D]), inc="v")
                bd.wait(s_, "v", mk_x1)
                mk_r = stats_ln(B[:, sI * 4096:sI * 4096 + D], O_SQ2, ("sq2", "wo0"))
                bd.guard("sq2", "s", bd.c["s"].n)
                bd.wait(v, "s", mk_r)
                mk_ln = bd.emit(v, lambda e, sI=sI, xo=xoff: e.tensor_scalar(
                    W[:, xo:xo + D], B[:, sI * 4096:sI * 4096 + D], stat[:, 2:3], stat[:, 7:8],
                    ALU.subtract, ALU.mult), inc="v")
                bd.wait(t, "v", mk_ln)
                mk_t, _lv, _ls = transpose_tile_to_A(xoff, sI)
                bd.guard(f"x2bf{sI % 2}", "t", mk_t)
                ln2_done_mk = bd.c["v"].n
                ln2_done_s = bd.c["s"].n
                bd.wait(t, "arts", mk_art)
                bd.wait(t, "v", ln2_done_mk)
                bd.wait(t, "s", ln2_done_s)
                bd.need(t, "psr")
                for ic in range(32):
                    mk_rh = bd.emit(t, lambda e, ic=ic, sI=sI: e.matmul(
                        PSR[:, 0:80], A[:, ic * 1024 + sI * 128:ic * 1024 + sI * 128 + 128],
                        W[:, O_ART + ic * 80:O_ART + ic * 80 + 80],
                        start=(ic == 0), stop=(ic == 31)), inc=("t" if ic == 31 else None))
                bd.guard("art", "t", mk_rh)
                bd.wait(v, "t", mk_rh)
                bd.emit(v, lambda e: e.tensor_copy(FA[:, 48:112], PSR[:, 0:64]))
                mk_nm = bd.emit(v, lambda e: e.reduce_max(stat[:, 6:7], PSR[:, 64:80],
                                                          axis=AX.X, negate=True), inc="v")
                bd.wait(s_, "v", mk_nm)
                mk_el = bd.emit(s_, lambda e: e.activation(FA[:, 0:16], PSR[:, 64:80],
                                                           AF.Exp, bias=stat[:, 6:7]), inc="s")
                bd.guard("psr", "s", mk_el)
                bd.wait(v, "s", mk_el)
                tiny = []
                tiny.append(lambda e: e.reduce_sum(stat[:, 3:4], FA[:, 0:16], axis=AX.X))
                tiny.append(lambda e: e.reduce_max(stat[:, 0:1], FA[:, 0:16], axis=AX.X))
                tiny.append(lambda e: e.tensor_scalar(FA[:, 32:48], FA[:, 0:16], stat[:, 0:1],
                                                      None, ALU.is_lt))
                tiny.append(lambda e: e.tensor_mul(FA[:, 16:32], FA[:, 0:16], FA[:, 32:48]))
                tiny.append(lambda e: e.reduce_max(stat[:, 2:3], FA[:, 16:32], axis=AX.X))
                tiny.append(lambda e: e.tensor_scalar(stat[:, 3:4], stat[:, 3:4], 1e-6, None,
                                                      ALU.mult))
                tiny.append(lambda e: e.tensor_add(stat[:, 3:4], stat[:, 3:4], stat[:, 0:1]))
                tiny.append(lambda e: e.tensor_add(stat[:, 3:4], stat[:, 3:4], stat[:, 2:3]))
                tiny.append(lambda e: e.reciprocal(stat[:, 4:5], stat[:, 3:4]))
                tiny.append(lambda e: e.tensor_scalar(FA[:, 32:48], FA[:, 0:16], stat[:, 2:3],
                                                      None, ALU.is_ge))
                tiny.append(lambda e: e.tensor_scalar(FA[:, 16:32], FA[:, 0:16], stat[:, 4:5],
                                                      0.5, ALU.mult, ALU.mult))
                tiny.append(lambda e: e.tensor_mul(FA[:, 16:32], FA[:, 16:32], FA[:, 32:48]))
                for fn in tiny:
                    mk_cmb = bd.emit(v, fn, inc="v")
                    bd.wait(v, "v", mk_cmb)
                wpend = []

                def wrh_flush():
                    eI, mk_w = wpend.pop(0)
                    slot = O_WSC + 128 * (eI % 2)
                    bd.wait(t, "v", mk_w)
                    bd.need(t, f"pt{eI % 2}")
                    mk_tw = bd.emit(t, lambda e, sl=slot, eI=eI: e.transpose(
                        PT[eI % 2][:, 0:128], W[:, sl:sl + 128], ident_bf[:, :]), inc="t")
                    bd.guard(f"wrh{eI % 2}", "t", mk_tw)
                    bd.wait(v, "t", mk_tw)
                    half_p = 64 * (eI % 2)
                    dst = O_WRT + (eI // 2) * 1024 + sI * 128
                    mk_we = bd.emit(v, lambda e, eI=eI, hp=half_p, do=dst: e.tensor_copy(
                        W[hp:hp + 64, do:do + 128], PT[eI % 2][0:64, 0:128]), inc="v")
                    bd.guard(f"pt{eI % 2}", "v", mk_we)

                for eI in range(16):
                    slot = O_WSC + 128 * (eI % 2)
                    bd.need(v, f"wrh{eI % 2}", "wrhT")
                    mk_w = bd.emit(v, lambda e, sl=slot, eI=eI: e.tensor_scalar(
                        W[:, sl:sl + 64], FA[:, 48:112], FA[:, 16 + eI:17 + eI], None, ALU.mult),
                        inc="v")
                    wpend.append((eI, mk_w))
                    if len(wpend) > 1:
                        wrh_flush()
                while wpend:
                    wrh_flush()
            dpre_v = bd.c["v"].n

            bd.serial = serial or ('DM' in serial_phases)
            for op in range(16):
                slot = (op + 1) % 2
                poff = O_WX + 10496 * slot
                bd.need(g, f"wx{slot}")
                if op == 0:
                    bd.wait(g, "t", phaseC_t)
                    bd.need(g, "x2bf0", "x2bf1")
                if op == 1:
                    bd.wait(g, "v", dpre_v)
                mk_w1 = bd.emit(g, lambda e, op=op, po=poff: e.dma_start(
                    W[:, po:po + 8192].rearrange("p (c n) -> p c n", c=32),
                    bwt_d.rearrange("(c p) n -> p c n", p=128)[:, :, op * 256:op * 256 + 256]),
                    inc=f"wxs{slot}", inck=16)
                mk_w2 = bd.emit(g, lambda e, op=op, po=poff: e.dma_start(
                    W[:, po + 8192:po + 10240].rearrange("p (c n) -> p c n", c=8),
                    bext_d.rearrange("c p n -> p c n")[:, :, op * 256:op * 256 + 256]),
                    inc=f"wxs{slot}", inck=16)
                mk_w3 = bd.emit(g, lambda e, op=op, po=poff: e.dma_start(
                    W[0:1, po + 10240:po + 10496], bb_d[0:1, op * 256:op * 256 + 256]),
                    inc=f"wxs{slot}", inck=16)
                bd.wait(t, f"wxs{slot}", mk_w3)
                bd.wait(t, "v", dpre_v)
                for sI in range(8):
                    bank = sI % 2
                    bd.need(t, f"ps{bank}")
                    for j in range(32):
                        bd.emit(t, lambda e, bank=bank, j=j, sI=sI, po=poff: e.matmul(
                            PS[bank][:, 0:256],
                            A[:, j * 1024 + sI * 128:j * 1024 + sI * 128 + 128],
                            W[:, po + j * 256:po + j * 256 + 256], start=(j == 0), stop=False))
                    for j in range(8):
                        bd.emit(t, lambda e, bank=bank, j=j, sI=sI, po=poff: e.matmul(
                            PS[bank][:, 0:256],
                            W[:, O_WRT + j * 1024 + sI * 128:O_WRT + j * 1024 + sI * 128 + 128],
                            W[:, po + (32 + j) * 256:po + (32 + j) * 256 + 256],
                            start=False, stop=False))
                    mk_d = bd.emit(t, lambda e, bank=bank, po=poff: e.matmul(
                        PS[bank][:, 0:256], ones_r[:, :], W[0:1, po + 10240:po + 10496],
                        start=False, stop=True), inc="t")
                    bd.guard(f"wx{slot}", "t", mk_d)
                    bd.wait(v, "t", mk_d)
                    ob = 256 * (sI % 2)
                    bd.need(v, f"ob{sI % 2}")
                    mk_o = bd.emit(v, lambda e, bank=bank, ob=ob, sI=sI, op=op: e.tensor_add(
                        FA[:, ob:ob + 256], PS[bank][:, 0:256],
                        B[:, sI * 4096 + op * 256:sI * 4096 + op * 256 + 256]), inc="v")
                    bd.guard(f"ps{bank}", "v", mk_o)
                    bd.wait(x_, "v", mk_o)
                    t0 = tok0 + sI * 128
                    mk_y = bd.emit(x_, lambda e, ob=ob, t0=t0, op=op: e.dma_start(
                        y_d[t0:t0 + 128, op * 256:op * 256 + 256], FA[:, ob:ob + 256]),
                        inc=f"obs{sI % 2}", inck=16)
                    bd.guard(f"ob{sI % 2}", f"obs{sI % 2}", mk_y)
            end_t = bd.c["t"].n
            for key in ("war_lo", "xbf0", "xbf1", "wq0", "wq1", "wrhT", "bufA",
                        "kscr", "sqA"):
                bd.guard(key, "t", end_t)

        for sem_name in ("obs0", "obs1", "dmps"):
            if sem_name in bd.c:
                fin = bd.c[sem_name].n
                bd.wait(t, sem_name, fin)
                bd.wait(v, sem_name, fin)

        with nc.Block() as block:
            @block.gpsimd
            def _(e):
                for f in bd.q["g"]:
                    f(e)

            @block.sync
            def _(e):
                for f in bd.q["x"]:
                    f(e)

            @block.tensor
            def _(e):
                for f in bd.q["t"]:
                    f(e)

            @block.vector
            def _(e):
                for f in bd.q["v"]:
                    f(e)

            @block.scalar
            def _(e):
                for f in bd.q["s"]:
                    f(e)
    return nc


def _host_inputs(L0_attn_hidden_states, L0_attn_Wq, L0_attn_Wkv, L0_attn_Wo,
                 L0_ffn_base_w, L0_ffn_base_b, L0_ffn_A, L0_ffn_B, L0_ffn_router_w):
    f32 = np.float32
    wq = np.ascontiguousarray(L0_attn_Wq, dtype=f32)
    wkv = np.ascontiguousarray(L0_attn_Wkv, dtype=f32)
    wo = np.ascontiguousarray(L0_attn_Wo, dtype=f32)
    bwt = np.ascontiguousarray(np.asarray(L0_ffn_base_w, dtype=f32).T)
    bext = np.ascontiguousarray(
        np.asarray(L0_ffn_B, dtype=f32).transpose(0, 2, 1).reshape(8, 128, D))
    art = np.ascontiguousarray(np.concatenate(
        [np.asarray(L0_ffn_A, dtype=f32).T.reshape(32, 128, 64),
         np.asarray(L0_ffn_router_w, dtype=f32).T.reshape(32, 128, 16)], axis=2))
    bb = np.ascontiguousarray(np.asarray(L0_ffn_base_b, dtype=f32).reshape(1, D))
    p = np.arange(128)[:, None]
    j = np.arange(RELW)[None, :]
    u = (p - j + 511).astype(f32)
    relf = np.where(u <= 0, u, np.float32(MASKVAL)).astype(np.float16)
    jm = np.arange(896)[None, :]
    um = (p - jm + 384).astype(f32)
    mskf = np.where(um <= 0, np.float32(0.0), np.float32(MASKVAL)).astype(np.float16)
    ident = np.eye(128, dtype=f32)
    consts = np.zeros((128, 8), dtype=f32)
    consts[:, 0] = -SHIFT
    consts[:, 1] = -0.5
    consts[:, 3] = 1e-5
    consts[:, 4] = np.arange(128, dtype=f32)
    shared = dict(wq=wq, wkv=wkv, wo=wo, bwt=bwt, bext=bext, art=art, bb=bb,
                  rel=relf, msk=mskf, ident=ident, consts=consts)
    xs = np.ascontiguousarray(np.asarray(L0_attn_hidden_states, dtype=f32))
    return xs, shared


def kernel(**inputs):
    xs, shared = _host_inputs(**inputs)
    nc = build_nc()
    in_maps = [dict(shared, x=np.ascontiguousarray(xs[c])) for c in range(NC)]
    res = run_bass_kernel_spmd(nc, in_maps, core_ids=list(range(NC)))
    return np.stack([np.asarray(res.results[c]["y"], dtype=np.float32) for c in range(NC)],
                    axis=0)
